# version 05
# baseline (speedup 1.0000x reference)
import numpy as np
import concourse.bass as bass
import concourse.mybir as mybir
from concourse.bass_utils import run_bass_kernel_spmd

F32 = mybir.dt.float32
BF16 = mybir.dt.bfloat16
AF = mybir.ActivationFunctionType
ALU = mybir.AluOpType
AXX = mybir.AxisListType.X

D = 1024
DC = 8
KD = 512
HH = 4
VH = 256
NE = 8
HF = 64
CH = 128
SUB = 32
EPS = 1e-5
CLAMP = 80.0

ATTN_W = {'mu_x': [D], 'W_x1': [160, D], 'W_x2': [D, 160], 'x_bias': [5, D],
          'W_r': [KD, D], 'W_k': [KD, D], 'W_v': [D, D], 'W_g': [D, D],
          'A_w': [64, D], 'B_w': [KD, 64], 'b_w': [KD], 'bonus': [HH, CH],
          'gn_weight': [D], 'gn_bias': [D], 'W_o': [D, D]}
FFN_W = {'ln_g': [D], 'ln_b': [D], 'bn_g': [D], 'bn_b': [D], 'bn_mean': [D],
         'bn_var': [D], 'Wr': [D, NE], 'br': [NE], 'W_down': [NE, D, HF],
         'b_down': [NE, HF], 'W_up': [NE, HF, D], 'b_up': [NE, D]}

GRP = {'r': 0, 'w': 1, 'k': 2, 'v': 3, 'g': 4}


def sl(i, w):
    return slice(i * w, (i + 1) * w)


class MK:
    def close_stack(self, snm):
        stk = getattr(self, snm, None)
        if stk is not None:
            stk.close()
            if not hasattr(self, '_closed'):
                self._closed = set()
            self._closed.add(snm)

    def __init__(self, T_loc, dbg=(), phases=99):
        self.T_loc = T_loc
        self.TP = T_loc + CH
        self.NCH = T_loc // CH
        self.tiles = [(0, CH)] + [(CH + i * 256, 256) for i in range(T_loc // 256)]
        self.dbg = set(dbg)
        self.phases = phases
        self.dbg_out = {}
        self.final_evs = []

    def sb(self, name, shape, dt, stack=None):
        return (stack or self.stack).enter_context(self.nc.sbuf_tensor(name, shape, dt))

    def ps(self, name, shape, dt):
        return self.stack.enter_context(self.nc.psum_tensor(name, shape, dt))

    def mksem(self, name):
        s = self.stack.enter_context(self.nc.semaphore(name))
        self.sems[name] = s
        self.cnt[name] = 0

    def tick(self, key, instr, n=1, ev=None):
        self.cnt[key] += n
        instr.then_inc(self.sems[key], n)
        if ev is not None:
            self.ev[ev] = (key, self.cnt[key])
        return instr

    def mark(self, key, ev):
        self.ev[ev] = (key, self.cnt[key])

    def wait(self, eng, ev):
        key, val = self.ev[ev]
        if val > 0:
            eng.wait_ge(self.sems[key], val)

    def fence(self, eng, key, instr):
        self.tick(key, instr)
        eng.wait_ge(self.sems[key], self.cnt[key])
        return instr

    def ccol(self, val, n=1):
        idx = self.CC[val]
        return self.ccst[:, idx:idx + n]

    def crow(self, val):
        idx = self.CC[val]
        return self.ccst[0:1, idx:idx + 1]

    def ccol4(self, val):
        idx = self.CC[val]
        return self.ccst[0:4, idx:idx + 1]

    @staticmethod
    def aview(tile, shape):
        ap = tile[:, :, :] if len(tile.shape) == 3 else tile[:, :]
        flat = ap.rearrange("p a b -> p (a b)") if len(tile.shape) == 3 else ap
        n = 1
        for s in shape[1:]:
            n *= s
        flat = flat[:, 0:n]
        if len(shape) == 3:
            return flat.rearrange("p (a b) -> p a b", a=shape[1])
        return flat

    def dbg_t(self, name, shape, dt=F32):
        ap = self.nc.dram_tensor("dbg_" + name, shape, dt, kind="ExternalOutput").ap()
        self.dbg_out[name] = ap
        return ap

    def dbg_dump(self, name, shape, dt, src, wait_evs):
        if name not in self.dbg:
            return
        out = self.dbg_t(name, shape, dt)
        def f(sync):
            for e in wait_evs:
                self.wait(sync, e)
            self.tick('DS', sync.dma_start(out=out, in_=src() if callable(src) else src),
                      16, ev='dbg_' + name)
            self.final_evs.append('dbg_' + name)
        self.block.sync(f)

    def build(self):
        import contextlib
        self.stack = contextlib.ExitStack()
        nc = self.nc = bass.Bass("TRN2", target_bir_lowering=False, debug=False)
        TP, T_loc, NCH = self.TP, self.T_loc, self.NCH

        self.x0 = nc.dram_tensor("x0", [TP, D], F32, kind="ExternalInput").ap()
        self.bmaskD = nc.dram_tensor("bmask", [128, 1], F32, kind="ExternalInput").ap()
        self.identD = nc.dram_tensor("c_ident", [128, 128], F32, kind="ExternalInput").ap()
        self.masksuD = nc.dram_tensor("c_masksu", [128, 128], F32, kind="ExternalInput").ap()
        self.sel4D = nc.dram_tensor("c_sel4", [4, 512], F32, kind="ExternalInput").ap()
        self.W = {}
        for n, shp in ATTN_W.items():
            self.W['attn_' + n] = nc.dram_tensor('attn_' + n, shp, F32, kind="ExternalInput").ap()
        for n, shp in FFN_W.items():
            self.W['ffn_' + n] = nc.dram_tensor('ffn_' + n, shp, F32, kind="ExternalInput").ap()
        self.yout = nc.dram_tensor("y", [T_loc, D], F32, kind="ExternalOutput").ap()

        self.sc_z1 = nc.dram_tensor("sc_z1", [DC, 128, TP], BF16).ap()
        self.sc_z2 = nc.dram_tensor("sc_z2", [DC, 128, TP], BF16).ap()
        self.sc_k = nc.dram_tensor("sc_k", [NCH, 128, HH, CH], BF16).ap()
        self.sc_vt = nc.dram_tensor("sc_vt", [NCH, CH, D], BF16).ap()
        self.sc_c = nc.dram_tensor("sc_c", [NCH, 128, HH, CH], F32).ap()
        self.sc_sp = nc.dram_tensor("sc_sp", [NCH, 128, HH, VH], BF16).ap()
        self.sc_o = nc.dram_tensor("sc_o", [NCH, 128, DC, CH], BF16).ap()
        self.sc_sg = nc.dram_tensor("sc_sg", [NCH, 128, DC, CH], BF16).ap()
        self.sc_gn = nc.dram_tensor("sc_gn", [NCH, 2, HH, CH], F32).ap()
        self.sc_stat = nc.dram_tensor("sc_stat", [2, TP], BF16).ap()
        self.cc1_in = nc.dram_tensor("cc1_in", [128, DC], F32).ap()
        self.cc1_out = nc.dram_tensor("cc1_out", [128, DC], F32).ap()
        self.cc2_in = nc.dram_tensor("cc2_in", [128, DC], F32).ap()
        self.cc2_out = nc.dram_tensor("cc2_out", [128, DC], F32).ap()
        self.ccS_in = nc.dram_tensor("ccS_in", [128, HH * VH], F32).ap()
        self.ccS_out = nc.dram_tensor("ccS_out", [2, 128, HH * VH], F32).ap()
        self.RG = [[0, 1], [2, 3], [4, 5], [6, 7]]

        self.resid = self.sb("resid", [128, DC, TP], F32)
        self.identf = self.sb("identf", [128, 128], F32)
        self.identb = self.sb("identb", [128, 128], BF16)
        self.masksu = self.sb("masksu", [128, 128], BF16)
        self.onesb = self.sb("onesb", [128, 1], BF16)
        self.onesbr = self.sb("onesbr", [1, 128], BF16)
        self.onesfr = self.sb("onesfr", [1, 128], F32)
        self.bmk = self.sb("bmk", [128, 1], F32)
        self.ccst = self.sb("ccst", [128, 16], F32)
        self.CC = {}
        self.lnG = self.sb("lnG", [128, DC], F32)
        self.lnB = self.sb("lnB", [128, DC], F32)
        self.bnA = self.sb("bnA", [128, DC], F32)
        self.bnB = self.sb("bnB", [128, DC], F32)
        self.bnsc = self.sb("bnsc", [128, DC], F32)
        self.wrS = self.sb("wrS", [128, DC, NE], BF16)
        self.brS = self.sb("brS", [1, NE], F32)
        self.buAll = self.sb("buAll", [128, NE, DC], F32)
        self.bdAll = self.sb("bdAll", [64, NE], F32)
        self.zbnb = self.sb("zbnb", [128, DC], BF16)
        self.logit = self.sb("logit", [1, NE], F32)
        self.onehot = self.sb("onehot", [1, NE], F32)
        self.selsc = self.sb("selsc", [1, 8], F32)
        self.selrow = self.sb("selrow", [1, NE], F32)
        self.zsum = self.sb("zsum", [128, DC, len(self.tiles)], F32)
        self.zsumT = self.sb("zsumT", [128, DC], F32)

        self.psA = self.ps("psA", [128, 4, 512], F32)
        self.psB = self.ps("psB", [128, 2, 512], F32)
        self.psC = self.ps("psC", [128, 512], F32)
        self.psTb = self.ps("psTb", [128, 1024], BF16)

        import contextlib
        self.ffn_stack = contextlib.ExitStack()
        self.wdAll = self.sb("wdAll", [128, NE, DC, HF], BF16, self.ffn_stack)
        self.wuAll = self.sb("wuAll", [64, NE, DC, 128], BF16, self.ffn_stack)
        self.selmask = self.sb("selmask", [128, NE], F32, self.ffn_stack)
        self.gatec = self.sb("gatec", [128, 1], F32, self.ffn_stack)
        self.bupg = self.sb("bupg", [128, DC], F32, self.ffn_stack)
        self.bdS = self.sb("bdS", [64, 1], F32, self.ffn_stack)
        self.wdSel = self.sb("wdSel", [128, DC, HF], BF16, self.ffn_stack)
        self.wuSel = self.sb("wuSel", [64, DC, 128], BF16, self.ffn_stack)

        self.sems = {}
        self.cnt = {}
        self.ev = {}
        for s in ['T', 'V', 'S', 'G', 'DS', 'DG', 'DA', 'CC']:
            self.mksem(s)

        with nc.Block() as block:
            self.block = block
            self.emit()
        self.stack.close()
        return nc

    def emit(self):
        self.emit_setup()
        self.emit_p1()
        if self.phases >= 1:
            self.emit_sel(1)
            self.p1_stack.close()
            self.emit_p2(1)
            self.ffn_stack.close()
        if self.phases >= 2:
            self.emit_wprep()
            self.emit_p3pro()
            self.emit_p3a()
        if self.phases >= 3:
            self.emit_p3b()
        if self.phases >= 4:
            self.emit_p4a()
        if self.phases >= 5:
            self.emit_sel(2)
            self.emit_p2(2)
        if 'ccwatch' in self.dbg:
            for nm, evn in [('cc_p2', 'p2done_1'), ('cc_wp', 'wp_x2_done'),
                            ('cc_p3a', 'sin_ready'), ('cc_p3b', 'p3b_done')]:
                if evn in self.ev:
                    out = self.dbg_t(nm, [128, 16], F32)
                    def fw(sync, out=out, evn=evn, nm=nm):
                        self.wait(sync, evn)
                        self.tick('DS', sync.dma_start(out=out, in_=self.ccst[:, :]), 16, ev='dbg' + nm)
                        self.final_evs.append('dbg' + nm)
                    self.block.sync(fw)
        for snm in ['p4_st', 'p3b_st', 'p3a_st', 'p3s']:
            if getattr(self, snm, None) is not None and snm not in getattr(self, '_closed', set()):
                self.close_stack(snm)
        self.emit_finish()

    def emit_setup(self):
        W = self.W
        consts = [1.0 / 1024.0, 1.0 / float(2 * self.T_loc), 1.0 / float(VH),
                  0.5, -1.0, 0.0, CLAMP, 1.0, EPS, EPS * EPS, 1.0 + EPS, -0.5]
        for i, v in enumerate(consts):
            self.CC[v] = i
            self.nc.const_aps.register(F32, v, self.ccst[:, i:i + 1])

        def f(sync):
            ctx = self.nc.allow_non_contiguous_dma(reason="tiny one-time vector loads")
            ctx.__enter__()
            for dst, src in [(self.identf[:, :], self.identD[:, :]),
                             (self.bmk[:, :], self.bmaskD[:, :]),
                             (self.lnG[:, :], W['ffn_ln_g'].rearrange("(c p) -> p c", p=128)),
                             (self.lnB[:, :], W['ffn_ln_b'].rearrange("(c p) -> p c", p=128)),
                             (self.brS[:, :], W['ffn_br'][None, :]),
                             (self.bdAll[:, :], W['ffn_b_down'].rearrange("e h -> h e")),
                             (self.buAll[:, :, :], W['ffn_b_up'].rearrange("e (c p) -> p e c", p=128)),
                             (self.bnA[:, :], W['ffn_bn_var'].rearrange("(c p) -> p c", p=128)),
                             (self.bnB[:, :], W['ffn_bn_mean'].rearrange("(c p) -> p c", p=128)),
                             (self.bnsc[:, :], W['ffn_bn_g'].rearrange("(c p) -> p c", p=128)),
                             (self.selmask[:, 0:DC], W['ffn_bn_b'].rearrange("(c p) -> p c", p=128)),
                             ]:
                self.tick('DS', sync.dma_start(out=dst, in_=src), 16)
            ctx.__exit__(None, None, None)
            self.mark('DS', 'setup_f32')
        self.block.sync(f)

        def g(gp):
            for dst, src in [(self.identb[:, :], self.identD[:, :]),
                             (self.masksu[:, :], self.masksuD[:, :]),
                             (self.wrS[:, :, :], W['ffn_Wr'].rearrange("(c p) e -> p c e", p=128)),
                             (self.wdAll[:, :, :, :], W['ffn_W_down'].rearrange("e (c p) h -> p e c h", p=128)),
                             (self.wuAll[:, :, :, :], W['ffn_W_up'].rearrange("e h (c p) -> h e c p", p=128)),
                             ]:
                self.tick('DG', gp.dma_start(out=dst, in_=src), 16)
            self.mark('DG', 'setup_bf16')
        self.block.gpsimd(g)

        def v(vec):
            vec.memset(self.onesb[:, :], 1.0)
            vec.memset(self.onesbr[:, :], 1.0)
            vec.memset(self.onesfr[:, :], 1.0)
            for val, i in self.CC.items():
                vec.memset(self.ccst[:, i:i + 1], val)
            ins = vec.memset(self.zsum[:, :, :], 0.0)
            self.tick('V', ins, 1, ev='setup_memset')
        self.block.vector(v)

        def s(sc):
            self.wait(sc, 'setup_f32')
            self.wait(sc, 'setup_memset')
            self.fence(sc, 'S', sc.activation(self.bnA[:, :], self.bnA[:, :], AF.Ln, bias=EPS, scale=1.0))
            ins = sc.activation(self.bnA[:, :], self.bnA[:, :], AF.Exp, bias=0.0, scale=-0.5)
            self.tick('S', ins, 1, ev='bn_rsqrt')
        self.block.scalar(s)

        def v2(vec):
            self.wait(vec, 'bn_rsqrt')
            self.fence(vec, 'V', vec.tensor_tensor(self.bnA[:, :], self.bnsc[:, :], self.bnA[:, :], ALU.mult))
            self.fence(vec, 'V', vec.tensor_tensor(self.bnB[:, :], self.bnB[:, :], self.bnA[:, :], ALU.mult))
            ins = vec.tensor_tensor(self.bnB[:, :], self.selmask[:, 0:DC], self.bnB[:, :], ALU.subtract)
            self.tick('V', ins, 1, ev='bn_ready')
        self.block.vector(v2)

    def emit_p1(self):
        import contextlib
        pst = self.p1_stack = contextlib.ExitStack()
        TW = 256
        stg = [self.sb("p1stg0", [128, 2, 1024], F32, pst), self.sb("p1stg1", [128, 2, 1024], F32, pst)]
        mrow = self.sb("p1mrow", [1, TW], F32, pst)
        rrow = self.sb("p1rrow", [1, TW], F32, pst)
        mrowb = self.sb("p1mrowb", [1, TW], BF16, pst)
        rrowb = self.sb("p1rrowb", [1, TW], BF16, pst)
        xbf = self.sb("p1xbf", [128, DC, TW], BF16, pst)
        xsq = self.sb("p1xsq", [128, DC, TW], BF16, pst)
        zt = self.sb("p1zt", [128, DC, TW], BF16, pst)
        z1o = [self.sb("p1z1o0", [128, DC, TW], BF16, pst),
               self.sb("p1z1o1", [128, DC, TW], BF16, pst)]
        mB = self.sb("p1mB", [128, TW], BF16, pst)
        rB = self.sb("p1rB", [128, TW], BF16, pst)
        srow = self.sb("p1srow", [1, 2, TW], F32, pst)
        ntile = len(self.tiles)

        gseq = []
        for i, (t0, tl) in enumerate(self.tiles):
            for s in range(tl // 128):
                gseq.append((i, s))
        gidx = {k: n for n, k in enumerate(gseq)}

        for i, (t0, tl) in enumerate(self.tiles):
            nsub = tl // 128
            buf = i % 2

            def dma_in(sync, i=i, t0=t0, tl=tl, buf=buf, nsub=nsub):
                if i >= 2:
                    self.wait(sync, f'p1tpT_{i-2}')
                self.tick('DS', sync.dma_start(
                    out=stg[buf][:, 0:nsub, :],
                    in_=self.x0[t0:t0 + tl, :].rearrange("(s p) d -> p s d", p=128)),
                    16, ev=f'p1in_{i}')
            self.block.sync(dma_in)

            for s in range(nsub):
                def tp(t, i=i, s=s, buf=buf):
                    self.wait(t, f'p1in_{i}')
                    if i == 0 and s == 0:
                        self.wait(t, 'setup_f32')
                    n = gidx[(i, s)]
                    if n >= 2:
                        pi, psb = gseq[n - 2]
                        self.wait(t, f'p1tpev_{pi}_{psb}')
                    b0 = (gidx[(i, s)] % 2) * 2
                    ins = None
                    for c in range(DC):
                        ins = t.transpose(self.psA[:, b0 + c // 4, sl(c % 4, 128)],
                                          stg[buf][:, s, sl(c, 128)], self.identf[:, :])
                    self.tick('T', ins, 1, ev=f'p1tp_{i}_{s}')
                    if s == nsub - 1:
                        self.mark('T', f'p1tpT_{i}')
                self.block.tensor(tp)

                def tpe(vec, i=i, s=s, t0=t0):
                    self.wait(vec, f'p1tp_{i}_{s}')
                    b0 = (gidx[(i, s)] % 2) * 2
                    u = t0 + s * 128
                    vec.tensor_copy(self.resid[:, 0:4, u:u + 128],
                                    self.psA[:, b0, :].rearrange("p (c t) -> p c t", c=4))
                    ins = vec.tensor_copy(self.resid[:, 4:8, u:u + 128],
                                          self.psA[:, b0 + 1, :].rearrange("p (c t) -> p c t", c=4))
                    self.tick('V', ins, 1, ev=f'p1tpev_{i}_{s}')
                self.block.vector(tpe)

            def vc(vec, i=i, t0=t0, tl=tl, nsub=nsub):
                self.wait(vec, f'p1tpev_{i}_{nsub-1}')
                if i >= 1:
                    self.wait(vec, f'p1zaff_{i-1}')
                ins = vec.tensor_copy(xbf[:, :, 0:tl], self.resid[:, :, t0:t0 + tl])
                self.tick('V', ins, 1)
                ins = vec.tensor_tensor(xsq[:, :, 0:tl], xbf[:, :, 0:tl], xbf[:, :, 0:tl], ALU.mult)
                self.tick('V', ins, 1, ev=f'p1xsq_{i}')
            self.block.vector(vc)

            def st(t, i=i, tl=tl):
                self.wait(t, f'p1xsq_{i}')
                if i >= 1:
                    self.wait(t, f'p1var_{i-1}')
                ins = None
                for c in range(DC):
                    ins = t.matmul(self.psB[0:1, 0, 0:tl], self.onesb[:, 0:1], xbf[:, c, 0:tl],
                                   start=(c == 0), stop=(c == DC - 1))
                self.tick('T', ins, 1)
                for c in range(DC):
                    ins = t.matmul(self.psB[0:1, 1, 0:tl], self.onesb[:, 0:1], xsq[:, c, 0:tl],
                                   start=(c == 0), stop=(c == DC - 1))
                self.tick('T', ins, 1, ev=f'p1sums_{i}')
            self.block.tensor(st)

            def sv(vec, i=i, tl=tl):
                self.wait(vec, f'p1sums_{i}')
                vec.tensor_scalar(mrow[:, 0:tl], self.psB[0:1, 0, 0:tl], self.crow(1.0 / 1024.0), None, ALU.mult)
                vec.tensor_scalar(srow[:, 0, 0:tl], self.psB[0:1, 1, 0:tl], self.crow(1.0 / 1024.0), None, ALU.mult)
                vec.tensor_tensor(srow[:, 1, 0:tl], mrow[:, 0:tl], mrow[:, 0:tl], ALU.mult)
                ins = vec.tensor_tensor(rrow[:, 0:tl], srow[:, 0, 0:tl], srow[:, 1, 0:tl], ALU.subtract)
                self.tick('V', ins, 1, ev=f'p1var_{i}')
            self.block.vector(sv)

            def ss(sc, i=i, tl=tl):
                self.wait(sc, f'p1var_{i}')
                sc.activation(rrow[:, 0:tl], rrow[:, 0:tl], AF.Ln, bias=EPS * EPS, scale=1.0 + EPS)
                ins = sc.activation(rrow[:, 0:tl], rrow[:, 0:tl], AF.Exp, bias=0.0, scale=-0.5)
                self.tick('S', ins, 1, ev=f'p1rstd_{i}')
            self.block.scalar(ss)

            def sv2(vec, i=i, tl=tl):
                self.wait(vec, f'p1rstd_{i}')
                vec.tensor_copy(mrowb[:, 0:tl], mrow[:, 0:tl])
                ins = vec.tensor_copy(rrowb[:, 0:tl], rrow[:, 0:tl])
                self.tick('V', ins, 1, ev=f'p1rowsb_{i}')
            self.block.vector(sv2)

            def bc(t, i=i, tl=tl):
                self.wait(t, f'p1rowsb_{i}')
                if i >= 1:
                    self.wait(t, f'p1bcev_{i-1}')
                ins = t.matmul(self.psC[:, 0:tl], self.onesbr[0:1, :], mrowb[0:1, 0:tl], start=True, stop=True)
                self.tick('T', ins, 1)
                ins = t.matmul(self.psC[:, 256:256 + tl], self.onesbr[0:1, :], rrowb[0:1, 0:tl], start=True, stop=True)
                self.tick('T', ins, 1, ev=f'p1bc_{i}')
            self.block.tensor(bc)

            def bce(vec, i=i, tl=tl):
                self.wait(vec, f'p1bc_{i}')
                vec.tensor_copy(mB[:, 0:tl], self.psC[:, 0:tl])
                ins = vec.tensor_copy(rB[:, 0:tl], self.psC[:, 256:256 + tl])
                self.tick('V', ins, 1, ev=f'p1bcev_{i}')
            self.block.vector(bce)

            def zb(vec, i=i, tl=tl):
                self.wait(vec, f'p1bcev_{i}')
                vec.tensor_tensor(zt[:, :, 0:tl], xbf[:, :, 0:tl],
                                  mB[:, 0:tl].unsqueeze(1).broadcast_to((128, DC, tl)), ALU.subtract)
                ins = vec.tensor_tensor(zt[:, :, 0:tl], zt[:, :, 0:tl],
                                        rB[:, 0:tl].unsqueeze(1).broadcast_to((128, DC, tl)), ALU.mult)
                self.tick('V', ins, 1, ev=f'p1zt_{i}')
            self.block.vector(zb)

            def za(vec, i=i, tl=tl, buf=buf):
                self.wait(vec, f'p1zt_{i}')
                if i >= 2:
                    self.wait(vec, f'p1zout_{i-2}')
                vec.tensor_tensor(z1o[buf][:, :, 0:tl], zt[:, :, 0:tl],
                                  self.lnG[:, :].unsqueeze(2).broadcast_to((128, DC, tl)), ALU.mult)
                vec.tensor_tensor(z1o[buf][:, :, 0:tl], z1o[buf][:, :, 0:tl],
                                  self.lnB[:, :].unsqueeze(2).broadcast_to((128, DC, tl)), ALU.add)
                ins = vec.tensor_reduce(self.zsum[:, :, i:i + 1], z1o[buf][:, :, 0:tl], AXX, ALU.add)
                self.tick('V', ins, 1, ev=f'p1zaff_{i}')
            self.block.vector(za)

            def zs(sc, i=i, t0=t0, tl=tl, buf=buf):
                self.wait(sc, f'p1zaff_{i}')
                self.tick('DA', sc.dma_start(out=self.sc_z1[:, :, t0:t0 + tl].rearrange("c p t -> p c t"),
                                             in_=z1o[buf][:, :, 0:tl]),
                          16, ev=f'p1zout_{i}')
            self.block.scalar(zs)

        def fin(vec):
            self.wait(vec, f'p1zaff_{ntile-1}')
            ins = vec.tensor_reduce(self.zsumT[:, :].unsqueeze(2), self.zsum[:, :, 1:ntile], AXX, ALU.add)
            self.tick('V', ins, 1, ev='zsumT_1')
        self.block.vector(fin)
        self.mark('DA', 'p1_all_z_stored')

        self.dbg_dump('z1', [DC, 128, self.TP], BF16, self.sc_z1[:, :, :],
                      [f'p1zout_{ntile-1}'])
        self.dbg_dump('zsum', [128, DC, len(self.tiles)], F32, self.zsum[:, :, :], ['zsumT_1'])
        self.dbg_dump('z1o', [128, DC, 256], BF16, z1o[(len(self.tiles) - 1) % 2][:, :, :], [f'p1zaff_{ntile-1}'])
        self.dbg_dump('lnG', [128, DC], F32, self.lnG[:, :], ['setup_f32'])
        self.dbg_dump('lnB', [128, DC], F32, self.lnB[:, :], ['setup_f32'])
        self.dbg_dump('zt', [128, DC, 256], BF16, zt[:, :, :], [f'p1zaff_{ntile-1}'])
        self.dbg_dump('xbf', [128, DC, 256], BF16, xbf[:, :, :], [f'p1zaff_{ntile-1}'])
        self.dbg_dump('mB', [128, 256], BF16, mB[:, :], [f'p1bcev_{ntile-1}'])
        self.dbg_dump('rB', [128, 256], BF16, rB[:, :], [f'p1bcev_{ntile-1}'])
        self.dbg_dump('mrow', [1, 256], F32, mrow[:, :], [f'p1rstd_{ntile-1}'])
        self.dbg_dump('rrow', [1, 256], F32, rrow[:, :], [f'p1rstd_{ntile-1}'])
        self.dbg_dump('zsumT', [128, DC], F32, self.zsumT[:, :], ['zsumT_1'])

    def emit_sel(self, p):
        cc_in = self.cc1_in if p == 1 else self.cc2_in
        cc_out = self.cc1_out if p == 1 else self.cc2_out
        src_ev = 'zsumT_1' if p == 1 else 'zsumT_2'
        if p == 2:
            import contextlib
            self.ffn_stack = contextlib.ExitStack()
            self.selmask = self.sb("selmask2", [128, NE], F32, self.ffn_stack)
            self.gatec = self.sb("gatec2", [128, 1], F32, self.ffn_stack)
            self.bupg = self.sb("bupg2", [128, DC], F32, self.ffn_stack)
            self.bdS = self.sb("bdS2", [64, 1], F32, self.ffn_stack)
            self.wdSel = self.sb("wdSel2", [128, DC, HF], BF16, self.ffn_stack)
            self.wuSel = self.sb("wuSel2", [64, DC, 128], BF16, self.ffn_stack)
            self.ffn2_tmp = contextlib.ExitStack()
            self.wdAll = self.sb("wdAll2", [128, NE, DC, HF], BF16, self.ffn2_tmp)
            self.wuAll = self.sb("wuAll2", [64, NE, DC, 128], BF16, self.ffn2_tmp)
            W = self.W
            def rew(gp):
                self.wait(gp, 'zsumT_2')
                self.wait(gp, f'c_z2st_{self.NCH-1}')
                self.tick('DG', gp.dma_start(out=self.wdAll[:, :, :, :], in_=W['ffn_W_down'].rearrange("e (c p) h -> p e c h", p=128)), 16)
                self.tick('DG', gp.dma_start(out=self.wuAll[:, :, :, :], in_=W['ffn_W_up'].rearrange("e h (c p) -> h e c p", p=128)), 16)
                self.mark('DG', 'ffnw2')
            self.block.gpsimd(rew)

        def g(gp):
            self.wait(gp, src_ev)
            self.tick('DG', gp.dma_start(out=cc_in[:, :], in_=self.zsumT[:, :]), 16)
            gp.wait_ge(self.sems['DG'], self.cnt['DG'])
            ins = gp.collective_compute(
                "AllReduce", ALU.add, replica_groups=self.RG,
                ins=[cc_in[:, :]], outs=[cc_out[:, :]])
            self.tick('CC', ins, 1, ev=f'cc_z_{p}')
        self.block.gpsimd(g)

        def f(sync):
            self.wait(sync, f'cc_z_{p}')
            self.tick('DS', sync.dma_start(out=self.zsumT[:, :], in_=cc_out[:, :]), 16, ev=f'zsent_{p}')
        self.block.sync(f)

        zbnb = self.zbnb
        if p == 1 and 'cc1' in self.dbg:
            o1 = self.dbg_t('cc1in', [128, DC])
            o2 = self.dbg_t('cc1out', [128, DC])
            def fcc(sync):
                self.wait(sync, 'cc_z_1')
                self.tick('DS', sync.dma_start(out=o1, in_=self.cc1_in[:, :]), 16)
                self.tick('DS', sync.dma_start(out=o2, in_=self.cc1_out[:, :]), 16, ev='dbgcc')
                self.final_evs.append('dbgcc')
            self.block.sync(fcc)

        def v(vec):
            self.wait(vec, f'zsent_{p}')
            if p == 1:
                self.wait(vec, 'bn_ready')
            self.fence(vec, 'V', vec.tensor_scalar(self.zsumT[:, :], self.zsumT[:, :], self.ccol(1.0 / float(2 * self.T_loc)), None, ALU.mult))
            self.fence(vec, 'V', vec.tensor_tensor(self.zsumT[:, :], self.zsumT[:, :], self.bnA[:, :], ALU.mult))
            self.fence(vec, 'V', vec.tensor_tensor(self.zsumT[:, :], self.zsumT[:, :], self.bnB[:, :], ALU.add))
            ins = vec.tensor_copy(zbnb[:, :], self.zsumT[:, :])
            self.tick('V', ins, 1, ev=f'zbn_{p}')
        self.block.vector(v)

        def t(te):
            self.wait(te, f'zbn_{p}')
            if p == 1:
                self.wait(te, 'setup_bf16')
                self.wait(te, f'p1bcev_{len(self.tiles)-1}')
            else:
                self.wait(te, 'p4a_psC_free')
            ins = None
            for c in range(DC):
                ins = te.matmul(self.psC[0:1, 0:NE], zbnb[:, c:c + 1], self.wrS[:, c, :],
                                start=(c == 0), stop=(c == DC - 1))
            self.tick('T', ins, 1, ev=f'logmm_{p}')
        self.block.tensor(t)

        def v2(vec):
            self.wait(vec, f'logmm_{p}')
            ins = vec.tensor_tensor(self.logit[:, :], self.psC[0:1, 0:NE], self.brS[:, :], ALU.add)
            self.fence(vec, 'V', ins)
            self.mark('V', f'logit_{p}')
            self.fence(vec, 'V', vec.tensor_reduce(self.selsc[:, 0:1], self.logit[:, :], AXX, ALU.max))
            ins = vec.tensor_scalar(self.selsc[:, 1:2], self.selsc[:, 0:1], self.crow(-1.0), None, ALU.mult)
            self.tick('V', ins, 1, ev=f'logmax_{p}')
        self.block.vector(v2)

        def s(sc):
            self.wait(sc, f'logmax_{p}')
            ins = sc.activation(self.selrow[:, :], self.logit[:, :], AF.Exp,
                                bias=self.selsc[0:1, 1:2], scale=1.0, accum_out=self.selsc[:, 2:3])
            self.tick('S', ins, 1, ev=f'sumexp_{p}')
        self.block.scalar(s)

        def v3(vec):
            self.wait(vec, f'sumexp_{p}')
            self.fence(vec, 'V', vec.reciprocal(self.selsc[:, 3:4], self.selsc[:, 2:3]))
            self.fence(vec, 'V', vec.tensor_scalar(self.selsc[:, 4:5], self.selsc[:, 3:4], self.crow(0.5), None, ALU.mult))
            ins = vec.tensor_scalar(self.onehot[:, :], self.logit[:, :], self.selsc[0:1, 0:1], None, ALU.is_equal)
            self.tick('V', ins, 1, ev=f'onehot_{p}')
        self.block.vector(v3)

        def t2(te):
            self.wait(te, f'onehot_{p}')
            ins = te.matmul(self.psC[:, 8:16], self.onesfr[0:1, :], self.onehot[0:1, :], start=True, stop=True)
            self.tick('T', ins, 1)
            ins = te.matmul(self.psC[:, 16:17], self.onesfr[0:1, :], self.selsc[0:1, 4:5], start=True, stop=True)
            self.tick('T', ins, 1, ev=f'selbc_{p}')
        self.block.tensor(t2)

        def v4(vec):
            self.wait(vec, f'selbc_{p}')
            vec.tensor_copy(self.selmask[:, :], self.psC[:, 8:16])
            ins = vec.tensor_copy(self.gatec[:, :], self.psC[:, 16:17])
            self.tick('V', ins, 1, ev=f'selbcv_{p}')
        self.block.vector(v4)

        def s2(sc):
            self.wait(sc, f'selbcv_{p}')
            if p == 1:
                self.wait(sc, 'setup_bf16')
            else:
                self.wait(sc, 'ffnw2')
            sc.activation(self.wdSel[:, :, :], self.wdAll[:, 0, :, :], AF.Identity,
                          bias=0.0, scale=self.selmask[:, 0:1])
            sc.activation(self.wuSel[:, :, :], self.wuAll[:, 0, :, :], AF.Identity,
                          bias=0.0, scale=self.selmask[0:64, 0:1])
            sc.activation(self.bdS[:, :], self.bdAll[:, 0:1], AF.Identity,
                          bias=0.0, scale=self.selmask[0:64, 0:1])
            ins = sc.activation(self.bupg[:, :], self.buAll[:, 0, :], AF.Identity,
                                bias=0.0, scale=self.selmask[:, 0:1])
            self.tick('S', ins, 1, ev=f'sel0_{p}')
        self.block.scalar(s2)

        def v5(vec):
            self.wait(vec, f'sel0_{p}')
            for e in range(1, NE):
                vec.scalar_tensor_tensor(self.wdSel[:, :, :], self.wdAll[:, e, :, :],
                                         self.selmask[:, e:e + 1], self.wdSel[:, :, :], ALU.mult, ALU.add)
                vec.scalar_tensor_tensor(self.wuSel[:, :, :], self.wuAll[:, e, :, :],
                                         self.selmask[0:64, e:e + 1], self.wuSel[:, :, :], ALU.mult, ALU.add)
                self.fence(vec, 'V', vec.scalar_tensor_tensor(self.bdS[:, :], self.bdAll[:, e:e + 1],
                                     self.selmask[0:64, e:e + 1], self.bdS[:, :], ALU.mult, ALU.add))
                self.fence(vec, 'V', vec.scalar_tensor_tensor(self.bupg[:, :], self.buAll[:, e, :],
                                     self.selmask[:, e:e + 1], self.bupg[:, :], ALU.mult, ALU.add))
            ins = vec.tensor_scalar(self.bupg[:, :], self.bupg[:, :], self.gatec[:, 0:1], None, ALU.mult)
            self.tick('V', ins, 1, ev=f'selw_{p}')
        self.block.vector(v5)

        if p == 2:
            self.ffn2_tmp.close()

        if f'seldbg_{p}' in self.dbg:
            out = self.dbg_t(f'seldbg_{p}', [3, NE], F32)
            def fd(sync):
                self.wait(sync, f'selw_{p}')
                self.tick('DS', sync.dma_start(out=out[0:1, :], in_=self.logit[:, :]), 16)
                self.tick('DS', sync.dma_start(out=out[1:2, :], in_=self.onehot[:, :]), 16)
                self.tick('DS', sync.dma_start(out=out[2:3, :], in_=self.selsc[:, :]), 16, ev=f'dbgsel_{p}')
                self.final_evs.append(f'dbgsel_{p}')
            self.block.sync(fd)

    def emit_p2(self, p):
        import contextlib
        pst = contextlib.ExitStack()
        sc_z = self.sc_z1 if p == 1 else self.sc_z2
        TW = 512
        if p == 1:
            tiles = [(0, CH)] + [(CH + i * TW, TW) for i in range(self.T_loc // TW)]
        else:
            tiles = [(CH + i * TW, TW) for i in range(self.T_loc // TW)]
        zin = [self.sb(f"p2zin0_{p}", [128, DC, TW], BF16, pst), self.sb(f"p2zin1_{p}", [128, DC, TW], BF16, pst)]
        hbf = self.sb(f"p2h_{p}", [64, TW], BF16, pst)
        tmpt = self.sb(f"p2tmp_{p}", [128, TW], F32, pst)
        if p == 2:
            ostg = [self.sb("p4ostg0", [128, TW // 128, 1024], F32, pst), self.sb("p4ostg1", [128, TW // 128, 1024], F32, pst)]

        for ii, (t0, tl) in enumerate(tiles):
            buf = ii % 2

            def zld(sync, ii=ii, t0=t0, tl=tl, buf=buf, p=p):
                if p == 1:
                    self.wait(sync, 'p1_all_z_stored')
                else:
                    self.wait(sync, 'p4a_done')
                    if ii < 2:
                        self.wait(sync, f'selw_{p}')
                if ii >= 2:
                    self.wait(sync, f'p2dn_{p}_{ii-2}')
                self.tick('DS', sync.dma_start(out=zin[buf][:, :, 0:tl],
                                               in_=sc_z[:, :, t0:t0 + tl].rearrange("c p t -> p c t")),
                          16, ev=f'p2zin_{p}_{ii}')
            self.block.sync(zld)

            def dn(t, ii=ii, tl=tl, buf=buf, p=p):
                self.wait(t, f'p2zin_{p}_{ii}')
                self.wait(t, f'selw_{p}')
                if ii >= 1:
                    self.wait(t, f'p2h_{p}_{ii-1}')
                ins = None
                for c in range(DC):
                    ins = t.matmul(self.psA[0:64, 0, 0:tl], self.wdSel[:, c, :], zin[buf][:, c, 0:tl],
                                   start=(c == 0), stop=(c == DC - 1))
                self.tick('T', ins, 1, ev=f'p2dn_{p}_{ii}')
            self.block.tensor(dn)

            def hre(sc, ii=ii, tl=tl, p=p):
                self.wait(sc, f'p2dn_{p}_{ii}')
                ins = sc.activation(hbf[:, 0:tl], self.psA[0:64, 0, 0:tl], AF.Relu,
                                    bias=self.bdS[:, 0:1], scale=1.0)
                self.tick('S', ins, 1, ev=f'p2h_{p}_{ii}')
            self.block.scalar(hre)

            for c in range(DC):
                def up(t, ii=ii, tl=tl, c=c, p=p):
                    if c == 0:
                        self.wait(t, f'p2h_{p}_{ii}')
                    prev = (ii, c - 2) if c >= 2 else ((ii - 1, DC - 2 + c) if ii >= 1 else None)
                    if prev is not None:
                        self.wait(t, 'p2ev_{}_{}_{}'.format(p, *prev))
                    ins = t.matmul(self.psB[:, c % 2, 0:tl], self.wuSel[0:64, c, :], hbf[0:64, 0:tl],
                                   start=True, stop=True)
                    self.tick('T', ins, 1, ev=f'p2up_{p}_{ii}_{c}')
                self.block.tensor(up)

                def ev(vec, ii=ii, t0=t0, tl=tl, c=c, p=p):
                    self.wait(vec, f'p2up_{p}_{ii}_{c}')
                    vec.tensor_scalar(tmpt[:, 0:tl], self.psB[:, c % 2, 0:tl],
                                      self.gatec[:, 0:1], self.bupg[:, c:c + 1], ALU.mult, op1=ALU.add)
                    ins = vec.tensor_tensor(self.resid[:, c, t0:t0 + tl], self.resid[:, c, t0:t0 + tl],
                                            tmpt[:, 0:tl], ALU.add)
                    self.tick('V', ins, 1, ev=f'p2ev_{p}_{ii}_{c}')
                self.block.vector(ev)

            self.mark('V', f'p2tile_{p}_{ii}')

            if p == 2:
                nsub = tl // 128
                for s in range(nsub):
                    def otp(t, ii=ii, t0=t0, s=s):
                        self.wait(t, f'p2tile_2_{ii}')
                        if ii >= 2:
                            self.wait(t, f'yout_{ii-2}')
                        if s >= 1:
                            self.wait(t, f'p4otpev_{ii}_{s-1}')
                        elif ii >= 1:
                            self.wait(t, f'p4otpev_{ii-1}_{(tiles[ii-1][1] // 128) - 1}')
                        u = t0 + s * 128
                        ins = None
                        for c in range(DC):
                            ins = t.transpose(self.psA[:, 2 + c // 4, sl(c % 4, 128)],
                                              self.resid[:, c, u:u + 128], self.identf[:, :])
                        self.tick('T', ins, 1, ev=f'p4otp_{ii}_{s}')
                    self.block.tensor(otp)

                    def otpe(vec, ii=ii, s=s, buf=buf):
                        self.wait(vec, f'p4otp_{ii}_{s}')
                        vec.tensor_copy(ostg[buf][:, s, 0:512], self.psA[:, 2, :])
                        ins = vec.tensor_copy(ostg[buf][:, s, 512:1024], self.psA[:, 3, :])
                        self.tick('V', ins, 1, ev=f'p4otpev_{ii}_{s}')
                    self.block.vector(otpe)

                def ost(sync, ii=ii, t0=t0, tl=tl, buf=buf, nsub=nsub):
                    self.wait(sync, f'p4otpev_{ii}_{nsub-1}')
                    self.tick('DS', sync.dma_start(
                        out=self.yout[t0 - CH:t0 - CH + tl, :].rearrange("(s p) d -> p s d", p=128),
                        in_=ostg[buf][:, 0:nsub, :]),
                        16, ev=f'yout_{ii}')
                self.block.sync(ost)

        pst.close()
        if p == 2:
            self.ffn_stack.close()
            for ii in range(len(tiles)):
                self.final_evs.append(f'yout_{ii}')
        else:
            self.mark('V', 'p2done_1')
            self.mark('S', 'p2_s_done')
            self.dbg_dump('x1', [128, DC, self.TP], F32, self.resid[:, :, :],
                          [f'p2ev_1_{len(tiles)-1}_{DC-1}'] + [f'p2ev_1_{i}_{DC-1}' for i in range(len(tiles) - 1)])



    def emit_wprep(self):
        import contextlib
        self.p3s = contextlib.ExitStack()
        TP = self.TP
        self.zlA = self.sb("zlA", [128, TP], BF16, self.p3s)
        self.zlB = self.sb("zlB", [128, (self.NCH // 4) * CH], BF16, self.p3s)
        self.Ccum = self.sb("Ccum", [128, HH, self.NCH + 1], F32, self.p3s)
        self.sinb = self.sb("sinb", [128, HH, VH], BF16, self.p3s)
        self.muxc = self.sb("muxc", [128, DC], F32, self.p3s)
        self.xbias5 = self.sb("xbias5", [128, 5, DC], F32, self.p3s)
        self.bwc = self.sb("bwc", [128, 4], F32, self.p3s)
        self.uc = self.sb("uc", [128, HH], F32, self.p3s)
        self.gnW = self.sb("gnW", [128, DC], F32, self.p3s)
        self.gnB = self.sb("gnB", [128, DC], F32, self.p3s)
        self.yprevB = self.sb("yprevB", [128, DC, 1], BF16, self.p3s)

        self.wx2T = self.sb("wx2T", [128, 16, 128], BF16, self.p3s)
        self.p3a_st = contextlib.ExitStack()
        st = self.p3a_st
        self.Sst = self.sb("Sst", [128, HH, VH], F32, st)
        self.yprevA = self.sb("yprevA", [128, DC, 1], BF16, st)
        self.zro = self.sb("zro", [128, CH], F32, st)
        self.wx1T = self.sb("wx1T", [128, DC, 160], BF16, st)
        self.awT = self.sb("awT", [128, DC, 64], BF16, st)
        self.bwT = self.sb("bwT", [64, 4, 128], BF16, st)
        self.wkT = self.sb("wkT", [128, DC, 512], BF16, st)
        self.wvT = self.sb("wvT", [128, DC, 1024], BF16, st)
        import contextlib as _ct
        wpst = _ct.ExitStack()
        wst = self.sb("wst", [128, 4, 1024], BF16, wpst)
        wx2tmp = self.sb("wx2tmp", [32, 128], BF16, wpst)

        W = self.W

        def vecs(sync):
            self.wait(sync, 'p2done_1')
            self.wait(sync, 'p2_s_done')
            ctx = self.nc.allow_non_contiguous_dma(reason="tiny vector loads")
            ctx.__enter__()
            for dst, src_ in [(self.muxc[:, :], W['attn_mu_x'].rearrange("(c p) -> p c", p=128)),
                              (self.xbias5[:, :, :], W['attn_x_bias'].rearrange("n (c p) -> p n c", p=128)),
                              (self.bwc[:, :], W['attn_b_w'].rearrange("(m p) -> p m", p=128)),
                              (self.uc[:, :], W['attn_bonus'].rearrange("h p -> p h")),
                              (self.gnW[:, :], W['attn_gn_weight'].rearrange("(c p) -> p c", p=128)),
                              (self.gnB[:, :], W['attn_gn_bias'].rearrange("(c p) -> p c", p=128)),
                              ]:
                self.tick('DS', sync.dma_start(out=dst, in_=src_), 16)
            ctx.__exit__(None, None, None)
            self.mark('DS', 'wp_vecs')
        self.block.sync(vecs)

        def vhalf(vec):
            self.wait(vec, 'wp_vecs')
            vec.memset(self.zro[:, :], 0.0)
            self.fence(vec, 'V', vec.tensor_scalar(self.gnW[:, :], self.gnW[:, :], self.ccol(0.5), None, ALU.mult))
            ins = vec.tensor_scalar(self.gnB[:, :], self.gnB[:, :], self.ccol(0.5), None, ALU.mult)
            self.tick('V', ins, 1, ev='wp_gn')
        self.block.vector(vhalf)

        self.mark('V', 'wp_prev')
        self.mark('T', 'wp_prevT')

        def load_T(tag, hbm, R, out):
            rch = (R + 127) // 128
            for g0 in range(0, rch, 4):
                gn = min(4, rch - g0)
                def ld(gp, hbm=hbm, R=R, g0=g0, gn=gn, tag=tag):
                    self.wait(gp, 'wp_prev')
                    for k in range(gn):
                        rc = g0 + k
                        rr = min(128, R - rc * 128)
                        self.tick('DG', gp.dma_start(out=wst[0:rr, k, 0:D], in_=hbm[rc * 128:rc * 128 + rr, :]), 16)
                    self.mark('DG', f'wp_ld_{tag}_{g0}')
                self.block.gpsimd(ld)

                for k in range(gn):
                    rc = g0 + k
                    rr = min(128, R - rc * 128)
                    def tp(te, k=k, rc=rc, rr=rr, tag=tag, g0=g0):
                        self.wait(te, f'wp_ld_{tag}_{g0}')
                        self.wait(te, 'setup_bf16')
                        if rc == 0:
                            self.wait(te, 'wp_prev')
                        else:
                            self.wait(te, f'wp_ev_{tag}_{rc-1}')
                        ins = None
                        for c in range(DC):
                            ins = te.transpose(self.psTb[:, c * 128:c * 128 + rr], wst[0:rr, k, sl(c, 128)],
                                               self.identb[0:rr, 0:rr])
                        self.tick('T', ins, 1, ev=f'wp_tp_{tag}_{rc}')
                    self.block.tensor(tp)

                    def ev(vec, rc=rc, rr=rr, tag=tag, out=out):
                        self.wait(vec, f'wp_tp_{tag}_{rc}')
                        ins = vec.tensor_copy(out[:, :, rc * 128:rc * 128 + rr],
                                              self.psTb[:, :].rearrange("p (c r) -> p c r", c=DC)[:, :, 0:rr])
                        self.tick('V', ins, 1, ev=f'wp_ev_{tag}_{rc}')
                    self.block.vector(ev)
                self.ev['wp_prev'] = self.ev[f'wp_ev_{tag}_{g0 + gn - 1}']
            self.mark('T', 'wp_prevT')

        load_T('k', W['attn_W_k'], KD, self.wkT)
        load_T('v', W['attn_W_v'], D, self.wvT)
        load_T('x1', W['attn_W_x1'], 160, self.wx1T)
        load_T('aw', W['attn_A_w'], 64, self.awT)
        self.ev['wp_done_k'] = self.ev['wp_ev_k_3']
        self.ev['wp_done_v'] = self.ev['wp_ev_v_7']
        self.ev['wp_done_x1'] = self.ev['wp_ev_x1_1']
        self.ev['wp_done_aw'] = self.ev['wp_ev_aw_0']

        def ldbw(gp):
            self.wait(gp, 'wp_prev')
            self.tick('DG', gp.dma_start(out=wst[:, 0:4, 0:64],
                                         in_=W['attn_B_w'].rearrange("(m p) h -> p m h", p=128)),
                      16, ev='wp_ld_bw')
        self.block.gpsimd(ldbw)

        def tpbw(te):
            self.wait(te, 'wp_ld_bw')
            self.wait(te, 'wp_prev')
            ins = None
            for m in range(4):
                ins = te.transpose(self.psTb[0:64, m * 128:(m + 1) * 128], wst[:, m, 0:64], self.identb[:, :])
            self.tick('T', ins, 1, ev='wp_tp_bw')
        self.block.tensor(tpbw)

        def evbw(vec):
            self.wait(vec, 'wp_tp_bw')
            ins = vec.tensor_copy(self.bwT[:, :, :], self.psTb[0:64, 0:512].rearrange("p (m t) -> p m t", m=4))
            self.tick('V', ins, 1, ev='wp_done_bw')
        self.block.vector(evbw)
        self.ev['wp_prev'] = self.ev['wp_done_bw']

        def zx2(vec):
            ins = vec.memset(self.wx2T[:, :, :], 0.0)
            self.tick('V', ins, 1, ev='wp_x2_zero')
        self.block.vector(zx2)

        for ch0 in (0, 4):
            def ldx2(gp, ch0=ch0):
                self.wait(gp, 'wp_prev')
                self.tick('DG', gp.dma_start(out=wst[:, 0:4, 0:160],
                                             in_=W['attn_W_x2'][ch0 * 128:(ch0 + 4) * 128, :].rearrange("(c p) r -> p c r", p=128)),
                          16, ev=f'wp_ld_x2_{ch0}')
            self.block.gpsimd(ldx2)

            last = [None]
            for g in range(5):
                for ck in range(4):
                    c = ch0 + ck
                    def tpx2(te, g=g, ck=ck, c=c, ch0=ch0):
                        self.wait(te, f'wp_ld_x2_{ch0}')
                        if g == 0 and ck == 0:
                            self.wait(te, 'wp_prev')
                        else:
                            pg, pc = (g, c - 1) if ck > 0 else (g - 1, ch0 + 3)
                            self.wait(te, f'wp_x2ev_{pg}_{pc}')
                        ins = te.transpose(self.psTb[0:32, 0:128], wst[:, ck, g * 32:(g + 1) * 32], self.identb[:, :])
                        self.tick('T', ins, 1, ev=f'wp_x2tp_{g}_{c}')
                    self.block.tensor(tpx2)

                    def evx2(vec, g=g, c=c):
                        self.wait(vec, f'wp_x2tp_{g}_{c}')
                        if g == 0 and c == 0:
                            self.wait(vec, 'wp_x2_zero')
                        ins = vec.tensor_copy(wx2tmp[:, :], self.psTb[0:32, 0:128])
                        self.tick('V', ins, 1, ev=f'wp_x2cp_{g}_{c}')
                    self.block.vector(evx2)

                    def dmx2(sync, g=g, c=c):
                        self.wait(sync, f'wp_x2cp_{g}_{c}')
                        if g < 4:
                            ins = sync.dma_start(out=self.wx2T[32 * g:32 * g + 32, c, :], in_=wx2tmp[:, :])
                            self.tick('DS', ins, 16, ev=f'wp_x2ev_{g}_{c}')
                        else:
                            for q in range(4):
                                ins = sync.dma_start(out=self.wx2T[32 * q:32 * q + 32, 8 + c, :], in_=wx2tmp[:, :])
                                self.tick('DS', ins, 16)
                            self.mark('DS', f'wp_x2ev_{g}_{c}')
                    self.block.sync(dmx2)
            self.ev['wp_prev'] = self.ev[f'wp_x2ev_4_{ch0 + 3}']

        self.mark('DS', 'wp_x2_done')
        wpst.close()

    def emit_p3pro(self):
        import contextlib
        pst = contextlib.ExitStack()
        TW = 256
        xbf = self.sb("p3pxbf", [128, DC, TW], BF16, pst)
        xsq = self.sb("p3pxsq", [128, DC, TW], BF16, pst)
        mrow = self.sb("p3pmrow", [1, TW], F32, pst)
        rrow = self.sb("p3prrow", [1, TW], F32, pst)
        rowb = [self.sb("p3prowb0", [1, 2, TW], BF16, pst), self.sb("p3prowb1", [1, 2, TW], BF16, pst)]
        srow = self.sb("p3psrow", [1, 2, TW], F32, pst)
        ypcol = self.sb("p3ypcol", [128, 2], F32, pst)
        yprow = self.sb("p3yprow", [1, 2], BF16, pst)

        ntile = len(self.tiles)
        for i, (t0, tl) in enumerate(self.tiles):
            def vc(vec, i=i, t0=t0, tl=tl):
                if i == 0:
                    self.wait(vec, 'p2done_1')
                ins = vec.tensor_copy(xbf[:, :, 0:tl], self.resid[:, :, t0:t0 + tl])
                self.tick('V', ins, 1)
                ins = vec.tensor_tensor(xsq[:, :, 0:tl], xbf[:, :, 0:tl], xbf[:, :, 0:tl], ALU.mult)
                self.tick('V', ins, 1, ev=f'p3pxsq_{i}')
            self.block.vector(vc)

            def stm(t, i=i, tl=tl):
                self.wait(t, f'p3pxsq_{i}')
                if i >= 1:
                    self.wait(t, f'p3pvar_{i-1}')
                ins = None
                for c in range(DC):
                    ins = t.matmul(self.psB[0:1, 0, 0:tl], self.onesb[:, 0:1], xbf[:, c, 0:tl],
                                   start=(c == 0), stop=(c == DC - 1))
                self.tick('T', ins, 1)
                for c in range(DC):
                    ins = t.matmul(self.psB[0:1, 1, 0:tl], self.onesb[:, 0:1], xsq[:, c, 0:tl],
                                   start=(c == 0), stop=(c == DC - 1))
                self.tick('T', ins, 1, ev=f'p3psums_{i}')
            self.block.tensor(stm)

            def sv(vec, i=i, tl=tl):
                self.wait(vec, f'p3psums_{i}')
                if i >= 1:
                    self.wait(vec, f'p3prowb_{i-1}')
                vec.tensor_scalar(mrow[:, 0:tl], self.psB[0:1, 0, 0:tl], self.crow(1.0 / 1024.0), None, ALU.mult)
                vec.tensor_scalar(srow[:, 0, 0:tl], self.psB[0:1, 1, 0:tl], self.crow(1.0 / 1024.0), None, ALU.mult)
                vec.tensor_tensor(srow[:, 1, 0:tl], mrow[:, 0:tl], mrow[:, 0:tl], ALU.mult)
                ins = vec.tensor_tensor(rrow[:, 0:tl], srow[:, 0, 0:tl], srow[:, 1, 0:tl], ALU.subtract)
                self.tick('V', ins, 1, ev=f'p3pvar_{i}')
            self.block.vector(sv)

            def ss(sc, i=i, tl=tl):
                self.wait(sc, f'p3pvar_{i}')
                self.fence(sc, 'S', sc.activation(rrow[:, 0:tl], rrow[:, 0:tl], AF.Ln, bias=EPS, scale=1.0))
                ins = sc.activation(rrow[:, 0:tl], rrow[:, 0:tl], AF.Exp, bias=0.0, scale=-0.5)
                self.tick('S', ins, 1, ev=f'p3prstd_{i}')
            self.block.scalar(ss)

            def sv2(vec, i=i, tl=tl):
                self.wait(vec, f'p3prstd_{i}')
                if i >= 2:
                    self.wait(vec, f'p3pst_{i-2}')
                vec.tensor_copy(rowb[i % 2][:, 0, 0:tl], mrow[:, 0:tl])
                ins = vec.tensor_copy(rowb[i % 2][:, 1, 0:tl], rrow[:, 0:tl])
                self.tick('V', ins, 1, ev=f'p3prowb_{i}')
            self.block.vector(sv2)

            def stst(sync, i=i, t0=t0, tl=tl):
                self.wait(sync, f'p3prowb_{i}')
                self.tick('DS', sync.dma_start(out=self.sc_stat[:, t0:t0 + tl].rearrange("k t -> () k t"),
                                               in_=rowb[i % 2][0:1, :, 0:tl]),
                          16, ev=f'p3pst_{i}')
            self.block.sync(stst)

        def ypld(sync):
            self.wait(sync, 'p3pst_0')
            ctx = self.nc.allow_non_contiguous_dma(reason="2-element stat fetch")
            ctx.__enter__()
            self.tick('DS', sync.dma_start(out=yprow[:, :], in_=self.sc_stat[:, 127:128].rearrange("k t -> t k")), 16, ev='yprow')
            ctx.__exit__(None, None, None)
        self.block.sync(ypld)

        self.mark('T', 'p3a_psC_free_init')

        def ypbc(t):
            self.wait(t, 'yprow')
            self.wait(t, 'selbcv_1')
            self.wait(t, 'p2_s_done')
            ins = t.matmul(self.psC[:, 0:2], self.onesbr[0:1, :], yprow[0:1, :], start=True, stop=True)
            self.tick('T', ins, 1, ev='ypbc')
        self.block.tensor(ypbc)

        def ypv(vec):
            self.wait(vec, 'ypbc')
            self.fence(vec, 'V', vec.tensor_copy(ypcol[:, :], self.psC[:, 0:2]))
            self.fence(vec, 'V', vec.tensor_scalar(self.yprevA[:, :, 0], self.resid[:, :, 127], ypcol[:, 0:1], None, ALU.subtract))
            self.fence(vec, 'V', vec.tensor_scalar(self.yprevA[:, :, 0], self.yprevA[:, :, 0], ypcol[:, 1:2], None, ALU.mult))
            self.fence(vec, 'V', vec.tensor_scalar(self.yprevA[:, :, 0], self.yprevA[:, :, 0], self.bmk[:, 0:1], None, ALU.mult))
            self.fence(vec, 'V', vec.tensor_copy(self.yprevB[:, :, :], self.yprevA[:, :, :]))
            vec.memset(self.Sst[:, :, :], 0.0)
            ins = vec.memset(self.Ccum[:, :, 0:1], 0.0)
            self.tick('V', ins, 1, ev='p3init')
        self.block.vector(ypv)

        pst.close()

    def emit_p3a(self):
        st = self.p3a_st
        NCH = self.NCH
        ybf = self.sb("aybf", [128, DC, CH], BF16, st)
        delta = self.sb("adelta", [128, DC, CH], BF16, st)
        xzb = self.sb("axz", [128, DC, CH], BF16, st)
        musb = self.sb("amus", [128, DC, CH], BF16, st)
        xib = self.sb("axi", [128, DC, CH], BF16, st)
        t1b = xib
        zbt = self.sb("azbt", [32, CH], BF16, st)
        twa = self.sb("atwa", [64, CH], BF16, st)
        wtl = self.sb("awtl", [128, HH, CH], F32, st)
        cP = self.sb("acP", [128, HH, CH], F32, st)
        kfm = self.sb("akfm", [128, HH, CH], BF16, st)
        khtm = self.sb("akhtm", [128, HH, CH], BF16, st)
        khat = xzb.tensor.ap().rearrange("p (c t) -> p c t", c=DC)[:, 0:HH, :] if False else None
        vtm = self.sb("avtm", [128, D], BF16, st)
        vbf = musb
        khat = self.aview(xzb, [128, HH, CH])
        dec = self.sb("adec", [128, HH], F32, st)
        mBt = self.sb("amB", [128, CH], BF16, st)
        rBt = self.sb("arB", [128, CH], BF16, st)
        strow = [self.sb("astr0", [1, 2, CH], BF16, st), self.sb("astr1", [1, 2, CH], BF16, st)]

        zq = (self.NCH // 4) * CH
        self.mark('DA', 'a_store_-1')

        for j in range(1, NCH + 1):
            jj = j - 1
            u0 = j * CH
            buf = jj % 2

            def ldst(sync, jj=jj, u0=u0, buf=buf):
                if jj >= 2:
                    self.wait(sync, f'a_bc_{jj-2}')
                self.tick('DS', sync.dma_start(out=strow[buf][0:1, :, :],
                                               in_=self.sc_stat[:, u0:u0 + CH].rearrange("k t -> () k t")),
                          16, ev=f'a_strow_{jj}')
            self.block.sync(ldst)

            def bc(t, jj=jj, buf=buf):
                self.wait(t, f'a_strow_{jj}')
                if jj == 0:
                    for e in ['p3init', 'wp_x2_done', 'wp_done_k', 'wp_done_v',
                              'wp_done_x1', 'wp_done_aw', 'wp_done_bw', 'wp_gn',
                              'p2_s_done']:
                        self.wait(t, e)
                else:
                    self.wait(t, f'a_bcev_{jj-1}')
                ins = t.matmul(self.psC[:, 0:CH], self.onesbr[0:1, :], strow[buf][0:1, 0, :], start=True, stop=True)
                self.tick('T', ins, 1)
                ins = t.matmul(self.psC[:, CH:2 * CH], self.onesbr[0:1, :], strow[buf][0:1, 1, :], start=True, stop=True)
                self.tick('T', ins, 1, ev=f'a_bc_{jj}')
            self.block.tensor(bc)

            def ybuild(vec, jj=jj, u0=u0):
                self.wait(vec, f'a_bc_{jj}')
                if jj >= 1:
                    self.wait(vec, f'a_xi_{jj-1}_2')
                    self.wait(vec, f'a_ktp_{jj-1}')
                vec.tensor_copy(mBt[:, :], self.psC[:, 0:CH])
                ins = vec.tensor_copy(rBt[:, :], self.psC[:, CH:2 * CH])
                self.tick('V', ins, 1, ev=f'a_bcev_{jj}')
                vec.scalar_tensor_tensor(t1b[:, :, :], self.resid[:, :, u0:u0 + CH], 1.0,
                                         mBt[:, :].unsqueeze(1).broadcast_to((128, DC, CH)), ALU.mult, ALU.subtract)
                ins = vec.tensor_tensor(ybf[:, :, :], t1b[:, :, :],
                                        rBt[:, :].unsqueeze(1).broadcast_to((128, DC, CH)), ALU.mult)
                self.tick('V', ins, 1, ev=f'a_y_{jj}')
                vec.tensor_tensor(delta[:, :, 1:CH], ybf[:, :, 0:CH - 1], ybf[:, :, 1:CH], ALU.subtract)
                self.fence(vec, 'V', vec.tensor_tensor(delta[:, :, 0:1], self.yprevA[:, :, :], ybf[:, :, 0:1], ALU.subtract))
                self.fence(vec, 'V', vec.tensor_copy(self.yprevA[:, :, :], ybf[:, :, CH - 1:CH]))
                ins = None
                for c in range(DC):
                    ins = vec.scalar_tensor_tensor(xzb[:, c, :], delta[:, c, :], self.muxc[:, c:c + 1],
                                                   ybf[:, c, :], ALU.mult, ALU.add)
                self.tick('V', ins, 1, ev=f'a_xz_{jj}')
            self.block.vector(ybuild)

            def zmm(t, jj=jj):
                self.wait(t, f'a_xz_{jj}')
                if jj >= 1:
                    self.wait(t, f'a_wexp_{jj-1}')
                ins = None
                for c in range(DC):
                    ins = t.matmul(self.psA[:, 0, 0:CH], self.wx1T[:, c, 0:128], xzb[:, c, :],
                                   start=(c == 0), stop=(c == DC - 1))
                self.tick('T', ins, 1)
                for c in range(DC):
                    ins = t.matmul(self.psA[0:32, 0, CH:2 * CH], self.wx1T[:, c, 128:160], xzb[:, c, :],
                                   start=(c == 0), stop=(c == DC - 1))
                self.tick('T', ins, 1, ev=f'a_zmm_{jj}')
            self.block.tensor(zmm)

            def ztanh(sc, jj=jj, u0=u0):
                self.wait(sc, f'a_zmm_{jj}')
                if jj >= 1:
                    self.wait(sc, f'a_zB2_{jj-1}')
                ins = sc.activation(self.zlA[:, u0:u0 + CH], self.psA[:, 0, 0:CH], AF.Tanh, bias=0.0, scale=1.0)
                self.tick('S', ins, 1)
                ins = sc.activation(zbt[:, :], self.psA[0:32, 0, CH:2 * CH], AF.Tanh, bias=0.0, scale=1.0)
                self.tick('S', ins, 1, ev=f'a_zB_{jj}')
            self.block.scalar(ztanh)

            def zbd(sync, jj=jj, zq=zq):
                self.wait(sync, f'a_zB_{jj}')
                q = jj // (self.NCH // 4)
                col = (jj % (self.NCH // 4)) * CH
                self.tick('DS', sync.dma_start(out=self.zlB[32 * q:32 * q + 32, col:col + CH],
                                               in_=zbt[:, :]), 16, ev=f'a_zB2_{jj}')
            self.block.sync(zbd)

            for bi, br in enumerate(['w', 'k', 'v']):
                g = GRP[br]

                def musmm(t, jj=jj, g=g, bi=bi, u0=u0, zq=zq):
                    if bi == 0:
                        self.wait(t, f'a_zB_{jj}')
                    prev = (jj, bi - 1) if bi >= 1 else (jj - 1, 2)
                    if prev[0] >= 0:
                        self.wait(t, 'a_musev_{}_{}'.format(*prev))
                    ins = None
                    for c in range(DC):
                        if g < 4:
                            band = 32 * (g % 4)
                            kw = {'tile_position': (band, 0)} if band == 96 else {}
                            ins = t.matmul(self.psA[:, 1 + c // 4, sl(c % 4, CH)],
                                           self.wx2T[band:band + 32, c, :],
                                           self.zlA[32 * g:32 * g + 32, u0:u0 + CH],
                                           start=True, stop=True, **kw)
                        else:
                            q = jj // (self.NCH // 4)
                            col = (jj % (self.NCH // 4)) * CH
                            band = 32 * q
                            kw = {'tile_position': (band, 0)} if band == 96 else {}
                            ins = t.matmul(self.psA[:, 1 + c // 4, sl(c % 4, CH)],
                                           self.wx2T[band:band + 32, 8 + c, :],
                                           self.zlB[band:band + 32, col:col + CH],
                                           start=True, stop=True, **kw)
                    self.tick('T', ins, 1, ev=f'a_mus_{jj}_{bi}')
                self.block.tensor(musmm)

                def xiv(vec, jj=jj, g=g, bi=bi):
                    self.wait(vec, f'a_mus_{jj}_{bi}')
                    if bi == 0 and jj >= 1:
                        self.wait(vec, f'a_vtp_{jj-1}')
                    musps = self.psA[:, 1:3, :].rearrange("p b (m t) -> p (b m) t", m=4)
                    ins = vec.tensor_tensor(musb[:, :, :], musps,
                                            self.xbias5[:, g, :].unsqueeze(2).broadcast_to((128, DC, CH)), ALU.add)
                    self.tick('V', ins, 1, ev=f'a_musev_{jj}_{bi}')
                    vec.tensor_tensor(xib[:, :, :], delta[:, :, :], musb[:, :, :], ALU.mult)
                    ins = vec.tensor_tensor(xib[:, :, :], xib[:, :, :], ybf[:, :, :], ALU.add)
                    self.tick('V', ins, 1, ev=f'a_xi_{jj}_{bi}')
                self.block.vector(xiv)

                def projmm(t, jj=jj, br=br, bi=bi):
                    self.wait(t, f'a_xi_{jj}_{bi}')
                    ins = None
                    if br == 'k':
                        if jj >= 1:
                            self.wait(t, f'a_kev_{jj-1}')
                        for mc in range(4):
                            for c in range(DC):
                                ins = t.matmul(self.psA[:, 3, sl(mc, CH)], self.wkT[:, c, sl(mc, CH)],
                                               xib[:, c, :], start=(c == 0), stop=(c == DC - 1))
                        self.tick('T', ins, 1, ev=f'a_kmm_{jj}')
                    elif br == 'v':
                        if jj >= 1:
                            self.wait(t, f'a_sadd_{jj-1}')
                        for h in range(2):
                            for mc in range(4):
                                for c in range(DC):
                                    ins = t.matmul(self.psB[:, h, sl(mc, CH)], self.wvT[:, c, sl(4 * h + mc, CH)],
                                                   xib[:, c, :], start=(c == 0), stop=(c == DC - 1))
                        self.tick('T', ins, 1, ev=f'a_vmm_{jj}')
                    else:
                        if jj >= 1:
                            self.wait(t, f'a_twaev_{jj-1}')
                        for c in range(DC):
                            ins = t.matmul(self.psC[0:64, 2 * CH:3 * CH], self.awT[:, c, :],
                                           xib[:, c, :], start=(c == 0), stop=(c == DC - 1))
                        self.tick('T', ins, 1, ev=f'a_awmm_{jj}')
                self.block.tensor(projmm)

                if br == 'w':
                    def twas(sc, jj=jj):
                        self.wait(sc, f'a_awmm_{jj}')
                        ins = sc.activation(twa[:, :], self.psC[0:64, 2 * CH:3 * CH], AF.Tanh, bias=0.0, scale=1.0)
                        self.tick('S', ins, 1, ev=f'a_twa_{jj}')
                        self.mark('S', f'a_twaev_{jj}')
                    self.block.scalar(twas)

                    def wbmm(t, jj=jj):
                        self.wait(t, f'a_twa_{jj}')
                        self.wait(t, f'a_zB_{jj}')
                        ins = None
                        for mc in range(4):
                            ins = t.matmul(self.psA[:, 0, sl(mc, CH)], self.bwT[0:64, mc, :],
                                           twa[0:64, :], start=True, stop=True)
                        self.tick('T', ins, 1, ev=f'a_wbmm_{jj}')
                    self.block.tensor(wbmm)

                    def wexp(sc, jj=jj):
                        self.wait(sc, f'a_wbmm_{jj}')
                        ins = None
                        for mc in range(4):
                            ins = sc.activation(wtl[:, mc, :], self.psA[:, 0, sl(mc, CH)], AF.Exp,
                                                bias=self.bwc[:, mc:mc + 1], scale=1.0)
                        self.tick('S', ins, 1, ev=f'a_wexp_{jj}')
                    self.block.scalar(wexp)

                    def cscan(vec, jj=jj):
                        self.wait(vec, f'a_wexp_{jj}')
                        if jj >= 1:
                            self.wait(vec, f'a_store_{jj-1}')
                        ins = None
                        for mc in range(HH):
                            ins = vec.tensor_tensor_scan(cP[:, mc, :], wtl[:, mc, :], self.zro[:, :],
                                                         0.0, ALU.add, ALU.add)
                        self.tick('V', ins, 1, ev=f'a_cP_{jj}')
                        for mc in range(HH):
                            ins = vec.tensor_scalar(wtl[:, mc, :], cP[:, mc, :], cP[:, mc, CH - 1:CH], None, ALU.subtract)
                        self.tick('V', ins, 1, ev=f'a_ek_{jj}')
                    self.block.vector(cscan)

                    def ekxs(sc, jj=jj):
                        self.wait(sc, f'a_ek_{jj}')
                        ins = sc.activation(wtl[:, :, :], wtl[:, :, :], AF.Exp, bias=0.0, scale=1.0)
                        self.tick('S', ins, 1)
                        ins = sc.activation(dec[:, :], cP[:, :, CH - 1], AF.Exp, bias=0.0, scale=-1.0)
                        self.tick('S', ins, 1, ev=f'a_dec_{jj}')
                    self.block.scalar(ekxs)

            def kev(vec, jj=jj):
                self.wait(vec, f'a_kmm_{jj}')
                self.wait(vec, f'a_dec_{jj}')
                ins = vec.tensor_copy(kfm[:, :, :], self.psA[:, 3, :].rearrange("p (m t) -> p m t", m=HH))
                self.tick('V', ins, 1, ev=f'a_kev_{jj}')
                ins = vec.tensor_tensor(khat[:, :, :], kfm[:, :, :], wtl[:, :, :], ALU.mult)
                self.tick('V', ins, 1, ev=f'a_khat_{jj}')
            self.block.vector(kev)

            def vev(vec, jj=jj):
                self.wait(vec, f'a_vmm_{jj}')
                vec.tensor_copy(vbf[:, 0:4, :], self.psB[:, 0, :].rearrange("p (m t) -> p m t", m=4))
                ins = vec.tensor_copy(vbf[:, 4:8, :], self.psB[:, 1, :].rearrange("p (m t) -> p m t", m=4))
                self.tick('V', ins, 1, ev=f'a_vev_{jj}')
            self.block.vector(vev)

            def tps(t, jj=jj):
                self.wait(t, f'a_vev_{jj}')
                if jj >= 1:
                    self.wait(t, f'a_khtmev_{jj-1}')
                ins = None
                for c in range(DC):
                    ins = t.transpose(self.psTb[:, sl(c, CH)], vbf[:, c, :], self.identb[:, :])
                self.tick('T', ins, 1, ev=f'a_vtp_{jj}')
            self.block.tensor(tps)

            def tpev(vec, jj=jj):
                self.wait(vec, f'a_vtp_{jj}')
                ins = vec.tensor_copy(vtm[:, :], self.psTb[:, :])
                self.tick('V', ins, 1, ev=f'a_vtmev_{jj}')
            self.block.vector(tpev)

            def tps2(t, jj=jj):
                self.wait(t, f'a_khat_{jj}')
                self.wait(t, f'a_vtmev_{jj}')
                ins = None
                for mc in range(HH):
                    ins = t.transpose(self.psTb[:, sl(mc, CH)], khat[:, mc, :], self.identb[:, :])
                self.tick('T', ins, 1, ev=f'a_ktp_{jj}')
            self.block.tensor(tps2)

            def tpev2(vec, jj=jj):
                self.wait(vec, f'a_ktp_{jj}')
                ins = vec.tensor_copy(khtm[:, :, :], self.psTb[:, 0:512].rearrange("p (m t) -> p m t", m=HH))
                self.tick('V', ins, 1, ev=f'a_khtmev_{jj}')
            self.block.vector(tpev2)

            def panel(gp, jj=jj):
                if jj == 0:
                    self.wait(gp, 'p3init')
                else:
                    self.wait(gp, f'a_sadd_{jj-1}')
                ins = gp.dma_start(out=self.sc_sp[jj, :, :, :], in_=self.Sst[:, :, :])
                self.tick('DG', ins, 16, ev=f'a_spb_{jj}')
            self.block.gpsimd(panel)

            def smm(t, jj=jj):
                self.wait(t, f'a_khtmev_{jj}')
                self.wait(t, f'a_vev_{jj}')
                ins = None
                for h in range(HH):
                    ins = t.matmul(self.psB[:, h // 2, sl(h % 2, VH)], khtm[:, h, :],
                                   vtm[:, h * VH:(h + 1) * VH], start=True, stop=True)
                self.tick('T', ins, 1, ev=f'a_smm_{jj}')
            self.block.tensor(smm)

            def supd(vec, jj=jj):
                self.wait(vec, f'a_smm_{jj}')
                self.wait(vec, f'a_spb_{jj}')
                ins = None
                for h in range(HH):
                    ins = vec.tensor_scalar(self.Sst[:, h, :], self.Sst[:, h, :], dec[:, h:h + 1], None, ALU.mult)
                self.tick('V', ins, 1)
                ins = vec.tensor_tensor(self.Sst[:, :, :], self.Sst[:, :, :],
                                        self.psB[:, :, :].rearrange("p b (h v) -> p (b h) v", h=2), ALU.add)
                self.tick('V', ins, 1, ev=f'a_sadd_{jj}')
                ins = vec.tensor_tensor(self.Ccum[:, :, jj + 1:jj + 2], self.Ccum[:, :, jj:jj + 1],
                                        cP[:, :, CH - 1:CH], ALU.add)
                self.tick('V', ins, 1, ev=f'a_ccum_{jj}')
            self.block.vector(supd)

            def sts(sc, jj=jj):
                self.wait(sc, f'a_kev_{jj}')
                self.tick('DA', sc.dma_start(out=self.sc_k[jj, :, :, :], in_=kfm[:, :, :]), 16)
                self.wait(sc, f'a_vtmev_{jj}')
                self.tick('DA', sc.dma_start(out=self.sc_vt[jj, :, :], in_=vtm[:, :]), 16)
                self.wait(sc, f'a_cP_{jj}')
                self.tick('DA', sc.dma_start(out=self.sc_c[jj, :, :, :], in_=cP[:, :, :]), 16, ev=f'a_store_{jj}')
            self.block.scalar(sts)

        def xch(gp):
            self.wait(gp, f'a_sadd_{NCH-1}')
            self.tick('DG', gp.dma_start(out=self.ccS_in[:, :], in_=self.Sst[:, :, :].rearrange("p h v -> p (h v)")), 16)
            gp.wait_ge(self.sems['DG'], self.cnt['DG'])
            ins = gp.collective_compute("AllGather", ALU.bypass, replica_groups=self.RG,
                                        ins=[self.ccS_in[:, :]], outs=[self.ccS_out[:, :, :]])
            self.tick('CC', ins, 1, ev='cc_S')
        self.block.gpsimd(xch)

        def sinl(gp):
            self.wait(gp, 'cc_S')
            self.tick('DG', gp.dma_start(out=self.sinb[:, :, :],
                                         in_=self.ccS_out[0, :, :].rearrange("p (h v) -> p h v", h=HH)),
                      16, ev='sin_raw')
        self.block.gpsimd(sinl)

        def sinv(vec):
            self.wait(vec, 'sin_raw')
            ins = vec.tensor_scalar(self.sinb[:, :, :], self.sinb[:, :, :], self.bmk[:, 0:1], None, ALU.mult)
            self.tick('V', ins, 1, ev='sin_ready')
        self.block.vector(sinv)

        self.dbg_dump('Sloc', [128, HH, VH], F32, self.Sst[:, :, :], [f'a_sadd_{NCH-1}'])
        self.dbg_dump('sck', [self.NCH, 128, HH, CH], BF16, self.sc_k[:, :, :, :], [f'a_store_{NCH-1}'])
        self.dbg_dump('scvt', [self.NCH, CH, D], BF16, self.sc_vt[:, :, :], [f'a_store_{NCH-1}'])
        self.dbg_dump('scc', [self.NCH, 128, HH, CH], F32, self.sc_c[:, :, :, :], [f'a_store_{NCH-1}'])
        self.dbg_dump('scsp', [self.NCH, 128, HH, VH], BF16, self.sc_sp[:, :, :, :], [f'a_store_{NCH-1}'])
        self.dbg_dump('sinb', [128, HH, VH], BF16, self.sinb[:, :, :], ['sin_ready'])
        self.dbg_dump('zlA', [128, self.TP], BF16, self.zlA[:, :], [f'a_zB2_{NCH-1}', f'a_zB_{NCH-1}'])

    def emit_p3b(self):
        import contextlib
        self.close_stack('p3a_st')
        self.p3b_st = contextlib.ExitStack()
        st = self.p3b_st
        NCH = self.NCH
        self.wrT = self.sb("wrT", [128, DC, KD], BF16, st)
        self.wgT = self.sb("wgT", [128, DC, D], BF16, st)
        kfm2 = self.sb("bkfm", [128, HH, CH], BF16, st)
        vtm2 = self.sb("bvtm", [128, D], BF16, st)
        sp2 = self.sb("bsp", [128, HH, VH], BF16, st)
        cPl = self.sb("bcP", [128, HH, CH], F32, st)
        ybf = self.sb("bybf", [128, DC, CH], BF16, st)
        delta = self.sb("bdelta", [128, DC, CH], BF16, st)
        xib = self.sb("bxi", [128, DC, CH], BF16, st)
        musb = self.sb("bmus", [128, DC, CH], BF16, st)
        rbf = self.sb("brbf", [128, HH, CH], BF16, st)
        Psh = self.sb("bPsh", [128, HH, CH], F32, st)
        el = self.sb("bel", [128, HH, CH], F32, st)
        eq = self.sb("beq", [128, HH, CH], F32, st)
        elb = el[:, :, :].rearrange("p a b -> p (a b)").bitcast(BF16)
        eqb = eq[:, :, :].rearrange("p a b -> p (a b)").bitcast(BF16)
        rtil = self.aview(delta, [128, HH, CH])
        rloc = self.sb("brloc", [128, HH, CH], BF16, st)
        rhat = self.sb("brhat", [128, HH, CH], BF16, st)
        kloc = self.sb("bkloc", [128, HH, CH], BF16, st)
        Rt = self.sb("bRt", [128, HH, 6, SUB], BF16, st)
        epair = self.sb("bepair", [128, HH, 6], F32, st)
        expCc = self.sb("bexpCc", [128, HH], F32, st)
        Db = eqb[:, 0:HH * CH].rearrange("p (a b) -> p a b", a=HH)
        Am = Psh[:, :, :].rearrange("p a b -> p (a b)").bitcast(BF16)[:, 0:HH * CH].rearrange(
            "p (a b) -> p a b", a=HH)
        drow = Am[0:1, :, :]
        fb = Am
        tgh = self.aview(xib, [128, DC, CH])
        gb = self.aview(ybf, [128, DC, CH])
        obf = self.aview(musb, [128, DC, CH])
        osq = elb.rearrange("p (a b) -> p a b", a=DC)
        gnrowA = eq[0:1, :, :]
        gnrowB = Psh[0:1, :, :]
        strow0 = self.sb("bstr0", [1, 2, CH], BF16, st)
        strow = [strow0, strow0]
        mB2 = self.sb("bmB2", [128, CH], BF16, st)
        rB2 = self.sb("brB2", [128, CH], BF16, st)
        wbst_stack = contextlib.ExitStack()
        wstb = self.sb("wstb", [128, 1, 1024], BF16, wbst_stack)

        self.ev['b_prev'] = self.ev['sin_ready']
        def load_T(tag, hbm, R, out):
            rch = R // 128
            for g0 in range(0, rch, 1):
                gn = 1
                def ld(gp, hbm=hbm, g0=g0, gn=gn, tag=tag):
                    self.wait(gp, 'b_prev')
                    if tag == 'r' and g0 == 0:
                        for e in [f'a_store_{self.NCH-1}', f'a_zB_{self.NCH-1}', f'a_sadd_{self.NCH-1}']:
                            self.wait(gp, e)
                    for k in range(gn):
                        rc = g0 + k
                        self.tick('DG', gp.dma_start(out=wstb[:, k, 0:D], in_=hbm[rc * 128:(rc + 1) * 128, :]), 16)
                    self.mark('DG', f'b_ld_{tag}_{g0}')
                self.block.gpsimd(ld)
                for k in range(gn):
                    rc = g0 + k
                    def tp(te, k=k, rc=rc, tag=tag, g0=g0):
                        self.wait(te, f'b_ld_{tag}_{g0}')
                        if rc == 0:
                            self.wait(te, 'b_prev')
                        else:
                            self.wait(te, f'b_ev_{tag}_{rc-1}')
                        ins = None
                        for c in range(DC):
                            ins = te.transpose(self.psTb[:, sl(c, 128)], wstb[:, k, sl(c, 128)], self.identb[:, :])
                        self.tick('T', ins, 1, ev=f'b_tp_{tag}_{rc}')
                    self.block.tensor(tp)
                    def ev(vec, rc=rc, tag=tag, out=out):
                        self.wait(vec, f'b_tp_{tag}_{rc}')
                        ins = vec.tensor_copy(out[:, :, sl(rc, 128)],
                                              self.psTb[:, :].rearrange("p (c r) -> p c r", c=DC))
                        self.tick('V', ins, 1, ev=f'b_ev_{tag}_{rc}')
                    self.block.vector(ev)
                self.ev['b_prev'] = self.ev[f'b_ev_{tag}_{g0 + gn - 1}']
        load_T('r', self.W['attn_W_r'], KD, self.wrT)
        load_T('g', self.W['attn_W_g'], D, self.wgT)
        self.ev['b_wdone'] = self.ev['b_prev']
        wbst_stack.close()
        self.mark('DA', 'b_store_-1')
        self.mark('DG', 'b_sgst_-1')

        for jj in range(NCH):
            u0 = (jj + 1) * CH
            buf = jj % 2

            def lds(sync, jj=jj, u0=u0, buf=buf):
                if jj == 0:
                    for e in [f'a_store_{self.NCH-1}', 'sin_ready', f'a_zB_{self.NCH-1}',
                              f'a_sadd_{self.NCH-1}', f'a_spb_{self.NCH-1}']:
                        self.wait(sync, e)
                else:
                    self.wait(sync, f'b_panelfree_{jj-1}')
                    self.wait(sync, f'b_bc_{jj-1}')
                self.tick('DS', sync.dma_start(out=strow[buf][0:1, :, :],
                                               in_=self.sc_stat[:, u0:u0 + CH].rearrange("k t -> () k t")), 16)
                self.tick('DS', sync.dma_start(out=kfm2[:, :, :], in_=self.sc_k[jj, :, :, :]), 16)
                self.tick('DS', sync.dma_start(out=vtm2[:, :], in_=self.sc_vt[jj, :, :]), 16)
                self.tick('DS', sync.dma_start(out=sp2[:, :, :], in_=self.sc_sp[jj, :, :, :]), 16)
                self.tick('DS', sync.dma_start(out=cPl[:, :, :], in_=self.sc_c[jj, :, :, :]), 16, ev=f'b_ld_{jj}')
            self.block.sync(lds)

            def bc(t, jj=jj, buf=buf):
                self.wait(t, f'b_ld_{jj}')
                if jj == 0:
                    self.wait(t, 'b_wdone')
                    self.wait(t, 'sin_ready')
                else:
                    self.wait(t, f'b_bcev_{jj-1}')
                ins = t.matmul(self.psC[:, 0:CH], self.onesbr[0:1, :], strow[buf][0:1, 0, :], start=True, stop=True)
                self.tick('T', ins, 1)
                ins = t.matmul(self.psC[:, CH:2 * CH], self.onesbr[0:1, :], strow[buf][0:1, 1, :], start=True, stop=True)
                self.tick('T', ins, 1, ev=f'b_bc_{jj}')
            self.block.tensor(bc)

            def ybuild(vec, jj=jj, u0=u0):
                self.wait(vec, f'b_bc_{jj}')
                if jj >= 1:
                    self.wait(vec, f'b_omm_{jj-1}')
                    self.wait(vec, f'b_sgst_{jj-1}')
                vec.tensor_copy(mB2[:, :], self.psC[:, 0:CH])
                ins = vec.tensor_copy(rB2[:, :], self.psC[:, CH:2 * CH])
                self.tick('V', ins, 1, ev=f'b_bcev_{jj}')
                vec.scalar_tensor_tensor(xib[:, :, :], self.resid[:, :, u0:u0 + CH], 1.0,
                                         mB2[:, :].unsqueeze(1).broadcast_to((128, DC, CH)), ALU.mult, ALU.subtract)
                ins = vec.tensor_tensor(ybf[:, :, :], xib[:, :, :],
                                        rB2[:, :].unsqueeze(1).broadcast_to((128, DC, CH)), ALU.mult)
                self.tick('V', ins, 1, ev=f'b_y_{jj}')
                vec.tensor_tensor(delta[:, :, 1:CH], ybf[:, :, 0:CH - 1], ybf[:, :, 1:CH], ALU.subtract)
                self.fence(vec, 'V', vec.tensor_tensor(delta[:, :, 0:1], self.yprevB[:, :, :], ybf[:, :, 0:1], ALU.subtract))
                self.fence(vec, 'V', vec.tensor_copy(self.yprevB[:, :, :], ybf[:, :, CH - 1:CH]))
                self.mark('V', f'b_delta_{jj}')
            self.block.vector(ybuild)

            for bi, br in enumerate(['r', 'g']):
                g = GRP[br]
                def musmm(t, jj=jj, g=g, bi=bi, u0=u0):
                    if bi == 0:
                        self.wait(t, f'b_delta_{jj}')
                        if jj >= 1:
                            self.wait(t, f'b_musev_{jj-1}_1')
                    else:
                        self.wait(t, f'b_musev_{jj}_0')
                    ins = None
                    for c in range(DC):
                        if g < 4:
                            band = 0
                            ins = t.matmul(self.psA[:, 1 + c // 4, sl(c % 4, CH)],
                                           self.wx2T[band:band + 32, c, :],
                                           self.zlA[32 * g:32 * g + 32, u0:u0 + CH],
                                           start=True, stop=True)
                        else:
                            q = jj // (self.NCH // 4)
                            col = (jj % (self.NCH // 4)) * CH
                            band = 32 * q
                            kw = {'tile_position': (band, 0)} if band == 96 else {}
                            ins = t.matmul(self.psA[:, 1 + c // 4, sl(c % 4, CH)],
                                           self.wx2T[band:band + 32, 8 + c, :],
                                           self.zlB[band:band + 32, col:col + CH],
                                           start=True, stop=True, **kw)
                    self.tick('T', ins, 1, ev=f'b_mus_{jj}_{bi}')
                self.block.tensor(musmm)

                def xiv(vec, jj=jj, g=g, bi=bi):
                    self.wait(vec, f'b_mus_{jj}_{bi}')
                    musps = self.psA[:, 1:3, :].rearrange("p b (m t) -> p (b m) t", m=4)
                    ins = vec.tensor_tensor(musb[:, :, :], musps,
                                            self.xbias5[:, g, :].unsqueeze(2).broadcast_to((128, DC, CH)), ALU.add)
                    self.tick('V', ins, 1, ev=f'b_musev_{jj}_{bi}')
                    vec.tensor_tensor(xib[:, :, :], delta[:, :, :], musb[:, :, :], ALU.mult)
                    ins = vec.tensor_tensor(xib[:, :, :], xib[:, :, :], ybf[:, :, :], ALU.add)
                    self.tick('V', ins, 1, ev=f'b_xi_{jj}_{bi}')
                self.block.vector(xiv)

                def projmm(t, jj=jj, br=br, bi=bi):
                    self.wait(t, f'b_xi_{jj}_{bi}')
                    ins = None
                    if br == 'r':
                        if jj >= 1:
                            self.wait(t, f'b_rev_{jj-1}')
                        for mc in range(4):
                            for c in range(DC):
                                ins = t.matmul(self.psA[:, 3, sl(mc, CH)], self.wrT[:, c, sl(mc, CH)],
                                               xib[:, c, :], start=(c == 0), stop=(c == DC - 1))
                        self.tick('T', ins, 1, ev=f'b_rmm_{jj}')
                    else:
                        if jj >= 1:
                            self.wait(t, f'b_oev_{jj-1}')
                        for h in range(2):
                            for mc in range(4):
                                for c in range(DC):
                                    ins = t.matmul(self.psB[:, h, sl(mc, CH)], self.wgT[:, c, sl(4 * h + mc, CH)],
                                                   xib[:, c, :], start=(c == 0), stop=(c == DC - 1))
                        self.tick('T', ins, 1, ev=f'b_gmm_{jj}')
                self.block.tensor(projmm)

            def silu(sc, jj=jj):
                self.wait(sc, f'b_gmm_{jj}')
                ins = sc.activation(tgh[:, 0:4, :].rearrange("p c t -> p (c t)"), self.psB[:, 0, :], AF.Tanh, bias=0.0, scale=0.5)
                self.tick('S', ins, 1)
                ins = sc.activation(tgh[:, 4:8, :].rearrange("p c t -> p (c t)"), self.psB[:, 1, :], AF.Tanh, bias=0.0, scale=0.5)
                self.tick('S', ins, 1, ev=f'b_tgh_{jj}')
            self.block.scalar(silu)

            def siluv(vec, jj=jj):
                self.wait(vec, f'b_tgh_{jj}')
                vec.tensor_copy(gb[:, 0:4, :].rearrange("p c t -> p (c t)"), self.psB[:, 0, :])
                ins = vec.tensor_copy(gb[:, 4:8, :].rearrange("p c t -> p (c t)"), self.psB[:, 1, :])
                self.tick('V', ins, 1, ev=f'b_gev_{jj}')
                vec.tensor_tensor(osq[:, :, :], gb[:, :, :], tgh[:, :, :], ALU.mult)
                ins = vec.tensor_tensor(gb[:, :, :], gb[:, :, :], osq[:, :, :], ALU.add)
                self.tick('V', ins, 1, ev=f'b_sg_{jj}')
            self.block.vector(siluv)

            def sgst(gp, jj=jj):
                self.wait(gp, f'b_sg_{jj}')
                self.tick('DG', gp.dma_start(out=self.sc_sg[jj, :, :, :], in_=gb[:, :, :]), 16, ev=f'b_sgst_{jj}')
            self.block.gpsimd(sgst)

            def rev(vec, jj=jj):
                self.wait(vec, f'b_rmm_{jj}')
                ins = vec.tensor_copy(rbf[:, :, :], self.psA[:, 3, :].rearrange("p (m t) -> p m t", m=HH))
                self.tick('V', ins, 1, ev=f'b_rev_{jj}')
            self.block.vector(rev)

            def dk1(vec, jj=jj):
                self.wait(vec, f'b_ld_{jj}')
                self.wait(vec, f'b_rev_{jj}')
                self.wait(vec, f'b_sg_{jj}')
                if jj >= 1:
                    self.wait(vec, f'b_store_{jj-1}')
                vec.memset(Psh[:, :, 0:1], 0.0)
                ins = vec.tensor_copy(Psh[:, :, 1:CH], cPl[:, :, 0:CH - 1])
                self.tick('V', ins, 1, ev=f'b_psh_{jj}')
                ins = None
                for mc in range(HH):
                    for sb in range(4):
                        ins = vec.tensor_scalar(el[:, mc, sl(sb, SUB)], cPl[:, mc, sl(sb, SUB)],
                                                Psh[:, mc, 32 * sb:32 * sb + 1], None, ALU.subtract)
                self.tick('V', ins, 1)
                ins = vec.tensor_scalar(el[:, :, :], el[:, :, :], self.ccol(CLAMP), None, ALU.min)
                self.tick('V', ins, 1, ev=f'b_el_{jj}')
                for mc in range(HH):
                    for qb in range(4):
                        ins = vec.tensor_scalar(eq[:, mc, sl(qb, SUB)], Psh[:, mc, sl(qb, SUB)],
                                                Psh[:, mc, 32 * qb:32 * qb + 1], None, ALU.subtract)
                self.tick('V', ins, 1)
                ins = None
                pi = 0
                for sb in range(4):
                    for qb in range(sb + 1, 4):
                        for mc in range(HH):
                            ins = vec.tensor_scalar(epair[:, mc, pi:pi + 1], Psh[:, mc, 32 * sb:32 * sb + 1],
                                                    Psh[:, mc, 32 * qb:32 * qb + 1], None, ALU.subtract)
                        pi += 1
                self.tick('V', ins, 1, ev=f'b_eq_{jj}')
            self.block.vector(dk1)

            def dk2(sc, jj=jj):
                self.wait(sc, f'b_eq_{jj}')
                sc.activation(el[:, :, :], el[:, :, :], AF.Exp, bias=0.0, scale=1.0)
                sc.activation(eq[:, :, :], eq[:, :, :], AF.Exp, bias=0.0, scale=-1.0)
                sc.activation(epair[:, :, :], epair[:, :, :], AF.Exp, bias=0.0, scale=1.0)
                sc.activation(expCc[:, :], self.Ccum[:, :, jj], AF.Exp, bias=0.0, scale=-1.0)
                ins = sc.activation(Psh[:, :, :], Psh[:, :, :], AF.Exp, bias=0.0, scale=-1.0)
                self.tick('S', ins, 1, ev=f'b_dk2_{jj}')
            self.block.scalar(dk2)

            def dk3(vec, jj=jj):
                self.wait(vec, f'b_dk2_{jj}')
                vec.tensor_tensor(rtil[:, :, :], rbf[:, :, :], Psh[:, :, :], ALU.mult)
                ins = None
                for h in range(HH):
                    ins = vec.tensor_scalar(rhat[:, h, :], rtil[:, h, :], expCc[:, h:h + 1], None, ALU.mult)
                self.tick('V', ins, 1)
                vec.tensor_tensor(rloc[:, :, :], rbf[:, :, :], eq[:, :, :], ALU.mult)
                ins = vec.tensor_tensor(kloc[:, :, :], kfm2[:, :, :], el[:, :, :], ALU.mult)
                self.tick('V', ins, 1, ev=f'b_dk3_{jj}')
                ins = None
                pi = 0
                for sb in range(4):
                    for qb in range(sb + 1, 4):
                        for mc in range(HH):
                            ins = vec.tensor_scalar(Rt[:, mc, pi, :], rloc[:, mc, sl(qb, SUB)],
                                                    epair[:, mc, pi:pi + 1], None, ALU.mult)
                        pi += 1
                self.tick('V', ins, 1, ev=f'b_rt_{jj}')
            self.block.vector(dk3)

            def bon(vec, jj=jj):
                self.wait(vec, f'b_dk3_{jj}')
                ins = None
                for h in range(HH):
                    ins = vec.scalar_tensor_tensor(fb[:, h, :], kfm2[:, h, :], self.uc[:, h:h + 1],
                                                   rbf[:, h, :], ALU.mult, ALU.mult)
                self.tick('V', ins, 1, ev=f'b_fb_{jj}')
            self.block.vector(bon)

            def dmm(t, jj=jj):
                self.wait(t, f'b_fb_{jj}')
                self.wait(t, f'b_musev_{jj}_1')
                ins = None
                for h in range(HH):
                    ins = t.matmul(self.psA[0:1, 1, sl(h, CH)], self.onesb[:, 0:1], fb[:, h, :],
                                   start=True, stop=True)
                self.tick('T', ins, 1, ev=f'b_dmm_{jj}')
            self.block.tensor(dmm)

            def drowv(vec, jj=jj):
                self.wait(vec, f'b_dmm_{jj}')
                ins = vec.tensor_copy(drow[0:1, :, :], self.psA[0:1, 1, :].rearrange("p (h t) -> p h t", h=HH))
                self.tick('V', ins, 1, ev=f'b_drow_{jj}')
            self.block.vector(drowv)

            def dbcm(t, jj=jj):
                self.wait(t, f'b_drow_{jj}')
                ins = None
                for h in range(HH):
                    ins = t.matmul(self.psA[:, 2, sl(h, CH)], self.onesbr[0:1, :], drow[0:1, h, :],
                                   start=True, stop=True)
                self.tick('T', ins, 1, ev=f'b_dbc_{jj}')
            self.block.tensor(dbcm)

            def dbv(vec, jj=jj):
                self.wait(vec, f'b_dbc_{jj}')
                ins = vec.tensor_tensor(Db[:, :, :], self.psA[:, 2, :].rearrange("p (h t) -> p h t", h=HH),
                                        self.identb[:, :].unsqueeze(1).broadcast_to((128, HH, CH)), ALU.mult)
                self.tick('V', ins, 1, ev=f'b_db_{jj}')
            self.block.vector(dbv)

            def amm(t, jj=jj):
                self.wait(t, f'b_rt_{jj}')
                if jj >= 1:
                    self.wait(t, f'b_amev_{jj-1}')
                sbst = {0: 0, 1: 3, 2: 5}
                ins = None
                for mc in range(HH):
                    for sb in range(4):
                        kw = {'tile_position': (0, 32 * sb)} if sb == 3 else {}
                        ins = t.matmul(self.psA[32 * sb:32 * sb + 32, 0, mc * CH + 32 * sb:mc * CH + 32 * sb + 32],
                                       kloc[:, mc, sl(sb, SUB)], rloc[:, mc, sl(sb, SUB)],
                                       start=True, stop=True, **kw)
                        if sb < 3:
                            pi0 = sbst[sb]
                            nq = 3 - sb
                            ins = t.matmul(self.psA[32 * sb:32 * sb + 32, 0,
                                                    mc * CH + 32 * (sb + 1):mc * CH + 32 * (sb + 1) + nq * SUB],
                                           kloc[:, mc, sl(sb, SUB)],
                                           Rt[:, mc, pi0:pi0 + nq, :].rearrange("p a b -> p (a b)"),
                                           start=True, stop=True)
                self.tick('T', ins, 1, ev=f'b_atail_{jj}')
            self.block.tensor(amm)

            def amask(vec, jj=jj):
                self.wait(vec, f'b_atail_{jj}')
                self.wait(vec, f'b_db_{jj}')
                vec.scalar_tensor_tensor(Am[:, :, :], self.psA[:, 0, :].rearrange("p (h t) -> p h t", h=HH), 1.0,
                                         self.masksu[:, :].unsqueeze(1).broadcast_to((128, HH, CH)), ALU.mult, ALU.mult)
                ins = vec.tensor_tensor(Am[:, :, :], Am[:, :, :], Db[:, :, :], ALU.add)
                self.tick('V', ins, 1, ev=f'b_amev_{jj}')
            self.block.vector(amask)

            def omm(t, jj=jj):
                self.wait(t, f'b_amev_{jj}')
                self.wait(t, f'b_sg_{jj}')
                ins = None
                for h in range(HH):
                    for vc in range(2):
                        c = 2 * h + vc
                        ins = t.matmul(self.psB[:, c // 4, sl(c % 4, CH)],
                                       vtm2[:, (2 * h + vc) * 128:(2 * h + vc + 1) * 128],
                                       Am[:, h, :], start=True, stop=False)
                        ins = t.matmul(self.psB[:, c // 4, sl(c % 4, CH)],
                                       sp2[:, h, sl(vc, 128)], rtil[:, h, :], start=False, stop=False)
                        ins = t.matmul(self.psB[:, c // 4, sl(c % 4, CH)],
                                       self.sinb[:, h, sl(vc, 128)], rhat[:, h, :], start=False, stop=True)
                self.tick('T', ins, 1, ev=f'b_omm_{jj}')
                self.mark('T', f'b_panelfree_{jj}')
            self.block.tensor(omm)

            def oev(vec, jj=jj):
                self.wait(vec, f'b_omm_{jj}')
                vec.tensor_copy(obf[:, 0:4, :].rearrange("p c t -> p (c t)"), self.psB[:, 0, :])
                ins = vec.tensor_copy(obf[:, 4:8, :].rearrange("p c t -> p (c t)"), self.psB[:, 1, :])
                self.tick('V', ins, 1, ev=f'b_oev_{jj}')
                ins = vec.tensor_tensor(osq[:, :, :], obf[:, :, :], obf[:, :, :], ALU.mult)
                self.tick('V', ins, 1, ev=f'b_osq_{jj}')
            self.block.vector(oev)

            def ost(sc, jj=jj):
                self.wait(sc, f'b_oev_{jj}')
                self.tick('DA', sc.dma_start(out=self.sc_o[jj, :, :, :], in_=obf[:, :, :]), 16, ev=f'b_ost_{jj}')
            self.block.scalar(ost)

            def gnm(t, jj=jj):
                self.wait(t, f'b_osq_{jj}')
                self.wait(t, f'b_drow_{jj}')
                ins = None
                for h in range(HH):
                    for vc in range(2):
                        ins = t.matmul(self.psA[0:1, 3, sl(h, CH)], self.onesb[:, 0:1], obf[:, 2 * h + vc, :],
                                       start=(vc == 0), stop=(vc == 1))
                for h in range(HH):
                    for vc in range(2):
                        ins = t.matmul(self.psA[0:1, 1, sl(h, CH)], self.onesb[:, 0:1], osq[:, 2 * h + vc, :],
                                       start=(vc == 0), stop=(vc == 1))
                self.tick('T', ins, 1, ev=f'b_gnm_{jj}')
            self.block.tensor(gnm)

            def gnv(vec, jj=jj):
                self.wait(vec, f'b_gnm_{jj}')
                vec.tensor_copy(gnrowA, self.psA[0:1, 3, :].rearrange("p (h t) -> p h t", h=HH))
                ins = vec.tensor_copy(gnrowB, self.psA[0:1, 1, :].rearrange("p (h t) -> p h t", h=HH))
                self.tick('V', ins, 1, ev=f'b_gnv_{jj}')
            self.block.vector(gnv)

            def gnst(sc, jj=jj):
                self.wait(sc, f'b_gnv_{jj}')
                self.tick('DA', sc.dma_start(out=self.sc_gn[jj, 0:1, :, :], in_=gnrowA), 16)
                self.tick('DA', sc.dma_start(out=self.sc_gn[jj, 1:2, :, :], in_=gnrowB), 16,
                          ev=f'b_store_{jj}')
            self.block.scalar(gnst)

            if 'p3bdbg' in self.dbg and jj == 0:
                o1 = self.nc.dram_tensor("dbg_str0", [1, 2, CH], BF16, kind="ExternalOutput").ap()
                self.dbg_out['str0'] = o1
                o2 = self.nc.dram_tensor("dbg_mB2", [128, 2, CH], BF16, kind="ExternalOutput").ap()
                self.dbg_out['mB2'] = o2
                def dbs0(sync, jj=jj):
                    self.wait(sync, f'b_bcev_{jj}')
                    self.tick('DS', sync.dma_start(out=o1, in_=strow[0][0:1, :, :]), 16)
                    self.tick('DS', sync.dma_start(out=o2[:, 0, :], in_=mB2[:, :]), 16)
                    self.tick('DS', sync.dma_start(out=o2[:, 1, :], in_=rB2[:, :]), 16, ev='dbgs0')
                    self.final_evs.append('dbgs0')
                self.block.sync(dbs0)
            if 'p3bdbg' in self.dbg:
                if jj == 0:
                    self.dbydr = self.nc.dram_tensor("dbg_ybf", [self.NCH, 128, DC, CH], BF16, kind="ExternalOutput").ap()
                    self.dbg_out['ybf'] = self.dbydr
                    self.dbrdr = self.nc.dram_tensor("dbg_rbf", [self.NCH, 128, HH, CH], BF16, kind="ExternalOutput").ap()
                    self.dbg_out['rbf'] = self.dbrdr
                    self.dbkl = self.nc.dram_tensor("dbg_kloc", [self.NCH, 128, HH, CH], BF16, kind="ExternalOutput").ap()
                    self.dbg_out['kloc'] = self.dbkl
                    self.dbrl = self.nc.dram_tensor("dbg_rloc", [self.NCH, 128, HH, CH], BF16, kind="ExternalOutput").ap()
                    self.dbg_out['rloc'] = self.dbrl
                    self.dbam = self.nc.dram_tensor("dbg_Am", [self.NCH, 128, HH, CH], BF16, kind="ExternalOutput").ap()
                    self.dbg_out['Am'] = self.dbam
                    self.dbrt = self.nc.dram_tensor("dbg_rtil", [self.NCH, 128, HH, CH], BF16, kind="ExternalOutput").ap()
                    self.dbg_out['rtil'] = self.dbrt
                def dbst(sync, jj=jj):
                    self.wait(sync, f'b_y_{jj}')
                    self.tick('DS', sync.dma_start(out=self.dbydr[jj], in_=ybf[:, :, :]), 16)
                    self.wait(sync, f'b_dk3_{jj}')
                    self.tick('DS', sync.dma_start(out=self.dbrdr[jj], in_=rbf[:, :, :]), 16)
                    self.tick('DS', sync.dma_start(out=self.dbkl[jj], in_=kloc[:, :, :]), 16)
                    self.tick('DS', sync.dma_start(out=self.dbrl[jj], in_=rloc[:, :, :]), 16)
                    self.wait(sync, f'b_rt_{jj}')
                    self.tick('DS', sync.dma_start(out=self.dbrt[jj], in_=rtil[:, :, :]), 16)
                    self.wait(sync, f'b_amev_{jj}')
                    self.tick('DS', sync.dma_start(out=self.dbam[jj], in_=Am[:, :, :]), 16, ev=f'b_dbst_{jj}')
                    self.final_evs.append(f'b_dbst_{jj}')
                self.block.sync(dbst)
                self.ev[f'b_panelfree_{jj}'] = self.ev[f'b_dbst_{jj}']

        self.mark('DA', 'p3b_done')
        self.dbg_dump('sco', [self.NCH, 128, DC, CH], BF16, self.sc_o[:, :, :, :], [f'b_ost_{NCH-1}'])
        self.dbg_dump('scsg', [self.NCH, 128, DC, CH], BF16, self.sc_sg[:, :, :, :], [f'b_sgst_{NCH-1}'])
        self.dbg_dump('scgn', [self.NCH, 2, HH, CH], F32, self.sc_gn[:, :, :, :], [f'b_store_{NCH-1}'])

    def emit_p4a(self):
        import contextlib
        self.close_stack('p3b_st')
        self.p4_st = contextlib.ExitStack()
        st = self.p4_st
        NCH = self.NCH
        self.woT = self.sb("woT", [128, DC, D], BF16, st)
        obuf = self.sb("cobuf", [128, DC, CH], BF16, st)
        sgbuf = self.sb("csg", [128, DC, CH], BF16, st)
        gnstt = self.sb("cgnst", [1, 2, HH * CH], F32, st)
        gnm2 = self.sb("cgnm", [1, 2, HH * CH], F32, st)
        gnmb = self.sb("cgnmb", [1, 2, HH * CH], BF16, st)
        rstdBs = self.sb("crstdB", [128, HH, CH], BF16, st)
        mrstdBs = self.sb("cmrstdB", [128, HH, CH], BF16, st)
        og = self.sb("cog", [128, DC, CH], BF16, st)
        t2b = self.sb("ct2", [128, DC, CH], BF16, st)
        xbf = self.sb("cxbf", [128, DC, CH], BF16, st)
        xsq = self.sb("cxsq", [128, DC, CH], BF16, st)
        z2o = self.sb("cz2o", [128, DC, CH], BF16, st)
        mrow = self.sb("cmrow", [1, CH], F32, st)
        rrow = self.sb("crrow", [1, CH], F32, st)
        srow = self.sb("csrow", [1, 2, CH], F32, st)
        rowb = self.sb("crowb", [1, 2, CH], BF16, st)
        mB4 = self.sb("cmB4", [128, CH], BF16, st)
        rB4 = self.sb("crB4", [128, CH], BF16, st)
        z2sums = self.sb("cz2sums", [128, DC, NCH], F32, st)
        wst4s = contextlib.ExitStack()
        wst4 = self.sb("wst4", [128, 1, 1024], BF16, wst4s)

        W = self.W

        self.ev['c_prev'] = self.ev['p3b_done']
        for rc in range(DC):
            def ld(gp, rc=rc):
                self.wait(gp, 'c_prev')
                self.tick('DG', gp.dma_start(out=wst4[:, 0, :], in_=W['attn_W_o'][rc * 128:(rc + 1) * 128, :]), 16,
                          ev=f'c_ld_{rc}')
            self.block.gpsimd(ld)

            def tp(te, rc=rc):
                self.wait(te, f'c_ld_{rc}')
                if rc == 0:
                    self.wait(te, f'b_gnm_{NCH-1}')
                ins = None
                for c in range(DC):
                    ins = te.transpose(self.psTb[:, sl(c, 128)], wst4[:, 0, sl(c, 128)], self.identb[:, :])
                self.tick('T', ins, 1, ev=f'c_tp_{rc}')
            self.block.tensor(tp)

            def ev(vec, rc=rc):
                self.wait(vec, f'c_tp_{rc}')
                ins = vec.tensor_copy(self.woT[:, :, sl(rc, 128)],
                                      self.psTb[:, :].rearrange("p (c r) -> p c r", c=DC))
                self.tick('V', ins, 1, ev=f'c_ev_{rc}')
            self.block.vector(ev)
            self.ev['c_prev'] = self.ev[f'c_ev_{rc}']
        self.ev['c_wdone'] = self.ev['c_prev']
        wst4s.close()
        self.mark('DA', 'c_z2st_-1')

        for jj in range(NCH):
            u0 = (jj + 1) * CH

            def lds(sync, jj=jj):
                if jj == 0:
                    self.wait(sync, f'b_store_{NCH-1}')
                    self.wait(sync, f'b_sgst_{NCH-1}')
                    self.wait(sync, f'b_ost_{NCH-1}')
                else:
                    self.wait(sync, f'c_ogdone_{jj-1}')
                    self.wait(sync, f'c_gnbc_{jj-1}')
                self.tick('DS', sync.dma_start(out=obuf[:, :, :], in_=self.sc_o[jj, :, :, :]), 16)
                self.tick('DS', sync.dma_start(out=sgbuf[:, :, :], in_=self.sc_sg[jj, :, :, :]), 16)
                self.tick('DS', sync.dma_start(out=gnstt[0:1, :, :], in_=self.sc_gn[jj, :, :, :].rearrange("a h t -> () a (h t)")), 16,
                          ev=f'c_ld_{jj}')
            self.block.sync(lds)

            if jj == 0 and 'p4dbg' in self.dbg:
                og0 = self.dbg_t('gnraw0', [1, 2, HH * CH], F32)
                def fdr(sync):
                    self.wait(sync, 'c_ld_0')
                    self.tick('DS', sync.dma_start(out=og0, in_=gnstt[:, :, :]), 16, ev='gnraw0')
                    self.final_evs.append('gnraw0')
                self.block.sync(fdr)

            def gns(vec, jj=jj):
                self.wait(vec, f'c_ld_{jj}')
                if jj == 0:
                    self.wait(vec, 'c_wdone')
                    if 'p4dbg' in self.dbg:
                        self.wait(vec, 'gnraw0')
                vec.tensor_scalar(gnm2[0:1, 0, :], gnstt[0:1, 0, :], self.crow(1.0 / float(VH)), None, ALU.mult)
                self.fence(vec, 'V', vec.tensor_scalar(gnm2[0:1, 1, :], gnstt[0:1, 1, :], self.crow(1.0 / float(VH)), None, ALU.mult))
                self.fence(vec, 'V', vec.tensor_tensor(gnstt[0:1, 0, :], gnm2[0:1, 0, :], gnm2[0:1, 0, :], ALU.mult))
                ins = vec.tensor_tensor(gnm2[0:1, 1, :], gnm2[0:1, 1, :], gnstt[0:1, 0, :], ALU.subtract)
                self.tick('V', ins, 1, ev=f'c_var_{jj}')
            self.block.vector(gns)

            def gnr(sc, jj=jj):
                self.wait(sc, f'c_var_{jj}')
                self.fence(sc, 'S', sc.activation(gnm2[0:1, 1, :], gnm2[0:1, 1, :], AF.Ln, bias=EPS, scale=1.0))
                ins = sc.activation(gnm2[0:1, 1, :], gnm2[0:1, 1, :], AF.Exp, bias=0.0, scale=-0.5)
                self.tick('S', ins, 1, ev=f'c_rstd_{jj}')
            self.block.scalar(gnr)

            def gnc(vec, jj=jj):
                self.wait(vec, f'c_rstd_{jj}')
                self.fence(vec, 'V', vec.tensor_tensor(gnm2[0:1, 0, :], gnm2[0:1, 0, :], gnm2[0:1, 1, :], ALU.mult))
                vec.tensor_copy(gnmb[0:1, 0, :], gnm2[0:1, 0, :])
                ins = vec.tensor_copy(gnmb[0:1, 1, :], gnm2[0:1, 1, :])
                self.tick('V', ins, 1, ev=f'c_gnmb_{jj}')
            self.block.vector(gnc)

            def gbc(t, jj=jj):
                self.wait(t, f'c_gnmb_{jj}')
                if jj >= 1:
                    self.wait(t, f'c_bcev_{jj-1}')
                ins = None
                for h in range(HH):
                    ins = t.matmul(self.psC[:, sl(h, CH)], self.onesbr[0:1, :], gnmb[0:1, 1, sl(h, CH)],
                                   start=True, stop=True)
                for h in range(HH):
                    ins = t.matmul(self.psA[:, 0, sl(h, CH)], self.onesbr[0:1, :], gnmb[0:1, 0, sl(h, CH)],
                                   start=True, stop=True)
                self.tick('T', ins, 1, ev=f'c_gnbc_{jj}')
            self.block.tensor(gbc)

            def ogv(vec, jj=jj):
                self.wait(vec, f'c_gnbc_{jj}')
                vec.tensor_copy(rstdBs[:, :, :], self.psC[:, :].rearrange("p (h t) -> p h t", h=HH))
                ins = vec.tensor_copy(mrstdBs[:, :, :], self.psA[:, 0, :].rearrange("p (h t) -> p h t", h=HH))
                self.tick('V', ins, 1, ev=f'c_bcev_{jj}')
                ins = None
                for h in range(HH):
                    rb2 = rstdBs[:, h, :].unsqueeze(1).broadcast_to((128, 2, CH))
                    mb2 = mrstdBs[:, h, :].unsqueeze(1).broadcast_to((128, 2, CH))
                    vec.tensor_tensor(t2b[:, 2 * h:2 * h + 2, :], obuf[:, 2 * h:2 * h + 2, :], rb2, ALU.mult)
                for h in range(HH):
                    mb2 = mrstdBs[:, h, :].unsqueeze(1).broadcast_to((128, 2, CH))
                    ins = vec.tensor_tensor(t2b[:, 2 * h:2 * h + 2, :], t2b[:, 2 * h:2 * h + 2, :], mb2, ALU.subtract)
                self.tick('V', ins, 1)
                vec.tensor_tensor(og[:, :, :], t2b[:, :, :],
                                  self.gnW[:, :].unsqueeze(2).broadcast_to((128, DC, CH)), ALU.mult)
                vec.tensor_tensor(og[:, :, :], og[:, :, :],
                                  self.gnB[:, :].unsqueeze(2).broadcast_to((128, DC, CH)), ALU.add)
                ins = vec.tensor_tensor(og[:, :, :], og[:, :, :], sgbuf[:, :, :], ALU.mult)
                self.tick('V', ins, 1, ev=f'c_og_{jj}')
            self.block.vector(ogv)

            for mc in range(DC):
                def wom(t, jj=jj, mc=mc):
                    if mc == 0:
                        self.wait(t, f'c_og_{jj}')
                    prev = (jj, mc - 2) if mc >= 2 else ((jj - 1, DC - 2 + mc) if jj >= 1 else None)
                    if prev is not None:
                        self.wait(t, 'c_x2_{}_{}'.format(*prev))
                    ins = None
                    for c in range(DC):
                        ins = t.matmul(self.psB[:, mc % 2, 0:CH], self.woT[:, c, sl(mc, 128)],
                                       og[:, c, :], start=(c == 0), stop=(c == DC - 1))
                    self.tick('T', ins, 1, ev=f'c_wo_{jj}_{mc}')
                self.block.tensor(wom)

                def x2v(vec, jj=jj, mc=mc, u0=u0):
                    self.wait(vec, f'c_wo_{jj}_{mc}')
                    ins = vec.tensor_tensor(self.resid[:, mc, u0:u0 + CH], self.resid[:, mc, u0:u0 + CH],
                                            self.psB[:, mc % 2, 0:CH], ALU.add)
                    self.tick('V', ins, 1, ev=f'c_x2_{jj}_{mc}')
                self.block.vector(x2v)
            self.mark('V', f'c_ogdone_{jj}')
            if jj == 0 and 'p4dbg' in self.dbg:
                od1 = self.dbg_t('og0', [128, DC, CH], BF16)
                od2 = self.dbg_t('rstdB0', [128, HH, CH], BF16)
                od3 = self.dbg_t('gnm0', [1, 2, HH * CH], F32)
                od4 = self.dbg_t('t2b0', [128, DC, CH], BF16)
                od5 = self.dbg_t('obuf0', [128, DC, CH], BF16)
                od6 = self.dbg_t('sgbuf0', [128, DC, CH], BF16)
                def fdz(sync):
                    self.wait(sync, 'c_og_0')
                    self.tick('DS', sync.dma_start(out=od1, in_=og[:, :, :]), 16)
                    self.tick('DS', sync.dma_start(out=od4, in_=t2b[:, :, :]), 16)
                    self.tick('DS', sync.dma_start(out=od5, in_=obuf[:, :, :]), 16)
                    self.tick('DS', sync.dma_start(out=od6, in_=sgbuf[:, :, :]), 16)
                    self.wait(sync, 'c_bcev_0')
                    self.tick('DS', sync.dma_start(out=od2, in_=rstdBs[:, :, :]), 16)
                    self.wait(sync, 'c_gnmb_0')
                    self.tick('DS', sync.dma_start(out=od3, in_=gnm2[:, :, :]), 16, ev='p4dbgst')
                    self.final_evs.append('p4dbgst')
                self.block.sync(fdz)
                self.ev['c_ogdone_0'] = self.ev['p4dbgst']

            def zc(vec, jj=jj, u0=u0):
                self.wait(vec, f'c_x2_{jj}_{DC-1}')
                ins = vec.tensor_copy(xbf[:, :, :], self.resid[:, :, u0:u0 + CH])
                self.tick('V', ins, 1)
                ins = vec.tensor_tensor(xsq[:, :, :], xbf[:, :, :], xbf[:, :, :], ALU.mult)
                self.tick('V', ins, 1, ev=f'c_xsq_{jj}')
            self.block.vector(zc)

            def zst(t, jj=jj):
                self.wait(t, f'c_xsq_{jj}')
                if jj >= 1:
                    self.wait(t, f'c_zvar_{jj-1}')
                ins = None
                for c in range(DC):
                    ins = t.matmul(self.psA[0:1, 2, 0:CH], self.onesb[:, 0:1], xbf[:, c, :],
                                   start=(c == 0), stop=(c == DC - 1))
                self.tick('T', ins, 1)
                for c in range(DC):
                    ins = t.matmul(self.psA[0:1, 2, CH:2 * CH], self.onesb[:, 0:1], xsq[:, c, :],
                                   start=(c == 0), stop=(c == DC - 1))
                self.tick('T', ins, 1, ev=f'c_zsums_{jj}')
            self.block.tensor(zst)

            def zsv(vec, jj=jj):
                self.wait(vec, f'c_zsums_{jj}')
                vec.tensor_scalar(mrow[:, :], self.psA[0:1, 2, 0:CH], self.crow(1.0 / 1024.0), None, ALU.mult)
                vec.tensor_scalar(srow[:, 0, :], self.psA[0:1, 2, CH:2 * CH], self.crow(1.0 / 1024.0), None, ALU.mult)
                vec.tensor_tensor(srow[:, 1, :], mrow[:, :], mrow[:, :], ALU.mult)
                ins = vec.tensor_tensor(rrow[:, :], srow[:, 0, :], srow[:, 1, :], ALU.subtract)
                self.tick('V', ins, 1, ev=f'c_zvar_{jj}')
            self.block.vector(zsv)

            def zss(sc, jj=jj):
                self.wait(sc, f'c_zvar_{jj}')
                self.fence(sc, 'S', sc.activation(rrow[:, :], rrow[:, :], AF.Ln, bias=EPS * EPS, scale=1.0 + EPS))
                ins = sc.activation(rrow[:, :], rrow[:, :], AF.Exp, bias=0.0, scale=-0.5)
                self.tick('S', ins, 1, ev=f'c_zrstd_{jj}')
            self.block.scalar(zss)

            def zrb(vec, jj=jj):
                self.wait(vec, f'c_zrstd_{jj}')
                vec.tensor_copy(rowb[:, 0, :], mrow[:, :])
                ins = vec.tensor_copy(rowb[:, 1, :], rrow[:, :])
                self.tick('V', ins, 1, ev=f'c_zrows_{jj}')
            self.block.vector(zrb)

            def zbc(t, jj=jj):
                self.wait(t, f'c_zrows_{jj}')
                if jj >= 1:
                    self.wait(t, f'c_zbcev_{jj-1}')
                ins = t.matmul(self.psA[:, 1, 0:CH], self.onesbr[0:1, :], rowb[0:1, 0, :], start=True, stop=True)
                self.tick('T', ins, 1)
                ins = t.matmul(self.psA[:, 1, CH:2 * CH], self.onesbr[0:1, :], rowb[0:1, 1, :], start=True, stop=True)
                self.tick('T', ins, 1, ev=f'c_zbc_{jj}')
            self.block.tensor(zbc)

            def zbuild(vec, jj=jj):
                self.wait(vec, f'c_zbc_{jj}')
                vec.tensor_copy(mB4[:, :], self.psA[:, 1, 0:CH])
                ins = vec.tensor_copy(rB4[:, :], self.psA[:, 1, CH:2 * CH])
                self.tick('V', ins, 1, ev=f'c_zbcev_{jj}')
                vec.tensor_tensor(xsq[:, :, :], xbf[:, :, :],
                                  mB4[:, :].unsqueeze(1).broadcast_to((128, DC, CH)), ALU.subtract)
                ins = vec.tensor_tensor(xsq[:, :, :], xsq[:, :, :],
                                        rB4[:, :].unsqueeze(1).broadcast_to((128, DC, CH)), ALU.mult)
                self.tick('V', ins, 1, ev=f'c_zt_{jj}')
            self.block.vector(zbuild)

            def zaff(vec, jj=jj):
                self.wait(vec, f'c_zt_{jj}')
                if jj >= 2:
                    self.wait(vec, f'c_z2st_{jj-2}')
                vec.tensor_tensor(z2o[:, :, :], xsq[:, :, :],
                                  self.lnG[:, :].unsqueeze(2).broadcast_to((128, DC, CH)), ALU.mult)
                vec.tensor_tensor(z2o[:, :, :], z2o[:, :, :],
                                  self.lnB[:, :].unsqueeze(2).broadcast_to((128, DC, CH)), ALU.add)
                ins = vec.tensor_reduce(z2sums[:, :, jj:jj + 1], z2o[:, :, :], AXX, ALU.add)
                self.tick('V', ins, 1, ev=f'c_zaff_{jj}')
            self.block.vector(zaff)

            def z2st(sc, jj=jj, u0=u0):
                self.wait(sc, f'c_zaff_{jj}')
                self.tick('DA', sc.dma_start(out=self.sc_z2[:, :, u0:u0 + CH].rearrange("c p t -> p c t"),
                                             in_=z2o[:, :, :]), 16, ev=f'c_z2st_{jj}')
            self.block.scalar(z2st)

        def fin(vec):
            self.wait(vec, f'c_zaff_{NCH-1}')
            ins = vec.tensor_reduce(self.zsumT[:, :].unsqueeze(2), z2sums[:, :, :], AXX, ALU.add)
            self.tick('V', ins, 1, ev='zsumT_2')
        self.block.vector(fin)
        self.mark('DA', 'p4a_done_da')
        self.mark('V', 'p4a_psC_free')
        self.mark('DA', 'p4a_done')
        self.close_stack('p4_st')
        self.dbg_dump('x2', [128, DC, self.TP], F32, self.resid[:, :, :], [f'c_x2_{NCH-1}_{DC-1}'])

        self.dbg_dump('scz2', [DC, 128, self.TP], BF16, self.sc_z2[:, :, :], [f'c_z2st_{NCH-1}'])

    def emit_finish(self):
        def f(sync):
            for ev in self.final_evs:
                self.wait(sync, ev)
            for k in ['DS', 'DA', 'DG', 'CC']:
                if self.cnt[k] > 0:
                    sync.wait_ge(self.sems[k], self.cnt[k])
        self.block.sync(f)


def host_consts():
    ident = np.eye(128, dtype=np.float32)
    mask_su = np.triu(np.ones((128, 128), np.float32), 1)
    sel4 = np.zeros((4, 4, 128), np.float32)
    for h in range(4):
        sel4[h, h, :] = 1.0
    return ident, mask_su, sel4.reshape(4, 512)


def make_in_maps(inputs, T_loc):
    x = np.asarray(inputs['L0_attn_hidden_states'])
    B = x.shape[0]
    ident, mask_su, sel4 = host_consts()
    in_maps = []
    for core in range(2 * B):
        b, half = core // 2, core % 2
        t0 = half * T_loc
        brow = x[b, t0 - 1] if half == 1 else np.zeros(D, np.float32)
        x0ext = np.concatenate([np.broadcast_to(brow, (CH, D)), x[b, t0:t0 + T_loc]], 0)
        im = {
            'x0': np.ascontiguousarray(x0ext, np.float32),
            'bmask': np.full((128, 1), 1.0 if half else 0.0, np.float32),
            'c_ident': ident, 'c_masksu': mask_su, 'c_sel4': sel4,
        }
        for n in ATTN_W:
            im['attn_' + n] = np.ascontiguousarray(inputs['L0_attn_' + n], np.float32)
        for n in FFN_W:
            im['ffn_' + n] = np.ascontiguousarray(inputs['L0_ffn_' + n], np.float32)
        in_maps.append(im)
    return in_maps


def kernel(**inputs):
    x = np.asarray(inputs['L0_attn_hidden_states'])
    B, T, Dm = x.shape
    T_loc = T // 2
    mk = MK(T_loc)
    nc = mk.build()
    in_maps = make_in_maps(inputs, T_loc)
    res = run_bass_kernel_spmd(nc, in_maps, core_ids=list(range(2 * B)))
    out = np.empty((B, T, D), np.float32)
    for core in range(2 * B):
        b, half = core // 2, core % 2
        out[b, half * T_loc:(half + 1) * T_loc] = res.results[core]['y']
    return out


# version 09
# speedup vs baseline: 1.0466x; 1.0466x over previous
import numpy as np
import ml_dtypes
import concourse.bass as bass
import concourse.mybir as mybir
from concourse.bass_utils import run_bass_kernel_spmd

F32 = mybir.dt.float32
BF16 = mybir.dt.bfloat16
AF = mybir.ActivationFunctionType
ALU = mybir.AluOpType
AXX = mybir.AxisListType.X

D = 1024
DC = 8
KD = 512
HH = 4
VH = 256
NE = 8
HF = 64
CH = 128
SUB = 32
EPS = 1e-5
CLAMP = 80.0

ATTN_W = {'mu_x': [D], 'W_x1': [160, D], 'W_x2': [D, 160], 'x_bias': [5, D],
          'W_r': [KD, D], 'W_k': [KD, D], 'W_v': [D, D], 'W_g': [D, D],
          'A_w': [64, D], 'B_w': [KD, 64], 'b_w': [KD], 'bonus': [HH, CH],
          'gn_weight': [D], 'gn_bias': [D], 'W_o': [D, D]}
FFN_W = {'ln_g': [D], 'ln_b': [D], 'bn_g': [D], 'bn_b': [D], 'bn_mean': [D],
         'bn_var': [D], 'Wr': [D, NE], 'br': [NE], 'W_down': [NE, D, HF],
         'b_down': [NE, HF], 'W_up': [NE, HF, D], 'b_up': [NE, D]}

GRP = {'r': 0, 'w': 1, 'k': 2, 'v': 3, 'g': 4}


def sl(i, w):
    return slice(i * w, (i + 1) * w)


class MK:
    def close_stack(self, snm):
        stk = getattr(self, snm, None)
        if stk is not None:
            stk.close()
            if not hasattr(self, '_closed'):
                self._closed = set()
            self._closed.add(snm)

    def __init__(self, T_loc, dbg=(), phases=99):
        self.T_loc = T_loc
        self.TP = T_loc + CH
        self.NCH = T_loc // CH
        self.tiles = [(0, CH)] + [(CH + i * 256, 256) for i in range(T_loc // 256)]
        self.dbg = set(dbg)
        self.phases = phases
        self.dbg_out = {}
        self.final_evs = []

    def sb(self, name, shape, dt, stack=None):
        return (stack or self.stack).enter_context(self.nc.sbuf_tensor(name, shape, dt))

    def ps(self, name, shape, dt):
        return self.stack.enter_context(self.nc.psum_tensor(name, shape, dt))

    def mksem(self, name):
        s = self.stack.enter_context(self.nc.semaphore(name))
        self.sems[name] = s
        self.cnt[name] = 0

    def tick(self, key, instr, n=1, ev=None):
        self.cnt[key] += n
        instr.then_inc(self.sems[key], n)
        if ev is not None:
            self.ev[ev] = (key, self.cnt[key])
        return instr

    def mark(self, key, ev):
        self.ev[ev] = (key, self.cnt[key])

    def wait(self, eng, ev):
        key, val = self.ev[ev]
        if val > 0:
            eng.wait_ge(self.sems[key], val)

    def fence(self, eng, key, instr):
        self.tick(key, instr)
        eng.wait_ge(self.sems[key], self.cnt[key])
        return instr

    def ccol(self, val, n=1):
        idx = self.CC[val]
        return self.ccst[:, idx:idx + n]

    def crow(self, val):
        idx = self.CC[val]
        return self.ccst[0:1, idx:idx + 1]

    def ccol4(self, val):
        idx = self.CC[val]
        return self.ccst[0:4, idx:idx + 1]

    @staticmethod
    def aview(tile, shape):
        ap = tile[:, :, :] if len(tile.shape) == 3 else tile[:, :]
        flat = ap.rearrange("p a b -> p (a b)") if len(tile.shape) == 3 else ap
        n = 1
        for s in shape[1:]:
            n *= s
        flat = flat[:, 0:n]
        if len(shape) == 3:
            return flat.rearrange("p (a b) -> p a b", a=shape[1])
        return flat

    def dbg_t(self, name, shape, dt=F32):
        ap = self.nc.dram_tensor("dbg_" + name, shape, dt, kind="ExternalOutput").ap()
        self.dbg_out[name] = ap
        return ap

    def dbg_dump(self, name, shape, dt, src, wait_evs):
        if name not in self.dbg:
            return
        out = self.dbg_t(name, shape, dt)
        def f(sync):
            for e in wait_evs:
                self.wait(sync, e)
            self.tick('DS', sync.dma_start(out=out, in_=src() if callable(src) else src),
                      16, ev='dbg_' + name)
            self.final_evs.append('dbg_' + name)
        self.block.sync(f)

    def build(self):
        import contextlib
        self.stack = contextlib.ExitStack()
        nc = self.nc = bass.Bass("TRN2", target_bir_lowering=False, debug=False)
        TP, T_loc, NCH = self.TP, self.T_loc, self.NCH

        self.x0 = nc.dram_tensor("x0", [TP, D], F32, kind="ExternalInput").ap()
        self.bmaskD = nc.dram_tensor("bmask", [128, 1], F32, kind="ExternalInput").ap()
        self.identD = nc.dram_tensor("c_ident", [128, 128], F32, kind="ExternalInput").ap()
        self.masksuD = nc.dram_tensor("c_masksu", [128, 128], F32, kind="ExternalInput").ap()
        self.sel4D = nc.dram_tensor("c_sel4", [4, 512], F32, kind="ExternalInput").ap()
        self.selpD = nc.dram_tensor("c_selp", [32, self.NCH * 128], BF16, kind="ExternalInput").ap()
        self.W = {}
        for n, shp in ATTN_W.items():
            self.W['attn_' + n] = nc.dram_tensor('attn_' + n, shp, F32, kind="ExternalInput").ap()
        for n, shp in FFN_W.items():
            self.W['ffn_' + n] = nc.dram_tensor('ffn_' + n, shp, F32, kind="ExternalInput").ap()
        self.yout = nc.dram_tensor("y", [T_loc, D], F32, kind="ExternalOutput").ap()

        self.sc_z1 = nc.dram_tensor("sc_z1", [DC, 128, TP], BF16).ap()
        self.sc_z2 = nc.dram_tensor("sc_z2", [DC, 128, TP], BF16).ap()
        self.sc_k = nc.dram_tensor("sc_k", [NCH, 128, HH, CH], BF16).ap()
        self.sc_vt = nc.dram_tensor("sc_vt", [NCH, CH, D], BF16).ap()
        self.sc_c = nc.dram_tensor("sc_c", [NCH, 128, HH, CH], F32).ap()
        self.sc_sp = nc.dram_tensor("sc_sp", [NCH, 128, HH, VH], BF16).ap()
        self.sc_o = nc.dram_tensor("sc_o", [NCH, 128, DC, CH], BF16).ap()
        self.sc_sg = nc.dram_tensor("sc_sg", [NCH, 128, DC, CH], BF16).ap()
        self.sc_gn = nc.dram_tensor("sc_gn", [NCH, 2, HH, CH], F32).ap()
        self.sc_stat = nc.dram_tensor("sc_stat", [2, TP], BF16).ap()
        self.cc1_in = nc.dram_tensor("cc1_in", [128, DC], F32).ap()
        self.cc1_out = nc.dram_tensor("cc1_out", [128, DC], F32).ap()
        self.cc2_in = nc.dram_tensor("cc2_in", [128, DC], F32).ap()
        self.cc2_out = nc.dram_tensor("cc2_out", [128, DC], F32).ap()
        self.ccS_in = nc.dram_tensor("ccS_in", [128, HH * VH], F32).ap()
        self.ccS_out = nc.dram_tensor("ccS_out", [2, 128, HH * VH], F32).ap()
        self.RG = [[0, 1], [2, 3], [4, 5], [6, 7]]

        self.resid = self.sb("resid", [128, DC, TP], F32)
        self.identf = self.sb("identf", [128, 128], F32)
        self.identb = self.sb("identb", [128, 128], BF16)
        self.masksu = self.sb("masksu", [128, 128], BF16)
        self.onesb = self.sb("onesb", [128, 1], BF16)
        self.onesbr = self.sb("onesbr", [1, 128], BF16)
        self.onesfr = self.sb("onesfr", [1, 128], F32)
        self.bmk = self.sb("bmk", [128, 1], F32)
        self.ccst = self.sb("ccst", [128, 16], F32)
        self.CC = {}
        self.lnG = self.sb("lnG", [128, DC], F32)
        self.lnB = self.sb("lnB", [128, DC], F32)
        self.bnA = self.sb("bnA", [128, DC], F32)
        self.bnB = self.sb("bnB", [128, DC], F32)
        self.bnsc = self.sb("bnsc", [128, DC], F32)
        self.wrS = self.sb("wrS", [128, DC, NE], BF16)
        self.brS = self.sb("brS", [1, NE], F32)
        self.buAll = self.sb("buAll", [128, NE, DC], F32)
        self.bdAll = self.sb("bdAll", [64, NE], F32)
        self.zbnb = self.sb("zbnb", [128, DC], BF16)
        self.logit = self.sb("logit", [1, NE], F32)
        self.onehot = self.sb("onehot", [1, NE], F32)
        self.selsc = self.sb("selsc", [1, 8], F32)
        self.selrow = self.sb("selrow", [1, NE], F32)
        self.zsum = self.sb("zsum", [128, DC, len(self.tiles)], F32)
        self.zsumT = self.sb("zsumT", [128, DC], F32)

        self.psA = self.ps("psA", [128, 4, 512], F32)
        self.psB = self.ps("psB", [128, 2, 512], F32)
        self.psC = self.ps("psC", [128, 512], F32)
        self.psTb = self.ps("psTb", [128, 1024], BF16)

        import contextlib
        self.ffn_stack = contextlib.ExitStack()
        self.wdAll = self.sb("wdAll", [128, NE, DC, HF], BF16, self.ffn_stack)
        self.wuAll = self.sb("wuAll", [64, NE, DC, 128], BF16, self.ffn_stack)
        self.selmask = self.sb("selmask", [128, NE], F32, self.ffn_stack)
        self.gatec = self.sb("gatec", [128, 1], F32, self.ffn_stack)
        self.bupg = self.sb("bupg", [128, DC], F32, self.ffn_stack)
        self.bdS = self.sb("bdS", [64, 1], F32, self.ffn_stack)
        self.wdSel = self.sb("wdSel", [128, DC, HF], BF16, self.ffn_stack)
        self.wuSel = self.sb("wuSel", [64, DC, 128], BF16, self.ffn_stack)

        self.sems = {}
        self.cnt = {}
        self.ev = {}
        for s in ['T', 'V', 'S', 'G', 'DS', 'DG', 'DA', 'CC']:
            self.mksem(s)

        with nc.Block() as block:
            self.block = block
            self.emit()
        self.stack.close()
        return nc

    def emit(self):
        self.emit_setup()
        self.emit_p1()
        if self.phases >= 1:
            self.emit_sel(1)
            self.p1_stack.close()
            self.emit_p2(1)
            self.ffn_stack.close()
        if self.phases >= 2:
            self.emit_wprep()
            self.emit_p3pro()
            self.emit_p3a()
        if self.phases >= 3:
            self.emit_p3b()
        if self.phases >= 4:
            self.emit_p4a()
        if self.phases >= 5:
            self.emit_sel(2)
            self.emit_p2(2)
        if 'ccwatch' in self.dbg:
            for nm, evn in [('cc_p2', 'p2done_1'), ('cc_wp', 'wp_x2_done'),
                            ('cc_p3a', 'sin_ready'), ('cc_p3b', 'p3b_done')]:
                if evn in self.ev:
                    out = self.dbg_t(nm, [128, 16], F32)
                    def fw(sync, out=out, evn=evn, nm=nm):
                        self.wait(sync, evn)
                        self.tick('DS', sync.dma_start(out=out, in_=self.ccst[:, :]), 16, ev='dbg' + nm)
                        self.final_evs.append('dbg' + nm)
                    self.block.sync(fw)
        for snm in ['p4_st', 'p3b_st', 'p3a_st', 'p3s']:
            if getattr(self, snm, None) is not None and snm not in getattr(self, '_closed', set()):
                self.close_stack(snm)
        self.emit_finish()

    def emit_setup(self):
        W = self.W
        consts = [1.0 / 1024.0, 1.0 / float(2 * self.T_loc), 1.0 / float(VH),
                  0.5, -1.0, 0.0, CLAMP, 1.0, EPS, EPS * EPS, 1.0 + EPS, -0.5]
        for i, v in enumerate(consts):
            self.CC[v] = i
            self.nc.const_aps.register(F32, v, self.ccst[:, i:i + 1])

        def f(sync):
            ctx = self.nc.allow_non_contiguous_dma(reason="tiny one-time vector loads")
            ctx.__enter__()
            for dst, src in [(self.identf[:, :], self.identD[:, :]),
                             (self.bmk[:, :], self.bmaskD[:, :]),
                             (self.lnG[:, :], W['ffn_ln_g'].rearrange("(c p) -> p c", p=128)),
                             (self.lnB[:, :], W['ffn_ln_b'].rearrange("(c p) -> p c", p=128)),
                             (self.brS[:, :], W['ffn_br'][None, :]),
                             (self.bdAll[:, :], W['ffn_b_down'].rearrange("e h -> h e")),
                             (self.buAll[:, :, :], W['ffn_b_up'].rearrange("e (c p) -> p e c", p=128)),
                             (self.bnA[:, :], W['ffn_bn_var'].rearrange("(c p) -> p c", p=128)),
                             (self.bnB[:, :], W['ffn_bn_mean'].rearrange("(c p) -> p c", p=128)),
                             (self.bnsc[:, :], W['ffn_bn_g'].rearrange("(c p) -> p c", p=128)),
                             (self.selmask[:, 0:DC], W['ffn_bn_b'].rearrange("(c p) -> p c", p=128)),
                             ]:
                self.tick('DS', sync.dma_start(out=dst, in_=src), 16)
            ctx.__exit__(None, None, None)
            self.mark('DS', 'setup_f32')
        self.block.sync(f)

        def g(gp):
            for dst, src in [(self.identb[:, :], self.identD[:, :]),
                             (self.masksu[:, :], self.masksuD[:, :]),
                             (self.wrS[:, :, :], W['ffn_Wr'].rearrange("(c p) e -> p c e", p=128)),
                             (self.wdAll[:, :, :, :], W['ffn_W_down'].rearrange("e (c p) h -> p e c h", p=128)),
                             (self.wuAll[:, :, :, :], W['ffn_W_up'].rearrange("e h (c p) -> h e c p", p=128)),
                             ]:
                self.tick('DG', gp.dma_start(out=dst, in_=src), 16)
            self.mark('DG', 'setup_bf16')
        self.block.gpsimd(g)

        def v(vec):
            vec.memset(self.onesb[:, :], 1.0)
            vec.memset(self.onesbr[:, :], 1.0)
            vec.memset(self.onesfr[:, :], 1.0)
            for val, i in self.CC.items():
                vec.memset(self.ccst[:, i:i + 1], val)
            ins = vec.memset(self.zsum[:, :, :], 0.0)
            self.tick('V', ins, 1, ev='setup_memset')
        self.block.vector(v)

        def s(sc):
            self.wait(sc, 'setup_f32')
            self.wait(sc, 'setup_memset')
            self.fence(sc, 'S', sc.activation(self.bnA[:, :], self.bnA[:, :], AF.Ln, bias=EPS, scale=1.0))
            ins = sc.activation(self.bnA[:, :], self.bnA[:, :], AF.Exp, bias=0.0, scale=-0.5)
            self.tick('S', ins, 1, ev='bn_rsqrt')
        self.block.scalar(s)

        def v2(vec):
            self.wait(vec, 'bn_rsqrt')
            self.fence(vec, 'V', vec.tensor_tensor(self.bnA[:, :], self.bnsc[:, :], self.bnA[:, :], ALU.mult))
            self.fence(vec, 'V', vec.tensor_tensor(self.bnB[:, :], self.bnB[:, :], self.bnA[:, :], ALU.mult))
            ins = vec.tensor_tensor(self.bnB[:, :], self.selmask[:, 0:DC], self.bnB[:, :], ALU.subtract)
            self.tick('V', ins, 1, ev='bn_ready')
        self.block.vector(v2)

    def emit_p1(self):
        import contextlib
        pst = self.p1_stack = contextlib.ExitStack()
        TW = 256
        stg = [self.sb("p1stg0", [128, 2, 1024], F32, pst), self.sb("p1stg1", [128, 2, 1024], F32, pst)]
        mrow = self.sb("p1mrow", [1, TW], F32, pst)
        rrow = self.sb("p1rrow", [1, TW], F32, pst)
        mrowb = self.sb("p1mrowb", [1, TW], BF16, pst)
        rrowb = self.sb("p1rrowb", [1, TW], BF16, pst)
        xbf = self.sb("p1xbf", [128, DC, TW], BF16, pst)
        xsq = self.sb("p1xsq", [128, DC, TW], BF16, pst)
        zt = self.sb("p1zt", [128, DC, TW], BF16, pst)
        z1o = [self.sb("p1z1o0", [128, DC, TW], BF16, pst),
               self.sb("p1z1o1", [128, DC, TW], BF16, pst)]
        mB = self.sb("p1mB", [128, TW], BF16, pst)
        rB = self.sb("p1rB", [128, TW], BF16, pst)
        srow = self.sb("p1srow", [1, 2, TW], F32, pst)
        ntile = len(self.tiles)

        gseq = []
        for i, (t0, tl) in enumerate(self.tiles):
            for s in range(tl // 128):
                gseq.append((i, s))
        gidx = {k: n for n, k in enumerate(gseq)}

        for i, (t0, tl) in enumerate(self.tiles):
            nsub = tl // 128
            buf = i % 2

            def dma_in(sync, i=i, t0=t0, tl=tl, buf=buf, nsub=nsub):
                if i >= 2:
                    self.wait(sync, f'p1tpT_{i-2}')
                self.tick('DS', sync.dma_start(
                    out=stg[buf][:, 0:nsub, :],
                    in_=self.x0[t0:t0 + tl, :].rearrange("(s p) d -> p s d", p=128)),
                    16, ev=f'p1in_{i}')
            self.block.sync(dma_in)

            for s in range(nsub):
                def tp(t, i=i, s=s, buf=buf):
                    self.wait(t, f'p1in_{i}')
                    if i == 0 and s == 0:
                        self.wait(t, 'setup_f32')
                    n = gidx[(i, s)]
                    if n >= 2:
                        pi, psb = gseq[n - 2]
                        self.wait(t, f'p1tpev_{pi}_{psb}')
                    b0 = (gidx[(i, s)] % 2) * 2
                    ins = None
                    for c in range(DC):
                        ins = t.transpose(self.psA[:, b0 + c // 4, sl(c % 4, 128)],
                                          stg[buf][:, s, sl(c, 128)], self.identf[:, :])
                    self.tick('T', ins, 1, ev=f'p1tp_{i}_{s}')
                    if s == nsub - 1:
                        self.mark('T', f'p1tpT_{i}')
                self.block.tensor(tp)

                def tpe(vec, i=i, s=s, t0=t0):
                    self.wait(vec, f'p1tp_{i}_{s}')
                    b0 = (gidx[(i, s)] % 2) * 2
                    u = t0 + s * 128
                    vec.tensor_copy(self.resid[:, 0:4, u:u + 128],
                                    self.psA[:, b0, :].rearrange("p (c t) -> p c t", c=4))
                    ins = vec.tensor_copy(self.resid[:, 4:8, u:u + 128],
                                          self.psA[:, b0 + 1, :].rearrange("p (c t) -> p c t", c=4))
                    self.tick('V', ins, 1, ev=f'p1tpev_{i}_{s}')
                self.block.vector(tpe)

            def vc(vec, i=i, t0=t0, tl=tl, nsub=nsub):
                self.wait(vec, f'p1tpev_{i}_{nsub-1}')
                if i >= 1:
                    self.wait(vec, f'p1zaff_{i-1}')
                ins = vec.tensor_copy(xbf[:, :, 0:tl], self.resid[:, :, t0:t0 + tl])
                self.tick('V', ins, 1)
                ins = vec.tensor_tensor(xsq[:, :, 0:tl], xbf[:, :, 0:tl], xbf[:, :, 0:tl], ALU.mult)
                self.tick('V', ins, 1, ev=f'p1xsq_{i}')
            self.block.vector(vc)

            def st(t, i=i, tl=tl):
                self.wait(t, f'p1xsq_{i}')
                if i >= 1:
                    self.wait(t, f'p1var_{i-1}')
                ins = None
                for c in range(DC):
                    ins = t.matmul(self.psB[0:1, 0, 0:tl], self.onesb[:, 0:1], xbf[:, c, 0:tl],
                                   start=(c == 0), stop=(c == DC - 1))
                self.tick('T', ins, 1)
                for c in range(DC):
                    ins = t.matmul(self.psB[0:1, 1, 0:tl], self.onesb[:, 0:1], xsq[:, c, 0:tl],
                                   start=(c == 0), stop=(c == DC - 1))
                self.tick('T', ins, 1, ev=f'p1sums_{i}')
            self.block.tensor(st)

            def sv(vec, i=i, tl=tl):
                self.wait(vec, f'p1sums_{i}')
                vec.tensor_scalar(mrow[:, 0:tl], self.psB[0:1, 0, 0:tl], self.crow(1.0 / 1024.0), None, ALU.mult)
                vec.tensor_scalar(srow[:, 0, 0:tl], self.psB[0:1, 1, 0:tl], self.crow(1.0 / 1024.0), None, ALU.mult)
                vec.tensor_tensor(srow[:, 1, 0:tl], mrow[:, 0:tl], mrow[:, 0:tl], ALU.mult)
                ins = vec.tensor_tensor(rrow[:, 0:tl], srow[:, 0, 0:tl], srow[:, 1, 0:tl], ALU.subtract)
                self.tick('V', ins, 1, ev=f'p1var_{i}')
            self.block.vector(sv)

            def ss(sc, i=i, tl=tl):
                self.wait(sc, f'p1var_{i}')
                sc.activation(rrow[:, 0:tl], rrow[:, 0:tl], AF.Ln, bias=EPS * EPS, scale=1.0 + EPS)
                ins = sc.activation(rrow[:, 0:tl], rrow[:, 0:tl], AF.Exp, bias=0.0, scale=-0.5)
                self.tick('S', ins, 1, ev=f'p1rstd_{i}')
            self.block.scalar(ss)

            def sv2(vec, i=i, tl=tl):
                self.wait(vec, f'p1rstd_{i}')
                vec.tensor_copy(mrowb[:, 0:tl], mrow[:, 0:tl])
                ins = vec.tensor_copy(rrowb[:, 0:tl], rrow[:, 0:tl])
                self.tick('V', ins, 1, ev=f'p1rowsb_{i}')
            self.block.vector(sv2)

            def bc(t, i=i, tl=tl):
                self.wait(t, f'p1rowsb_{i}')
                if i >= 1:
                    self.wait(t, f'p1bcev_{i-1}')
                ins = t.matmul(self.psC[:, 0:tl], self.onesbr[0:1, :], mrowb[0:1, 0:tl], start=True, stop=True)
                self.tick('T', ins, 1)
                ins = t.matmul(self.psC[:, 256:256 + tl], self.onesbr[0:1, :], rrowb[0:1, 0:tl], start=True, stop=True)
                self.tick('T', ins, 1, ev=f'p1bc_{i}')
            self.block.tensor(bc)

            def bce(vec, i=i, tl=tl):
                self.wait(vec, f'p1bc_{i}')
                vec.tensor_copy(mB[:, 0:tl], self.psC[:, 0:tl])
                ins = vec.tensor_copy(rB[:, 0:tl], self.psC[:, 256:256 + tl])
                self.tick('V', ins, 1, ev=f'p1bcev_{i}')
            self.block.vector(bce)

            def zb(vec, i=i, tl=tl):
                self.wait(vec, f'p1bcev_{i}')
                vec.tensor_tensor(zt[:, :, 0:tl], xbf[:, :, 0:tl],
                                  mB[:, 0:tl].unsqueeze(1).broadcast_to((128, DC, tl)), ALU.subtract)
                ins = vec.tensor_tensor(zt[:, :, 0:tl], zt[:, :, 0:tl],
                                        rB[:, 0:tl].unsqueeze(1).broadcast_to((128, DC, tl)), ALU.mult)
                self.tick('V', ins, 1, ev=f'p1zt_{i}')
            self.block.vector(zb)

            def za(vec, i=i, tl=tl, buf=buf):
                self.wait(vec, f'p1zt_{i}')
                if i >= 2:
                    self.wait(vec, f'p1zout_{i-2}')
                vec.tensor_tensor(z1o[buf][:, :, 0:tl], zt[:, :, 0:tl],
                                  self.lnG[:, :].unsqueeze(2).broadcast_to((128, DC, tl)), ALU.mult)
                vec.tensor_tensor(z1o[buf][:, :, 0:tl], z1o[buf][:, :, 0:tl],
                                  self.lnB[:, :].unsqueeze(2).broadcast_to((128, DC, tl)), ALU.add)
                ins = vec.tensor_reduce(self.zsum[:, :, i:i + 1], z1o[buf][:, :, 0:tl], AXX, ALU.add)
                self.tick('V', ins, 1, ev=f'p1zaff_{i}')
            self.block.vector(za)

            def zs(sc, i=i, t0=t0, tl=tl, buf=buf):
                self.wait(sc, f'p1zaff_{i}')
                self.tick('DA', sc.dma_start(out=self.sc_z1[:, :, t0:t0 + tl].rearrange("c p t -> p c t"),
                                             in_=z1o[buf][:, :, 0:tl]),
                          16, ev=f'p1zout_{i}')
            self.block.scalar(zs)

        def fin(vec):
            self.wait(vec, f'p1zaff_{ntile-1}')
            ins = vec.tensor_reduce(self.zsumT[:, :].unsqueeze(2), self.zsum[:, :, 1:ntile], AXX, ALU.add)
            self.tick('V', ins, 1, ev='zsumT_1')
        self.block.vector(fin)
        self.mark('DA', 'p1_all_z_stored')

        self.dbg_dump('z1', [DC, 128, self.TP], BF16, self.sc_z1[:, :, :],
                      [f'p1zout_{ntile-1}'])
        self.dbg_dump('zsum', [128, DC, len(self.tiles)], F32, self.zsum[:, :, :], ['zsumT_1'])
        self.dbg_dump('z1o', [128, DC, 256], BF16, z1o[(len(self.tiles) - 1) % 2][:, :, :], [f'p1zaff_{ntile-1}'])
        self.dbg_dump('lnG', [128, DC], F32, self.lnG[:, :], ['setup_f32'])
        self.dbg_dump('lnB', [128, DC], F32, self.lnB[:, :], ['setup_f32'])
        self.dbg_dump('zt', [128, DC, 256], BF16, zt[:, :, :], [f'p1zaff_{ntile-1}'])
        self.dbg_dump('xbf', [128, DC, 256], BF16, xbf[:, :, :], [f'p1zaff_{ntile-1}'])
        self.dbg_dump('mB', [128, 256], BF16, mB[:, :], [f'p1bcev_{ntile-1}'])
        self.dbg_dump('rB', [128, 256], BF16, rB[:, :], [f'p1bcev_{ntile-1}'])
        self.dbg_dump('mrow', [1, 256], F32, mrow[:, :], [f'p1rstd_{ntile-1}'])
        self.dbg_dump('rrow', [1, 256], F32, rrow[:, :], [f'p1rstd_{ntile-1}'])
        self.dbg_dump('zsumT', [128, DC], F32, self.zsumT[:, :], ['zsumT_1'])

    def emit_sel(self, p):
        cc_in = self.cc1_in if p == 1 else self.cc2_in
        cc_out = self.cc1_out if p == 1 else self.cc2_out
        src_ev = 'zsumT_1' if p == 1 else 'zsumT_2'
        if p == 2:
            import contextlib
            self.ffn_stack = contextlib.ExitStack()
            self.selmask = self.sb("selmask2", [128, NE], F32, self.ffn_stack)
            self.gatec = self.sb("gatec2", [128, 1], F32, self.ffn_stack)
            self.bupg = self.sb("bupg2", [128, DC], F32, self.ffn_stack)
            self.bdS = self.sb("bdS2", [64, 1], F32, self.ffn_stack)
            self.wdSel = self.sb("wdSel2", [128, DC, HF], BF16, self.ffn_stack)
            self.wuSel = self.sb("wuSel2", [64, DC, 128], BF16, self.ffn_stack)
            self.ffn2_tmp = contextlib.ExitStack()
            self.wdAll = self.sb("wdAll2", [128, NE, DC, HF], BF16, self.ffn2_tmp)
            self.wuAll = self.sb("wuAll2", [64, NE, DC, 128], BF16, self.ffn2_tmp)
            W = self.W
            def rew(gp):
                self.wait(gp, 'zsumT_2')
                self.wait(gp, f'c_z2st_{self.NCH-1}')
                self.tick('DG', gp.dma_start(out=self.wdAll[:, :, :, :], in_=W['ffn_W_down'].rearrange("e (c p) h -> p e c h", p=128)), 16)
                self.tick('DG', gp.dma_start(out=self.wuAll[:, :, :, :], in_=W['ffn_W_up'].rearrange("e h (c p) -> h e c p", p=128)), 16)
                self.mark('DG', 'ffnw2')
            self.block.gpsimd(rew)

        def g(gp):
            self.wait(gp, src_ev)
            self.tick('DG', gp.dma_start(out=cc_in[:, :], in_=self.zsumT[:, :]), 16)
            gp.wait_ge(self.sems['DG'], self.cnt['DG'])
            ins = gp.collective_compute(
                "AllReduce", ALU.add, replica_groups=self.RG,
                ins=[cc_in[:, :]], outs=[cc_out[:, :]])
            self.tick('CC', ins, 1, ev=f'cc_z_{p}')
        self.block.gpsimd(g)

        def f(sync):
            self.wait(sync, f'cc_z_{p}')
            self.tick('DS', sync.dma_start(out=self.zsumT[:, :], in_=cc_out[:, :]), 16, ev=f'zsent_{p}')
        self.block.sync(f)

        zbnb = self.zbnb
        if p == 1 and 'cc1' in self.dbg:
            o1 = self.dbg_t('cc1in', [128, DC])
            o2 = self.dbg_t('cc1out', [128, DC])
            def fcc(sync):
                self.wait(sync, 'cc_z_1')
                self.tick('DS', sync.dma_start(out=o1, in_=self.cc1_in[:, :]), 16)
                self.tick('DS', sync.dma_start(out=o2, in_=self.cc1_out[:, :]), 16, ev='dbgcc')
                self.final_evs.append('dbgcc')
            self.block.sync(fcc)

        def v(vec):
            self.wait(vec, f'zsent_{p}')
            if p == 1:
                self.wait(vec, 'bn_ready')
            self.fence(vec, 'V', vec.tensor_scalar(self.zsumT[:, :], self.zsumT[:, :], self.ccol(1.0 / float(2 * self.T_loc)), None, ALU.mult))
            self.fence(vec, 'V', vec.tensor_tensor(self.zsumT[:, :], self.zsumT[:, :], self.bnA[:, :], ALU.mult))
            self.fence(vec, 'V', vec.tensor_tensor(self.zsumT[:, :], self.zsumT[:, :], self.bnB[:, :], ALU.add))
            ins = vec.tensor_copy(zbnb[:, :], self.zsumT[:, :])
            self.tick('V', ins, 1, ev=f'zbn_{p}')
        self.block.vector(v)

        def t(te):
            self.wait(te, f'zbn_{p}')
            if p == 1:
                self.wait(te, 'setup_bf16')
                self.wait(te, f'p1bcev_{len(self.tiles)-1}')
            else:
                self.wait(te, 'p4a_psC_free')
            ins = None
            for c in range(DC):
                ins = te.matmul(self.psC[0:1, 0:NE], zbnb[:, c:c + 1], self.wrS[:, c, :],
                                start=(c == 0), stop=(c == DC - 1))
            self.tick('T', ins, 1, ev=f'logmm_{p}')
        self.block.tensor(t)

        def v2(vec):
            self.wait(vec, f'logmm_{p}')
            ins = vec.tensor_tensor(self.logit[:, :], self.psC[0:1, 0:NE], self.brS[:, :], ALU.add)
            self.fence(vec, 'V', ins)
            self.mark('V', f'logit_{p}')
            self.fence(vec, 'V', vec.tensor_reduce(self.selsc[:, 0:1], self.logit[:, :], AXX, ALU.max))
            ins = vec.tensor_scalar(self.selsc[:, 1:2], self.selsc[:, 0:1], self.crow(-1.0), None, ALU.mult)
            self.tick('V', ins, 1, ev=f'logmax_{p}')
        self.block.vector(v2)

        def s(sc):
            self.wait(sc, f'logmax_{p}')
            ins = sc.activation(self.selrow[:, :], self.logit[:, :], AF.Exp,
                                bias=self.selsc[0:1, 1:2], scale=1.0, accum_out=self.selsc[:, 2:3])
            self.tick('S', ins, 1, ev=f'sumexp_{p}')
        self.block.scalar(s)

        def v3(vec):
            self.wait(vec, f'sumexp_{p}')
            self.fence(vec, 'V', vec.reciprocal(self.selsc[:, 3:4], self.selsc[:, 2:3]))
            self.fence(vec, 'V', vec.tensor_scalar(self.selsc[:, 4:5], self.selsc[:, 3:4], self.crow(0.5), None, ALU.mult))
            ins = vec.tensor_scalar(self.onehot[:, :], self.logit[:, :], self.selsc[0:1, 0:1], None, ALU.is_equal)
            self.tick('V', ins, 1, ev=f'onehot_{p}')
        self.block.vector(v3)

        def t2(te):
            self.wait(te, f'onehot_{p}')
            ins = te.matmul(self.psC[:, 8:16], self.onesfr[0:1, :], self.onehot[0:1, :], start=True, stop=True)
            self.tick('T', ins, 1)
            ins = te.matmul(self.psC[:, 16:17], self.onesfr[0:1, :], self.selsc[0:1, 4:5], start=True, stop=True)
            self.tick('T', ins, 1, ev=f'selbc_{p}')
        self.block.tensor(t2)

        def v4(vec):
            self.wait(vec, f'selbc_{p}')
            vec.tensor_copy(self.selmask[:, :], self.psC[:, 8:16])
            ins = vec.tensor_copy(self.gatec[:, :], self.psC[:, 16:17])
            self.tick('V', ins, 1, ev=f'selbcv_{p}')
        self.block.vector(v4)

        def s2(sc):
            self.wait(sc, f'selbcv_{p}')
            if p == 1:
                self.wait(sc, 'setup_bf16')
            else:
                self.wait(sc, 'ffnw2')
            sc.activation(self.wdSel[:, :, :], self.wdAll[:, 0, :, :], AF.Identity,
                          bias=0.0, scale=self.selmask[:, 0:1])
            sc.activation(self.wuSel[:, :, :], self.wuAll[:, 0, :, :], AF.Identity,
                          bias=0.0, scale=self.selmask[0:64, 0:1])
            sc.activation(self.bdS[:, :], self.bdAll[:, 0:1], AF.Identity,
                          bias=0.0, scale=self.selmask[0:64, 0:1])
            ins = sc.activation(self.bupg[:, :], self.buAll[:, 0, :], AF.Identity,
                                bias=0.0, scale=self.selmask[:, 0:1])
            self.tick('S', ins, 1, ev=f'sel0_{p}')
        self.block.scalar(s2)

        def v5(vec):
            self.wait(vec, f'sel0_{p}')
            for e in range(1, NE):
                vec.scalar_tensor_tensor(self.wdSel[:, :, :], self.wdAll[:, e, :, :],
                                         self.selmask[:, e:e + 1], self.wdSel[:, :, :], ALU.mult, ALU.add)
                vec.scalar_tensor_tensor(self.wuSel[:, :, :], self.wuAll[:, e, :, :],
                                         self.selmask[0:64, e:e + 1], self.wuSel[:, :, :], ALU.mult, ALU.add)
                self.fence(vec, 'V', vec.scalar_tensor_tensor(self.bdS[:, :], self.bdAll[:, e:e + 1],
                                     self.selmask[0:64, e:e + 1], self.bdS[:, :], ALU.mult, ALU.add))
                self.fence(vec, 'V', vec.scalar_tensor_tensor(self.bupg[:, :], self.buAll[:, e, :],
                                     self.selmask[:, e:e + 1], self.bupg[:, :], ALU.mult, ALU.add))
            ins = vec.tensor_scalar(self.bupg[:, :], self.bupg[:, :], self.gatec[:, 0:1], None, ALU.mult)
            self.tick('V', ins, 1, ev=f'selw_{p}')
        self.block.vector(v5)

        if p == 2:
            self.ffn2_tmp.close()

        if f'seldbg_{p}' in self.dbg:
            out = self.dbg_t(f'seldbg_{p}', [3, NE], F32)
            def fd(sync):
                self.wait(sync, f'selw_{p}')
                self.tick('DS', sync.dma_start(out=out[0:1, :], in_=self.logit[:, :]), 16)
                self.tick('DS', sync.dma_start(out=out[1:2, :], in_=self.onehot[:, :]), 16)
                self.tick('DS', sync.dma_start(out=out[2:3, :], in_=self.selsc[:, :]), 16, ev=f'dbgsel_{p}')
                self.final_evs.append(f'dbgsel_{p}')
            self.block.sync(fd)

    def emit_p2(self, p):
        import contextlib
        pst = contextlib.ExitStack()
        sc_z = self.sc_z1 if p == 1 else self.sc_z2
        TW = 512
        if p == 1:
            tiles = [(0, CH)] + [(CH + i * TW, TW) for i in range(self.T_loc // TW)]
        else:
            tiles = [(CH + i * TW, TW) for i in range(self.T_loc // TW)]
        zin = [self.sb(f"p2zin0_{p}", [128, DC, TW], BF16, pst), self.sb(f"p2zin1_{p}", [128, DC, TW], BF16, pst)]
        hbf = self.sb(f"p2h_{p}", [64, TW], BF16, pst)
        tmpt = self.sb(f"p2tmp_{p}", [128, TW], F32, pst)
        if p == 2:
            ostg = [self.sb("p4ostg0", [128, TW // 128, 1024], F32, pst), self.sb("p4ostg1", [128, TW // 128, 1024], F32, pst)]

        for ii, (t0, tl) in enumerate(tiles):
            buf = ii % 2

            def zld(sync, ii=ii, t0=t0, tl=tl, buf=buf, p=p):
                if p == 1:
                    self.wait(sync, 'p1_all_z_stored')
                else:
                    self.wait(sync, 'p4a_done')
                    if ii < 2:
                        self.wait(sync, f'selw_{p}')
                if ii >= 2:
                    self.wait(sync, f'p2dn_{p}_{ii-2}')
                self.tick('DS', sync.dma_start(out=zin[buf][:, :, 0:tl],
                                               in_=sc_z[:, :, t0:t0 + tl].rearrange("c p t -> p c t")),
                          16, ev=f'p2zin_{p}_{ii}')
            self.block.sync(zld)

            def dn(t, ii=ii, tl=tl, buf=buf, p=p):
                self.wait(t, f'p2zin_{p}_{ii}')
                self.wait(t, f'selw_{p}')
                if ii >= 1:
                    self.wait(t, f'p2h_{p}_{ii-1}')
                ins = None
                for c in range(DC):
                    ins = t.matmul(self.psA[0:64, 0, 0:tl], self.wdSel[:, c, :], zin[buf][:, c, 0:tl],
                                   start=(c == 0), stop=(c == DC - 1))
                self.tick('T', ins, 1, ev=f'p2dn_{p}_{ii}')
            self.block.tensor(dn)

            def hre(sc, ii=ii, tl=tl, p=p):
                self.wait(sc, f'p2dn_{p}_{ii}')
                ins = sc.activation(hbf[:, 0:tl], self.psA[0:64, 0, 0:tl], AF.Relu,
                                    bias=self.bdS[:, 0:1], scale=1.0)
                self.tick('S', ins, 1, ev=f'p2h_{p}_{ii}')
            self.block.scalar(hre)

            for c in range(DC):
                def up(t, ii=ii, tl=tl, c=c, p=p):
                    if c == 0:
                        self.wait(t, f'p2h_{p}_{ii}')
                    prev = (ii, c - 2) if c >= 2 else ((ii - 1, DC - 2 + c) if ii >= 1 else None)
                    if prev is not None:
                        self.wait(t, 'p2ev_{}_{}_{}'.format(p, *prev))
                    ins = t.matmul(self.psB[:, c % 2, 0:tl], self.wuSel[0:64, c, :], hbf[0:64, 0:tl],
                                   start=True, stop=True)
                    self.tick('T', ins, 1, ev=f'p2up_{p}_{ii}_{c}')
                self.block.tensor(up)

                def ev(vec, ii=ii, t0=t0, tl=tl, c=c, p=p):
                    self.wait(vec, f'p2up_{p}_{ii}_{c}')
                    vec.tensor_scalar(tmpt[:, 0:tl], self.psB[:, c % 2, 0:tl],
                                      self.gatec[:, 0:1], self.bupg[:, c:c + 1], ALU.mult, op1=ALU.add)
                    ins = vec.tensor_tensor(self.resid[:, c, t0:t0 + tl], self.resid[:, c, t0:t0 + tl],
                                            tmpt[:, 0:tl], ALU.add)
                    self.tick('V', ins, 1, ev=f'p2ev_{p}_{ii}_{c}')
                self.block.vector(ev)

            self.mark('V', f'p2tile_{p}_{ii}')

            if p == 2:
                nsub = tl // 128
                for s in range(nsub):
                    def otp(t, ii=ii, t0=t0, s=s):
                        self.wait(t, f'p2tile_2_{ii}')
                        if ii >= 2:
                            self.wait(t, f'yout_{ii-2}')
                        if s >= 1:
                            self.wait(t, f'p4otpev_{ii}_{s-1}')
                        elif ii >= 1:
                            self.wait(t, f'p4otpev_{ii-1}_{(tiles[ii-1][1] // 128) - 1}')
                        u = t0 + s * 128
                        ins = None
                        for c in range(DC):
                            ins = t.transpose(self.psA[:, 2 + c // 4, sl(c % 4, 128)],
                                              self.resid[:, c, u:u + 128], self.identf[:, :])
                        self.tick('T', ins, 1, ev=f'p4otp_{ii}_{s}')
                    self.block.tensor(otp)

                    def otpe(vec, ii=ii, s=s, buf=buf):
                        self.wait(vec, f'p4otp_{ii}_{s}')
                        vec.tensor_copy(ostg[buf][:, s, 0:512], self.psA[:, 2, :])
                        ins = vec.tensor_copy(ostg[buf][:, s, 512:1024], self.psA[:, 3, :])
                        self.tick('V', ins, 1, ev=f'p4otpev_{ii}_{s}')
                    self.block.vector(otpe)

                def ost(sync, ii=ii, t0=t0, tl=tl, buf=buf, nsub=nsub):
                    self.wait(sync, f'p4otpev_{ii}_{nsub-1}')
                    self.tick('DS', sync.dma_start(
                        out=self.yout[t0 - CH:t0 - CH + tl, :].rearrange("(s p) d -> p s d", p=128),
                        in_=ostg[buf][:, 0:nsub, :]),
                        16, ev=f'yout_{ii}')
                self.block.sync(ost)

        pst.close()
        if p == 2:
            self.ffn_stack.close()
            for ii in range(len(tiles)):
                self.final_evs.append(f'yout_{ii}')
        else:
            self.mark('V', 'p2done_1')
            self.mark('S', 'p2_s_done')
            self.dbg_dump('x1', [128, DC, self.TP], F32, self.resid[:, :, :],
                          [f'p2ev_1_{len(tiles)-1}_{DC-1}'] + [f'p2ev_1_{i}_{DC-1}' for i in range(len(tiles) - 1)])



    def emit_wprep(self):
        import contextlib
        self.p3s = contextlib.ExitStack()
        TP = self.TP
        self.zlA = self.sb("zlA", [128, TP], BF16, self.p3s)
        self.zlB = self.sb("zlB", [128, (self.NCH // 4) * CH], BF16, self.p3s)
        self.Ccum = self.sb("Ccum", [128, HH, self.NCH + 1], F32, self.p3s)
        self.sinb = self.sb("sinb", [128, HH, VH], BF16, self.p3s)
        self.muxc = self.sb("muxc", [128, DC], F32, self.p3s)
        self.xbias5 = self.sb("xbias5", [128, 5, DC], F32, self.p3s)
        self.bwc = self.sb("bwc", [128, 4], F32, self.p3s)
        self.uc = self.sb("uc", [128, HH], F32, self.p3s)
        self.gnW = self.sb("gnW", [128, DC], F32, self.p3s)
        self.gnB = self.sb("gnB", [128, DC], F32, self.p3s)
        self.yprevB = self.sb("yprevB", [128, DC, 1], BF16, self.p3s)

        self.wx2T = self.sb("wx2T", [128, 16, 128], BF16, self.p3s)
        self.p3a_st = contextlib.ExitStack()
        st = self.p3a_st
        self.Sst = self.sb("Sst", [128, HH, VH], F32, st)
        self.yprevA = self.sb("yprevA", [128, DC, 1], BF16, st)
        self.zro = self.sb("zro", [128, CH], F32, st)
        self.wx1T = self.sb("wx1T", [128, DC, 160], BF16, st)
        self.awT = self.sb("awT", [128, DC, 64], BF16, st)
        self.bwT = self.sb("bwT", [64, 4, 128], BF16, st)
        self.wkT = self.sb("wkT", [128, DC, 512], BF16, st)
        self.wvT = self.sb("wvT", [128, DC, 1024], BF16, st)
        import contextlib as _ct
        wpst = _ct.ExitStack()
        wst = self.sb("wst", [128, 4, 1024], BF16, wpst)
        wx2tmp = self.sb("wx2tmp", [32, 128], BF16, wpst)

        W = self.W

        def vecs(sync):
            self.wait(sync, 'p2done_1')
            self.wait(sync, 'p2_s_done')
            ctx = self.nc.allow_non_contiguous_dma(reason="tiny vector loads")
            ctx.__enter__()
            for dst, src_ in [(self.muxc[:, :], W['attn_mu_x'].rearrange("(c p) -> p c", p=128)),
                              (self.xbias5[:, :, :], W['attn_x_bias'].rearrange("n (c p) -> p n c", p=128)),
                              (self.bwc[:, :], W['attn_b_w'].rearrange("(m p) -> p m", p=128)),
                              (self.uc[:, :], W['attn_bonus'].rearrange("h p -> p h")),
                              (self.gnW[:, :], W['attn_gn_weight'].rearrange("(c p) -> p c", p=128)),
                              (self.gnB[:, :], W['attn_gn_bias'].rearrange("(c p) -> p c", p=128)),
                              ]:
                self.tick('DS', sync.dma_start(out=dst, in_=src_), 16)
            ctx.__exit__(None, None, None)
            self.mark('DS', 'wp_vecs')
        self.block.sync(vecs)

        def vhalf(vec):
            self.wait(vec, 'wp_vecs')
            vec.memset(self.zro[:, :], 0.0)
            self.fence(vec, 'V', vec.tensor_scalar(self.gnW[:, :], self.gnW[:, :], self.ccol(0.5), None, ALU.mult))
            ins = vec.tensor_scalar(self.gnB[:, :], self.gnB[:, :], self.ccol(0.5), None, ALU.mult)
            self.tick('V', ins, 1, ev='wp_gn')
        self.block.vector(vhalf)

        self.mark('V', 'wp_prev')
        self.mark('T', 'wp_prevT')

        def load_T(tag, hbm, R, out):
            rch = (R + 127) // 128
            for g0 in range(0, rch, 4):
                gn = min(4, rch - g0)
                def ld(gp, hbm=hbm, R=R, g0=g0, gn=gn, tag=tag):
                    self.wait(gp, 'wp_prev')
                    for k in range(gn):
                        rc = g0 + k
                        rr = min(128, R - rc * 128)
                        self.tick('DG', gp.dma_start(out=wst[0:rr, k, 0:D], in_=hbm[rc * 128:rc * 128 + rr, :]), 16)
                    self.mark('DG', f'wp_ld_{tag}_{g0}')
                self.block.gpsimd(ld)

                for k in range(gn):
                    rc = g0 + k
                    rr = min(128, R - rc * 128)
                    def tp(te, k=k, rc=rc, rr=rr, tag=tag, g0=g0):
                        self.wait(te, f'wp_ld_{tag}_{g0}')
                        self.wait(te, 'setup_bf16')
                        if rc == 0:
                            self.wait(te, 'wp_prev')
                        else:
                            self.wait(te, f'wp_ev_{tag}_{rc-1}')
                        ins = None
                        for c in range(DC):
                            ins = te.transpose(self.psTb[:, c * 128:c * 128 + rr], wst[0:rr, k, sl(c, 128)],
                                               self.identb[0:rr, 0:rr])
                        self.tick('T', ins, 1, ev=f'wp_tp_{tag}_{rc}')
                    self.block.tensor(tp)

                    def ev(vec, rc=rc, rr=rr, tag=tag, out=out):
                        self.wait(vec, f'wp_tp_{tag}_{rc}')
                        ins = vec.tensor_copy(out[:, :, rc * 128:rc * 128 + rr],
                                              self.psTb[:, :].rearrange("p (c r) -> p c r", c=DC)[:, :, 0:rr])
                        self.tick('V', ins, 1, ev=f'wp_ev_{tag}_{rc}')
                    self.block.vector(ev)
                self.ev['wp_prev'] = self.ev[f'wp_ev_{tag}_{g0 + gn - 1}']
            self.mark('T', 'wp_prevT')

        load_T('k', W['attn_W_k'], KD, self.wkT)
        load_T('v', W['attn_W_v'], D, self.wvT)
        load_T('x1', W['attn_W_x1'], 160, self.wx1T)
        load_T('aw', W['attn_A_w'], 64, self.awT)
        self.ev['wp_done_k'] = self.ev['wp_ev_k_3']
        self.ev['wp_done_v'] = self.ev['wp_ev_v_7']
        self.ev['wp_done_x1'] = self.ev['wp_ev_x1_1']
        self.ev['wp_done_aw'] = self.ev['wp_ev_aw_0']

        def ldbw(gp):
            self.wait(gp, 'wp_prev')
            self.tick('DG', gp.dma_start(out=wst[:, 0:4, 0:64],
                                         in_=W['attn_B_w'].rearrange("(m p) h -> p m h", p=128)),
                      16, ev='wp_ld_bw')
        self.block.gpsimd(ldbw)

        def tpbw(te):
            self.wait(te, 'wp_ld_bw')
            self.wait(te, 'wp_prev')
            ins = None
            for m in range(4):
                ins = te.transpose(self.psTb[0:64, m * 128:(m + 1) * 128], wst[:, m, 0:64], self.identb[:, :])
            self.tick('T', ins, 1, ev='wp_tp_bw')
        self.block.tensor(tpbw)

        def evbw(vec):
            self.wait(vec, 'wp_tp_bw')
            ins = vec.tensor_copy(self.bwT[:, :, :], self.psTb[0:64, 0:512].rearrange("p (m t) -> p m t", m=4))
            self.tick('V', ins, 1, ev='wp_done_bw')
        self.block.vector(evbw)
        self.ev['wp_prev'] = self.ev['wp_done_bw']

        def zx2(vec):
            ins = vec.memset(self.wx2T[:, :, :], 0.0)
            self.tick('V', ins, 1, ev='wp_x2_zero')
        self.block.vector(zx2)

        for ch0 in (0, 4):
            def ldx2(gp, ch0=ch0):
                self.wait(gp, 'wp_prev')
                self.tick('DG', gp.dma_start(out=wst[:, 0:4, 0:160],
                                             in_=W['attn_W_x2'][ch0 * 128:(ch0 + 4) * 128, :].rearrange("(c p) r -> p c r", p=128)),
                          16, ev=f'wp_ld_x2_{ch0}')
            self.block.gpsimd(ldx2)

            last = [None]
            for g in range(5):
                for ck in range(4):
                    c = ch0 + ck
                    def tpx2(te, g=g, ck=ck, c=c, ch0=ch0):
                        self.wait(te, f'wp_ld_x2_{ch0}')
                        if g == 0 and ck == 0:
                            self.wait(te, 'wp_prev')
                        else:
                            pg, pc = (g, c - 1) if ck > 0 else (g - 1, ch0 + 3)
                            self.wait(te, f'wp_x2ev_{pg}_{pc}')
                        ins = te.transpose(self.psTb[0:32, 0:128], wst[:, ck, g * 32:(g + 1) * 32], self.identb[:, :])
                        self.tick('T', ins, 1, ev=f'wp_x2tp_{g}_{c}')
                    self.block.tensor(tpx2)

                    def evx2(vec, g=g, c=c):
                        self.wait(vec, f'wp_x2tp_{g}_{c}')
                        if g == 0 and c == 0:
                            self.wait(vec, 'wp_x2_zero')
                        ins = vec.tensor_copy(wx2tmp[:, :], self.psTb[0:32, 0:128])
                        self.tick('V', ins, 1, ev=f'wp_x2cp_{g}_{c}')
                    self.block.vector(evx2)

                    def dmx2(sync, g=g, c=c):
                        self.wait(sync, f'wp_x2cp_{g}_{c}')
                        if g < 4:
                            ins = sync.dma_start(out=self.wx2T[32 * g:32 * g + 32, c, :], in_=wx2tmp[:, :])
                            self.tick('DS', ins, 16, ev=f'wp_x2ev_{g}_{c}')
                        else:
                            for q in range(4):
                                ins = sync.dma_start(out=self.wx2T[32 * q:32 * q + 32, 8 + c, :], in_=wx2tmp[:, :])
                                self.tick('DS', ins, 16)
                            self.mark('DS', f'wp_x2ev_{g}_{c}')
                    self.block.sync(dmx2)
            self.ev['wp_prev'] = self.ev[f'wp_x2ev_4_{ch0 + 3}']

        self.mark('DS', 'wp_x2_done')
        wpst.close()

    def emit_p3pro(self):
        import contextlib
        pst = contextlib.ExitStack()
        TW = 256
        xbf = self.sb("p3pxbf", [128, DC, TW], BF16, pst)
        xsq = self.sb("p3pxsq", [128, DC, TW], BF16, pst)
        mrow = self.sb("p3pmrow", [1, TW], F32, pst)
        rrow = self.sb("p3prrow", [1, TW], F32, pst)
        rowb = [self.sb("p3prowb0", [1, 2, TW], BF16, pst), self.sb("p3prowb1", [1, 2, TW], BF16, pst)]
        srow = self.sb("p3psrow", [1, 2, TW], F32, pst)
        ypcol = self.sb("p3ypcol", [128, 2], F32, pst)
        yprow = self.sb("p3yprow", [1, 2], BF16, pst)

        ntile = len(self.tiles)
        for i, (t0, tl) in enumerate(self.tiles):
            def vc(vec, i=i, t0=t0, tl=tl):
                if i == 0:
                    self.wait(vec, 'p2done_1')
                ins = vec.tensor_copy(xbf[:, :, 0:tl], self.resid[:, :, t0:t0 + tl])
                self.tick('V', ins, 1)
                ins = vec.tensor_tensor(xsq[:, :, 0:tl], xbf[:, :, 0:tl], xbf[:, :, 0:tl], ALU.mult)
                self.tick('V', ins, 1, ev=f'p3pxsq_{i}')
            self.block.vector(vc)

            def stm(t, i=i, tl=tl):
                self.wait(t, f'p3pxsq_{i}')
                if i >= 1:
                    self.wait(t, f'p3pvar_{i-1}')
                ins = None
                for c in range(DC):
                    ins = t.matmul(self.psB[0:1, 0, 0:tl], self.onesb[:, 0:1], xbf[:, c, 0:tl],
                                   start=(c == 0), stop=(c == DC - 1))
                self.tick('T', ins, 1)
                for c in range(DC):
                    ins = t.matmul(self.psB[0:1, 1, 0:tl], self.onesb[:, 0:1], xsq[:, c, 0:tl],
                                   start=(c == 0), stop=(c == DC - 1))
                self.tick('T', ins, 1, ev=f'p3psums_{i}')
            self.block.tensor(stm)

            def sv(vec, i=i, tl=tl):
                self.wait(vec, f'p3psums_{i}')
                if i >= 1:
                    self.wait(vec, f'p3prowb_{i-1}')
                vec.tensor_scalar(mrow[:, 0:tl], self.psB[0:1, 0, 0:tl], self.crow(1.0 / 1024.0), None, ALU.mult)
                vec.tensor_scalar(srow[:, 0, 0:tl], self.psB[0:1, 1, 0:tl], self.crow(1.0 / 1024.0), None, ALU.mult)
                vec.tensor_tensor(srow[:, 1, 0:tl], mrow[:, 0:tl], mrow[:, 0:tl], ALU.mult)
                ins = vec.tensor_tensor(rrow[:, 0:tl], srow[:, 0, 0:tl], srow[:, 1, 0:tl], ALU.subtract)
                self.tick('V', ins, 1, ev=f'p3pvar_{i}')
            self.block.vector(sv)

            def ss(sc, i=i, tl=tl):
                self.wait(sc, f'p3pvar_{i}')
                self.fence(sc, 'S', sc.activation(rrow[:, 0:tl], rrow[:, 0:tl], AF.Ln, bias=EPS, scale=1.0))
                ins = sc.activation(rrow[:, 0:tl], rrow[:, 0:tl], AF.Exp, bias=0.0, scale=-0.5)
                self.tick('S', ins, 1, ev=f'p3prstd_{i}')
            self.block.scalar(ss)

            def sv2(vec, i=i, tl=tl):
                self.wait(vec, f'p3prstd_{i}')
                if i >= 2:
                    self.wait(vec, f'p3pst_{i-2}')
                vec.tensor_copy(rowb[i % 2][:, 0, 0:tl], mrow[:, 0:tl])
                ins = vec.tensor_copy(rowb[i % 2][:, 1, 0:tl], rrow[:, 0:tl])
                self.tick('V', ins, 1, ev=f'p3prowb_{i}')
            self.block.vector(sv2)

            def stst(sync, i=i, t0=t0, tl=tl):
                self.wait(sync, f'p3prowb_{i}')
                self.tick('DS', sync.dma_start(out=self.sc_stat[:, t0:t0 + tl].rearrange("k t -> () k t"),
                                               in_=rowb[i % 2][0:1, :, 0:tl]),
                          16, ev=f'p3pst_{i}')
            self.block.sync(stst)

        def ypld(sync):
            self.wait(sync, 'p3pst_0')
            ctx = self.nc.allow_non_contiguous_dma(reason="2-element stat fetch")
            ctx.__enter__()
            self.tick('DS', sync.dma_start(out=yprow[:, :], in_=self.sc_stat[:, 127:128].rearrange("k t -> t k")), 16, ev='yprow')
            ctx.__exit__(None, None, None)
        self.block.sync(ypld)

        self.mark('T', 'p3a_psC_free_init')

        def ypbc(t):
            self.wait(t, 'yprow')
            self.wait(t, 'selbcv_1')
            self.wait(t, 'p2_s_done')
            ins = t.matmul(self.psC[:, 0:2], self.onesbr[0:1, :], yprow[0:1, :], start=True, stop=True)
            self.tick('T', ins, 1, ev='ypbc')
        self.block.tensor(ypbc)

        def ypv(vec):
            self.wait(vec, 'ypbc')
            self.fence(vec, 'V', vec.tensor_copy(ypcol[:, :], self.psC[:, 0:2]))
            self.fence(vec, 'V', vec.tensor_scalar(self.yprevA[:, :, 0], self.resid[:, :, 127], ypcol[:, 0:1], None, ALU.subtract))
            self.fence(vec, 'V', vec.tensor_scalar(self.yprevA[:, :, 0], self.yprevA[:, :, 0], ypcol[:, 1:2], None, ALU.mult))
            self.fence(vec, 'V', vec.tensor_scalar(self.yprevA[:, :, 0], self.yprevA[:, :, 0], self.bmk[:, 0:1], None, ALU.mult))
            self.fence(vec, 'V', vec.tensor_copy(self.yprevB[:, :, :], self.yprevA[:, :, :]))
            vec.memset(self.Sst[:, :, :], 0.0)
            ins = vec.memset(self.Ccum[:, :, 0:1], 0.0)
            self.tick('V', ins, 1, ev='p3init')
        self.block.vector(ypv)

        pst.close()

    def emit_p3a(self):
        st = self.p3a_st
        NCH = self.NCH
        ybf = self.sb("aybf", [128, DC, CH], BF16, st)
        delta = self.sb("adelta", [128, DC, CH], BF16, st)
        xzb = self.sb("axz", [128, DC, CH], BF16, st)
        musb = self.sb("amus", [128, DC, CH], BF16, st)
        xib = self.sb("axi", [128, DC, CH], BF16, st)
        t1b = xib
        zbt = self.sb("azbt", [32, CH], BF16, st)
        twa = self.sb("atwa", [64, CH], BF16, st)
        wtl = self.sb("awtl", [128, HH, CH], F32, st)
        cP = self.sb("acP", [128, HH, CH], F32, st)
        kfm = self.sb("akfm", [128, HH, CH], BF16, st)
        khtm = self.sb("akhtm", [128, HH, CH], BF16, st)
        khat = xzb.tensor.ap().rearrange("p (c t) -> p c t", c=DC)[:, 0:HH, :] if False else None
        vtm = self.sb("avtm", [128, D], BF16, st)
        vbf = musb
        khat = self.aview(xzb, [128, HH, CH])
        dec = self.sb("adec", [128, HH], F32, st)
        mBt = self.sb("amB", [128, CH], BF16, st)
        rBt = self.sb("arB", [128, CH], BF16, st)
        strow = [self.sb("astr0", [1, 2, CH], BF16, st), self.sb("astr1", [1, 2, CH], BF16, st)]

        zq = (self.NCH // 4) * CH
        self.mark('DA', 'a_store_-1')

        for j in range(1, NCH + 1):
            jj = j - 1
            u0 = j * CH
            buf = jj % 2

            def ldst(sync, jj=jj, u0=u0, buf=buf):
                if jj >= 2:
                    self.wait(sync, f'a_bc_{jj-2}')
                self.tick('DS', sync.dma_start(out=strow[buf][0:1, :, :],
                                               in_=self.sc_stat[:, u0:u0 + CH].rearrange("k t -> () k t")),
                          16, ev=f'a_strow_{jj}')
            self.block.sync(ldst)

            def bc(t, jj=jj, buf=buf):
                self.wait(t, f'a_strow_{jj}')
                if jj == 0:
                    for e in ['p3init', 'wp_x2_done', 'wp_done_k', 'wp_done_v',
                              'wp_done_x1', 'wp_done_aw', 'wp_done_bw', 'wp_gn',
                              'p2_s_done']:
                        self.wait(t, e)
                else:
                    self.wait(t, f'a_bcev_{jj-1}')
                ins = t.matmul(self.psC[:, 0:CH], self.onesbr[0:1, :], strow[buf][0:1, 0, :], start=True, stop=True)
                self.tick('T', ins, 1)
                ins = t.matmul(self.psC[:, CH:2 * CH], self.onesbr[0:1, :], strow[buf][0:1, 1, :], start=True, stop=True)
                self.tick('T', ins, 1, ev=f'a_bc_{jj}')
            self.block.tensor(bc)

            def ybuild(vec, jj=jj, u0=u0):
                self.wait(vec, f'a_bc_{jj}')
                if jj >= 1:
                    self.wait(vec, f'a_xi_{jj-1}_2')
                    self.wait(vec, f'a_ktp_{jj-1}')
                vec.tensor_copy(mBt[:, :], self.psC[:, 0:CH])
                ins = vec.tensor_copy(rBt[:, :], self.psC[:, CH:2 * CH])
                self.tick('V', ins, 1, ev=f'a_bcev_{jj}')
                vec.scalar_tensor_tensor(t1b[:, :, :], self.resid[:, :, u0:u0 + CH], 1.0,
                                         mBt[:, :].unsqueeze(1).broadcast_to((128, DC, CH)), ALU.mult, ALU.subtract)
                ins = vec.tensor_tensor(ybf[:, :, :], t1b[:, :, :],
                                        rBt[:, :].unsqueeze(1).broadcast_to((128, DC, CH)), ALU.mult)
                self.tick('V', ins, 1, ev=f'a_y_{jj}')
                vec.tensor_tensor(delta[:, :, 1:CH], ybf[:, :, 0:CH - 1], ybf[:, :, 1:CH], ALU.subtract)
                self.fence(vec, 'V', vec.tensor_tensor(delta[:, :, 0:1], self.yprevA[:, :, :], ybf[:, :, 0:1], ALU.subtract))
                self.fence(vec, 'V', vec.tensor_copy(self.yprevA[:, :, :], ybf[:, :, CH - 1:CH]))
                ins = None
                for c in range(DC):
                    ins = vec.scalar_tensor_tensor(xzb[:, c, :], delta[:, c, :], self.muxc[:, c:c + 1],
                                                   ybf[:, c, :], ALU.mult, ALU.add)
                self.tick('V', ins, 1, ev=f'a_xz_{jj}')
            self.block.vector(ybuild)

            def zmm(t, jj=jj):
                self.wait(t, f'a_xz_{jj}')
                if jj >= 1:
                    self.wait(t, f'a_wexp_{jj-1}')
                ins = None
                for c in range(DC):
                    ins = t.matmul(self.psA[:, 0, 0:CH], self.wx1T[:, c, 0:128], xzb[:, c, :],
                                   start=(c == 0), stop=(c == DC - 1))
                self.tick('T', ins, 1)
                for c in range(DC):
                    ins = t.matmul(self.psA[0:32, 0, CH:2 * CH], self.wx1T[:, c, 128:160], xzb[:, c, :],
                                   start=(c == 0), stop=(c == DC - 1))
                self.tick('T', ins, 1, ev=f'a_zmm_{jj}')
            self.block.tensor(zmm)

            def ztanh(sc, jj=jj, u0=u0):
                self.wait(sc, f'a_zmm_{jj}')
                if jj >= 1:
                    self.wait(sc, f'a_zB2_{jj-1}')
                ins = sc.activation(self.zlA[:, u0:u0 + CH], self.psA[:, 0, 0:CH], AF.Tanh, bias=0.0, scale=1.0)
                self.tick('S', ins, 1)
                ins = sc.activation(zbt[:, :], self.psA[0:32, 0, CH:2 * CH], AF.Tanh, bias=0.0, scale=1.0)
                self.tick('S', ins, 1, ev=f'a_zB_{jj}')
            self.block.scalar(ztanh)

            def zbd(sync, jj=jj, zq=zq):
                self.wait(sync, f'a_zB_{jj}')
                q = jj // (self.NCH // 4)
                col = (jj % (self.NCH // 4)) * CH
                self.tick('DS', sync.dma_start(out=self.zlB[32 * q:32 * q + 32, col:col + CH],
                                               in_=zbt[:, :]), 16, ev=f'a_zB2_{jj}')
            self.block.sync(zbd)

            for bi, br in enumerate(['w', 'k', 'v']):
                g = GRP[br]

                def musmm(t, jj=jj, g=g, bi=bi, u0=u0, zq=zq):
                    if bi == 0:
                        self.wait(t, f'a_zB_{jj}')
                    prev = (jj, bi - 1) if bi >= 1 else (jj - 1, 2)
                    if prev[0] >= 0:
                        self.wait(t, 'a_musev_{}_{}'.format(*prev))
                    ins = None
                    for c in range(DC):
                        if g < 4:
                            band = 32 * (g % 4)
                            kw = {'tile_position': (band, 0)} if band == 96 else {}
                            ins = t.matmul(self.psA[:, 1 + c // 4, sl(c % 4, CH)],
                                           self.wx2T[band:band + 32, c, :],
                                           self.zlA[32 * g:32 * g + 32, u0:u0 + CH],
                                           start=True, stop=True, **kw)
                        else:
                            q = jj // (self.NCH // 4)
                            col = (jj % (self.NCH // 4)) * CH
                            band = 32 * q
                            kw = {'tile_position': (band, 0)} if band == 96 else {}
                            ins = t.matmul(self.psA[:, 1 + c // 4, sl(c % 4, CH)],
                                           self.wx2T[band:band + 32, 8 + c, :],
                                           self.zlB[band:band + 32, col:col + CH],
                                           start=True, stop=True, **kw)
                    self.tick('T', ins, 1, ev=f'a_mus_{jj}_{bi}')
                self.block.tensor(musmm)

                def xiv(vec, jj=jj, g=g, bi=bi):
                    self.wait(vec, f'a_mus_{jj}_{bi}')
                    if bi == 0 and jj >= 1:
                        self.wait(vec, f'a_vtp_{jj-1}')
                    musps = self.psA[:, 1:3, :].rearrange("p b (m t) -> p (b m) t", m=4)
                    ins = vec.tensor_tensor(musb[:, :, :], musps,
                                            self.xbias5[:, g, :].unsqueeze(2).broadcast_to((128, DC, CH)), ALU.add)
                    self.tick('V', ins, 1, ev=f'a_musev_{jj}_{bi}')
                    vec.tensor_tensor(xib[:, :, :], delta[:, :, :], musb[:, :, :], ALU.mult)
                    ins = vec.tensor_tensor(xib[:, :, :], xib[:, :, :], ybf[:, :, :], ALU.add)
                    self.tick('V', ins, 1, ev=f'a_xi_{jj}_{bi}')
                self.block.vector(xiv)

                def projmm(t, jj=jj, br=br, bi=bi):
                    self.wait(t, f'a_xi_{jj}_{bi}')
                    ins = None
                    if br == 'k':
                        if jj >= 1:
                            self.wait(t, f'a_kev_{jj-1}')
                        for mc in range(4):
                            for c in range(DC):
                                ins = t.matmul(self.psA[:, 3, sl(mc, CH)], self.wkT[:, c, sl(mc, CH)],
                                               xib[:, c, :], start=(c == 0), stop=(c == DC - 1))
                        self.tick('T', ins, 1, ev=f'a_kmm_{jj}')
                    elif br == 'v':
                        if jj >= 1:
                            self.wait(t, f'a_sadd_{jj-1}')
                        for h in range(2):
                            for mc in range(4):
                                for c in range(DC):
                                    ins = t.matmul(self.psB[:, h, sl(mc, CH)], self.wvT[:, c, sl(4 * h + mc, CH)],
                                                   xib[:, c, :], start=(c == 0), stop=(c == DC - 1))
                        self.tick('T', ins, 1, ev=f'a_vmm_{jj}')
                    else:
                        if jj >= 1:
                            self.wait(t, f'a_twaev_{jj-1}')
                        for c in range(DC):
                            ins = t.matmul(self.psC[0:64, 2 * CH:3 * CH], self.awT[:, c, :],
                                           xib[:, c, :], start=(c == 0), stop=(c == DC - 1))
                        self.tick('T', ins, 1, ev=f'a_awmm_{jj}')
                self.block.tensor(projmm)

                if br == 'w':
                    def twas(sc, jj=jj):
                        self.wait(sc, f'a_awmm_{jj}')
                        ins = sc.activation(twa[:, :], self.psC[0:64, 2 * CH:3 * CH], AF.Tanh, bias=0.0, scale=1.0)
                        self.tick('S', ins, 1, ev=f'a_twa_{jj}')
                        self.mark('S', f'a_twaev_{jj}')
                    self.block.scalar(twas)

                    def wbmm(t, jj=jj):
                        self.wait(t, f'a_twa_{jj}')
                        self.wait(t, f'a_zB_{jj}')
                        ins = None
                        for mc in range(4):
                            ins = t.matmul(self.psA[:, 0, sl(mc, CH)], self.bwT[0:64, mc, :],
                                           twa[0:64, :], start=True, stop=True)
                        self.tick('T', ins, 1, ev=f'a_wbmm_{jj}')
                    self.block.tensor(wbmm)

                    def wexp(sc, jj=jj):
                        self.wait(sc, f'a_wbmm_{jj}')
                        ins = None
                        for mc in range(4):
                            ins = sc.activation(wtl[:, mc, :], self.psA[:, 0, sl(mc, CH)], AF.Exp,
                                                bias=self.bwc[:, mc:mc + 1], scale=1.0)
                        self.tick('S', ins, 1, ev=f'a_wexp_{jj}')
                    self.block.scalar(wexp)

                    def cscan(vec, jj=jj):
                        self.wait(vec, f'a_wexp_{jj}')
                        if jj >= 1:
                            self.wait(vec, f'a_store_{jj-1}')
                        ins = None
                        for mc in range(HH):
                            ins = vec.tensor_tensor_scan(cP[:, mc, :], wtl[:, mc, :], self.zro[:, :],
                                                         0.0, ALU.add, ALU.add)
                        self.tick('V', ins, 1, ev=f'a_cP_{jj}')
                        for mc in range(HH):
                            ins = vec.tensor_scalar(wtl[:, mc, :], cP[:, mc, :], cP[:, mc, CH - 1:CH], None, ALU.subtract)
                        self.tick('V', ins, 1, ev=f'a_ek_{jj}')
                    self.block.vector(cscan)

                    def ekxs(sc, jj=jj):
                        self.wait(sc, f'a_ek_{jj}')
                        ins = sc.activation(wtl[:, :, :], wtl[:, :, :], AF.Exp, bias=0.0, scale=1.0)
                        self.tick('S', ins, 1)
                        ins = sc.activation(dec[:, :], cP[:, :, CH - 1], AF.Exp, bias=0.0, scale=-1.0)
                        self.tick('S', ins, 1, ev=f'a_dec_{jj}')
                    self.block.scalar(ekxs)

            def kev(vec, jj=jj):
                self.wait(vec, f'a_kmm_{jj}')
                self.wait(vec, f'a_dec_{jj}')
                ins = vec.tensor_copy(kfm[:, :, :], self.psA[:, 3, :].rearrange("p (m t) -> p m t", m=HH))
                self.tick('V', ins, 1, ev=f'a_kev_{jj}')
                ins = vec.tensor_tensor(khat[:, :, :], kfm[:, :, :], wtl[:, :, :], ALU.mult)
                self.tick('V', ins, 1, ev=f'a_khat_{jj}')
            self.block.vector(kev)

            def vev(vec, jj=jj):
                self.wait(vec, f'a_vmm_{jj}')
                vec.tensor_copy(vbf[:, 0:4, :], self.psB[:, 0, :].rearrange("p (m t) -> p m t", m=4))
                ins = vec.tensor_copy(vbf[:, 4:8, :], self.psB[:, 1, :].rearrange("p (m t) -> p m t", m=4))
                self.tick('V', ins, 1, ev=f'a_vev_{jj}')
            self.block.vector(vev)

            def tps(t, jj=jj):
                self.wait(t, f'a_vev_{jj}')
                if jj >= 1:
                    self.wait(t, f'a_khtmev_{jj-1}')
                ins = None
                for c in range(DC):
                    ins = t.transpose(self.psTb[:, sl(c, CH)], vbf[:, c, :], self.identb[:, :])
                self.tick('T', ins, 1, ev=f'a_vtp_{jj}')
            self.block.tensor(tps)

            def tpev(vec, jj=jj):
                self.wait(vec, f'a_vtp_{jj}')
                ins = vec.tensor_copy(vtm[:, :], self.psTb[:, :])
                self.tick('V', ins, 1, ev=f'a_vtmev_{jj}')
            self.block.vector(tpev)

            def tps2(t, jj=jj):
                self.wait(t, f'a_khat_{jj}')
                self.wait(t, f'a_vtmev_{jj}')
                ins = None
                for mc in range(HH):
                    ins = t.transpose(self.psTb[:, sl(mc, CH)], khat[:, mc, :], self.identb[:, :])
                self.tick('T', ins, 1, ev=f'a_ktp_{jj}')
            self.block.tensor(tps2)

            def tpev2(vec, jj=jj):
                self.wait(vec, f'a_ktp_{jj}')
                ins = vec.tensor_copy(khtm[:, :, :], self.psTb[:, 0:512].rearrange("p (m t) -> p m t", m=HH))
                self.tick('V', ins, 1, ev=f'a_khtmev_{jj}')
            self.block.vector(tpev2)

            def panel(gp, jj=jj):
                if jj == 0:
                    self.wait(gp, 'p3init')
                else:
                    self.wait(gp, f'a_sadd_{jj-1}')
                ins = gp.dma_start(out=self.sc_sp[jj, :, :, :], in_=self.Sst[:, :, :])
                self.tick('DG', ins, 16, ev=f'a_spb_{jj}')
            self.block.gpsimd(panel)

            def smm(t, jj=jj):
                self.wait(t, f'a_khtmev_{jj}')
                self.wait(t, f'a_vev_{jj}')
                ins = None
                for h in range(HH):
                    ins = t.matmul(self.psB[:, h // 2, sl(h % 2, VH)], khtm[:, h, :],
                                   vtm[:, h * VH:(h + 1) * VH], start=True, stop=True)
                self.tick('T', ins, 1, ev=f'a_smm_{jj}')
            self.block.tensor(smm)

            def supd(vec, jj=jj):
                self.wait(vec, f'a_smm_{jj}')
                self.wait(vec, f'a_spb_{jj}')
                ins = None
                for h in range(HH):
                    ins = vec.tensor_scalar(self.Sst[:, h, :], self.Sst[:, h, :], dec[:, h:h + 1], None, ALU.mult)
                self.tick('V', ins, 1)
                ins = vec.tensor_tensor(self.Sst[:, :, :], self.Sst[:, :, :],
                                        self.psB[:, :, :].rearrange("p b (h v) -> p (b h) v", h=2), ALU.add)
                self.tick('V', ins, 1, ev=f'a_sadd_{jj}')
                ins = vec.tensor_tensor(self.Ccum[:, :, jj + 1:jj + 2], self.Ccum[:, :, jj:jj + 1],
                                        cP[:, :, CH - 1:CH], ALU.add)
                self.tick('V', ins, 1, ev=f'a_ccum_{jj}')
            self.block.vector(supd)

            def sts(sc, jj=jj):
                self.wait(sc, f'a_kev_{jj}')
                self.tick('DA', sc.dma_start(out=self.sc_k[jj, :, :, :], in_=kfm[:, :, :]), 16)
                self.wait(sc, f'a_vtmev_{jj}')
                self.tick('DA', sc.dma_start(out=self.sc_vt[jj, :, :], in_=vtm[:, :]), 16)
                self.wait(sc, f'a_cP_{jj}')
                self.tick('DA', sc.dma_start(out=self.sc_c[jj, :, :, :], in_=cP[:, :, :]), 16, ev=f'a_store_{jj}')
            self.block.scalar(sts)

        def xch(gp):
            self.wait(gp, f'a_sadd_{NCH-1}')
            self.tick('DG', gp.dma_start(out=self.ccS_in[:, :], in_=self.Sst[:, :, :].rearrange("p h v -> p (h v)")), 16)
            gp.wait_ge(self.sems['DG'], self.cnt['DG'])
            ins = gp.collective_compute("AllGather", ALU.bypass, replica_groups=self.RG,
                                        ins=[self.ccS_in[:, :]], outs=[self.ccS_out[:, :, :]])
            self.tick('CC', ins, 1, ev='cc_S')
        self.block.gpsimd(xch)

        def sinl(gp):
            self.wait(gp, 'cc_S')
            self.tick('DG', gp.dma_start(out=self.sinb[:, :, :],
                                         in_=self.ccS_out[0, :, :].rearrange("p (h v) -> p h v", h=HH)),
                      16, ev='sin_raw')
        self.block.gpsimd(sinl)

        def sinv(vec):
            self.wait(vec, 'sin_raw')
            ins = vec.tensor_scalar(self.sinb[:, :, :], self.sinb[:, :, :], self.bmk[:, 0:1], None, ALU.mult)
            self.tick('V', ins, 1, ev='sin_ready')
        self.block.vector(sinv)

        self.dbg_dump('Sloc', [128, HH, VH], F32, self.Sst[:, :, :], [f'a_sadd_{NCH-1}'])
        self.dbg_dump('sck', [self.NCH, 128, HH, CH], BF16, self.sc_k[:, :, :, :], [f'a_store_{NCH-1}'])
        self.dbg_dump('scvt', [self.NCH, CH, D], BF16, self.sc_vt[:, :, :], [f'a_store_{NCH-1}'])
        self.dbg_dump('scc', [self.NCH, 128, HH, CH], F32, self.sc_c[:, :, :, :], [f'a_store_{NCH-1}'])
        self.dbg_dump('scsp', [self.NCH, 128, HH, VH], BF16, self.sc_sp[:, :, :, :], [f'a_store_{NCH-1}'])
        self.dbg_dump('sinb', [128, HH, VH], BF16, self.sinb[:, :, :], ['sin_ready'])
        self.dbg_dump('zlA', [128, self.TP], BF16, self.zlA[:, :], [f'a_zB2_{NCH-1}', f'a_zB_{NCH-1}'])

    def emit_p3b(self):
        import contextlib
        self.close_stack('p3a_st')
        self.p3b_st = contextlib.ExitStack()
        st = self.p3b_st
        NCH = self.NCH
        self.wrT = self.sb("wrT", [128, DC, KD], BF16, st)
        self.wgT = self.sb("wgT", [128, DC, D], BF16, st)
        kfm2 = self.sb("bkfm", [128, HH, CH], BF16, st)
        vtm2 = self.sb("bvtm", [128, D], BF16, st)
        sp2 = self.sb("bsp", [128, HH, VH], BF16, st)
        cPl = self.sb("bcP", [128, HH, CH], F32, st)
        ybf = self.sb("bybf", [128, DC, CH], BF16, st)
        delta = self.sb("bdelta", [128, DC, CH], BF16, st)
        xib = self.sb("bxi", [128, DC, CH], BF16, st)
        musb = self.sb("bmus", [128, DC, CH], BF16, st)
        rbf = self.sb("brbf", [128, HH, CH], BF16, st)
        Psh = self.sb("bPsh", [128, HH, CH], F32, st)
        el = self.sb("bel", [128, HH, CH], F32, st)
        eq = self.sb("beq", [128, HH, CH], F32, st)
        elb = el[:, :, :].rearrange("p a b -> p (a b)").bitcast(BF16)
        eqb = eq[:, :, :].rearrange("p a b -> p (a b)").bitcast(BF16)
        rtil = self.aview(delta, [128, HH, CH])
        rloc = self.sb("brloc", [128, HH, CH], BF16, st)
        rhat = self.sb("brhat", [128, HH, CH], BF16, st)
        kloc = self.sb("bkloc", [128, HH, CH], BF16, st)
        Rt = self.sb("bRt", [128, HH, 6, SUB], BF16, st)
        epair = self.sb("bepair", [128, HH, 6], F32, st)
        expCc = self.sb("bexpCc", [128, HH], F32, st)
        Db = eqb[:, 0:HH * CH].rearrange("p (a b) -> p a b", a=HH)
        Am = Psh[:, :, :].rearrange("p a b -> p (a b)").bitcast(BF16)[:, 0:HH * CH].rearrange(
            "p (a b) -> p a b", a=HH)
        drow = Am[0:1, :, :]
        fb = Am
        tgh = self.aview(xib, [128, DC, CH])
        gb = self.aview(ybf, [128, DC, CH])
        obf = self.aview(musb, [128, DC, CH])
        osq = elb.rearrange("p (a b) -> p a b", a=DC)
        gnrowA = eq[0:1, :, :]
        gnrowB = Psh[0:1, :, :]
        strow0 = self.sb("bstr0", [1, 2, CH], BF16, st)
        strow = [strow0, strow0]
        mB2 = self.sb("bmB2", [128, CH], BF16, st)
        rB2 = self.sb("brB2", [128, CH], BF16, st)
        wbst_stack = contextlib.ExitStack()
        wstb = self.sb("wstb", [128, 1, 1024], BF16, wbst_stack)

        self.ev['b_prev'] = self.ev['sin_ready']
        def load_T(tag, hbm, R, out):
            rch = R // 128
            for g0 in range(0, rch, 1):
                gn = 1
                def ld(gp, hbm=hbm, g0=g0, gn=gn, tag=tag):
                    self.wait(gp, 'b_prev')
                    if tag == 'r' and g0 == 0:
                        for e in [f'a_store_{self.NCH-1}', f'a_zB_{self.NCH-1}', f'a_sadd_{self.NCH-1}']:
                            self.wait(gp, e)
                    for k in range(gn):
                        rc = g0 + k
                        self.tick('DG', gp.dma_start(out=wstb[:, k, 0:D], in_=hbm[rc * 128:(rc + 1) * 128, :]), 16)
                    self.mark('DG', f'b_ld_{tag}_{g0}')
                self.block.gpsimd(ld)
                for k in range(gn):
                    rc = g0 + k
                    def tp(te, k=k, rc=rc, tag=tag, g0=g0):
                        self.wait(te, f'b_ld_{tag}_{g0}')
                        if rc == 0:
                            self.wait(te, 'b_prev')
                        else:
                            self.wait(te, f'b_ev_{tag}_{rc-1}')
                        ins = None
                        for c in range(DC):
                            ins = te.transpose(self.psTb[:, sl(c, 128)], wstb[:, k, sl(c, 128)], self.identb[:, :])
                        self.tick('T', ins, 1, ev=f'b_tp_{tag}_{rc}')
                    self.block.tensor(tp)
                    def ev(vec, rc=rc, tag=tag, out=out):
                        self.wait(vec, f'b_tp_{tag}_{rc}')
                        ins = vec.tensor_copy(out[:, :, sl(rc, 128)],
                                              self.psTb[:, :].rearrange("p (c r) -> p c r", c=DC))
                        self.tick('V', ins, 1, ev=f'b_ev_{tag}_{rc}')
                    self.block.vector(ev)
                self.ev['b_prev'] = self.ev[f'b_ev_{tag}_{g0 + gn - 1}']
        load_T('r', self.W['attn_W_r'], KD, self.wrT)
        load_T('g', self.W['attn_W_g'], D, self.wgT)
        self.ev['b_wdone'] = self.ev['b_prev']
        wbst_stack.close()
        self.mark('DA', 'b_store_-1')
        self.mark('DG', 'b_sgst_-1')

        for jj in range(NCH):
            u0 = (jj + 1) * CH
            buf = jj % 2

            def lds(sync, jj=jj, u0=u0, buf=buf):
                if jj == 0:
                    for e in [f'a_store_{self.NCH-1}', 'sin_ready', f'a_zB_{self.NCH-1}',
                              f'a_sadd_{self.NCH-1}', f'a_spb_{self.NCH-1}']:
                        self.wait(sync, e)
                else:
                    self.wait(sync, f'b_panelfree_{jj-1}')
                    self.wait(sync, f'b_bc_{jj-1}')
                self.tick('DS', sync.dma_start(out=strow[buf][0:1, :, :],
                                               in_=self.sc_stat[:, u0:u0 + CH].rearrange("k t -> () k t")), 16)
                self.tick('DS', sync.dma_start(out=kfm2[:, :, :], in_=self.sc_k[jj, :, :, :]), 16)
                self.tick('DS', sync.dma_start(out=vtm2[:, :], in_=self.sc_vt[jj, :, :]), 16)
                self.tick('DS', sync.dma_start(out=sp2[:, :, :], in_=self.sc_sp[jj, :, :, :]), 16)
                self.tick('DS', sync.dma_start(out=cPl[:, :, :], in_=self.sc_c[jj, :, :, :]), 16, ev=f'b_ld_{jj}')
            self.block.sync(lds)

            def bc(t, jj=jj, buf=buf):
                self.wait(t, f'b_ld_{jj}')
                if jj == 0:
                    self.wait(t, 'b_wdone')
                    self.wait(t, 'sin_ready')
                else:
                    self.wait(t, f'b_bcev_{jj-1}')
                ins = t.matmul(self.psC[:, 0:CH], self.onesbr[0:1, :], strow[buf][0:1, 0, :], start=True, stop=True)
                self.tick('T', ins, 1)
                ins = t.matmul(self.psC[:, CH:2 * CH], self.onesbr[0:1, :], strow[buf][0:1, 1, :], start=True, stop=True)
                self.tick('T', ins, 1, ev=f'b_bc_{jj}')
            self.block.tensor(bc)

            def ybuild(vec, jj=jj, u0=u0):
                self.wait(vec, f'b_bc_{jj}')
                if jj >= 1:
                    self.wait(vec, f'b_omm_{jj-1}')
                    self.wait(vec, f'b_sgst_{jj-1}')
                vec.tensor_copy(mB2[:, :], self.psC[:, 0:CH])
                ins = vec.tensor_copy(rB2[:, :], self.psC[:, CH:2 * CH])
                self.tick('V', ins, 1, ev=f'b_bcev_{jj}')
                vec.scalar_tensor_tensor(xib[:, :, :], self.resid[:, :, u0:u0 + CH], 1.0,
                                         mB2[:, :].unsqueeze(1).broadcast_to((128, DC, CH)), ALU.mult, ALU.subtract)
                ins = vec.tensor_tensor(ybf[:, :, :], xib[:, :, :],
                                        rB2[:, :].unsqueeze(1).broadcast_to((128, DC, CH)), ALU.mult)
                self.tick('V', ins, 1, ev=f'b_y_{jj}')
                vec.tensor_tensor(delta[:, :, 1:CH], ybf[:, :, 0:CH - 1], ybf[:, :, 1:CH], ALU.subtract)
                self.fence(vec, 'V', vec.tensor_tensor(delta[:, :, 0:1], self.yprevB[:, :, :], ybf[:, :, 0:1], ALU.subtract))
                self.fence(vec, 'V', vec.tensor_copy(self.yprevB[:, :, :], ybf[:, :, CH - 1:CH]))
                self.mark('V', f'b_delta_{jj}')
            self.block.vector(ybuild)

            for bi, br in enumerate(['r', 'g']):
                g = GRP[br]
                def musmm(t, jj=jj, g=g, bi=bi, u0=u0):
                    if bi == 0:
                        self.wait(t, f'b_delta_{jj}')
                        if jj >= 1:
                            self.wait(t, f'b_musev_{jj-1}_1')
                    else:
                        self.wait(t, f'b_musev_{jj}_0')
                    ins = None
                    for c in range(DC):
                        if g < 4:
                            band = 0
                            ins = t.matmul(self.psA[:, 1 + c // 4, sl(c % 4, CH)],
                                           self.wx2T[band:band + 32, c, :],
                                           self.zlA[32 * g:32 * g + 32, u0:u0 + CH],
                                           start=True, stop=True)
                        else:
                            q = jj // (self.NCH // 4)
                            col = (jj % (self.NCH // 4)) * CH
                            band = 32 * q
                            kw = {'tile_position': (band, 0)} if band == 96 else {}
                            ins = t.matmul(self.psA[:, 1 + c // 4, sl(c % 4, CH)],
                                           self.wx2T[band:band + 32, 8 + c, :],
                                           self.zlB[band:band + 32, col:col + CH],
                                           start=True, stop=True, **kw)
                    self.tick('T', ins, 1, ev=f'b_mus_{jj}_{bi}')
                self.block.tensor(musmm)

                def xiv(vec, jj=jj, g=g, bi=bi):
                    self.wait(vec, f'b_mus_{jj}_{bi}')
                    musps = self.psA[:, 1:3, :].rearrange("p b (m t) -> p (b m) t", m=4)
                    ins = vec.tensor_tensor(musb[:, :, :], musps,
                                            self.xbias5[:, g, :].unsqueeze(2).broadcast_to((128, DC, CH)), ALU.add)
                    self.tick('V', ins, 1, ev=f'b_musev_{jj}_{bi}')
                    vec.tensor_tensor(xib[:, :, :], delta[:, :, :], musb[:, :, :], ALU.mult)
                    ins = vec.tensor_tensor(xib[:, :, :], xib[:, :, :], ybf[:, :, :], ALU.add)
                    self.tick('V', ins, 1, ev=f'b_xi_{jj}_{bi}')
                self.block.vector(xiv)

                def projmm(t, jj=jj, br=br, bi=bi):
                    self.wait(t, f'b_xi_{jj}_{bi}')
                    ins = None
                    if br == 'r':
                        if jj >= 1:
                            self.wait(t, f'b_rev_{jj-1}')
                        for mc in range(4):
                            for c in range(DC):
                                ins = t.matmul(self.psA[:, 3, sl(mc, CH)], self.wrT[:, c, sl(mc, CH)],
                                               xib[:, c, :], start=(c == 0), stop=(c == DC - 1))
                        self.tick('T', ins, 1, ev=f'b_rmm_{jj}')
                    else:
                        if jj >= 1:
                            self.wait(t, f'b_oev_{jj-1}')
                        for h in range(2):
                            for mc in range(4):
                                for c in range(DC):
                                    ins = t.matmul(self.psB[:, h, sl(mc, CH)], self.wgT[:, c, sl(4 * h + mc, CH)],
                                                   xib[:, c, :], start=(c == 0), stop=(c == DC - 1))
                        self.tick('T', ins, 1, ev=f'b_gmm_{jj}')
                self.block.tensor(projmm)

            def silu(sc, jj=jj):
                self.wait(sc, f'b_gmm_{jj}')
                ins = sc.activation(tgh[:, 0:4, :].rearrange("p c t -> p (c t)"), self.psB[:, 0, :], AF.Tanh, bias=0.0, scale=0.5)
                self.tick('S', ins, 1)
                ins = sc.activation(tgh[:, 4:8, :].rearrange("p c t -> p (c t)"), self.psB[:, 1, :], AF.Tanh, bias=0.0, scale=0.5)
                self.tick('S', ins, 1, ev=f'b_tgh_{jj}')
            self.block.scalar(silu)

            def siluv(vec, jj=jj):
                self.wait(vec, f'b_tgh_{jj}')
                vec.tensor_copy(gb[:, 0:4, :].rearrange("p c t -> p (c t)"), self.psB[:, 0, :])
                ins = vec.tensor_copy(gb[:, 4:8, :].rearrange("p c t -> p (c t)"), self.psB[:, 1, :])
                self.tick('V', ins, 1, ev=f'b_gev_{jj}')
                vec.tensor_tensor(osq[:, :, :], gb[:, :, :], tgh[:, :, :], ALU.mult)
                ins = vec.tensor_tensor(gb[:, :, :], gb[:, :, :], osq[:, :, :], ALU.add)
                self.tick('V', ins, 1, ev=f'b_sg_{jj}')
            self.block.vector(siluv)

            def sgst(gp, jj=jj):
                self.wait(gp, f'b_sg_{jj}')
                self.tick('DG', gp.dma_start(out=self.sc_sg[jj, :, :, :], in_=gb[:, :, :]), 16, ev=f'b_sgst_{jj}')
            self.block.gpsimd(sgst)

            def rev(vec, jj=jj):
                self.wait(vec, f'b_rmm_{jj}')
                ins = vec.tensor_copy(rbf[:, :, :], self.psA[:, 3, :].rearrange("p (m t) -> p m t", m=HH))
                self.tick('V', ins, 1, ev=f'b_rev_{jj}')
            self.block.vector(rev)

            def dk1(vec, jj=jj):
                self.wait(vec, f'b_ld_{jj}')
                self.wait(vec, f'b_rev_{jj}')
                self.wait(vec, f'b_sg_{jj}')
                if jj >= 1:
                    self.wait(vec, f'b_store_{jj-1}')
                vec.memset(Psh[:, :, 0:1], 0.0)
                ins = vec.tensor_copy(Psh[:, :, 1:CH], cPl[:, :, 0:CH - 1])
                self.tick('V', ins, 1, ev=f'b_psh_{jj}')
                ins = None
                for mc in range(HH):
                    for sb in range(4):
                        ins = vec.tensor_scalar(el[:, mc, sl(sb, SUB)], cPl[:, mc, sl(sb, SUB)],
                                                Psh[:, mc, 32 * sb:32 * sb + 1], None, ALU.subtract)
                self.tick('V', ins, 1)
                ins = vec.tensor_scalar(el[:, :, :], el[:, :, :], self.ccol(CLAMP), None, ALU.min)
                self.tick('V', ins, 1, ev=f'b_el_{jj}')
                for mc in range(HH):
                    for qb in range(4):
                        ins = vec.tensor_scalar(eq[:, mc, sl(qb, SUB)], Psh[:, mc, sl(qb, SUB)],
                                                Psh[:, mc, 32 * qb:32 * qb + 1], None, ALU.subtract)
                self.tick('V', ins, 1)
                ins = None
                pi = 0
                for sb in range(4):
                    for qb in range(sb + 1, 4):
                        for mc in range(HH):
                            ins = vec.tensor_scalar(epair[:, mc, pi:pi + 1], Psh[:, mc, 32 * sb:32 * sb + 1],
                                                    Psh[:, mc, 32 * qb:32 * qb + 1], None, ALU.subtract)
                        pi += 1
                self.tick('V', ins, 1, ev=f'b_eq_{jj}')
            self.block.vector(dk1)

            def dk2(sc, jj=jj):
                self.wait(sc, f'b_eq_{jj}')
                sc.activation(el[:, :, :], el[:, :, :], AF.Exp, bias=0.0, scale=1.0)
                sc.activation(eq[:, :, :], eq[:, :, :], AF.Exp, bias=0.0, scale=-1.0)
                sc.activation(epair[:, :, :], epair[:, :, :], AF.Exp, bias=0.0, scale=1.0)
                sc.activation(expCc[:, :], self.Ccum[:, :, jj], AF.Exp, bias=0.0, scale=-1.0)
                ins = sc.activation(Psh[:, :, :], Psh[:, :, :], AF.Exp, bias=0.0, scale=-1.0)
                self.tick('S', ins, 1, ev=f'b_dk2_{jj}')
            self.block.scalar(dk2)

            def dk3(vec, jj=jj):
                self.wait(vec, f'b_dk2_{jj}')
                vec.tensor_tensor(rtil[:, :, :], rbf[:, :, :], Psh[:, :, :], ALU.mult)
                ins = None
                for h in range(HH):
                    ins = vec.tensor_scalar(rhat[:, h, :], rtil[:, h, :], expCc[:, h:h + 1], None, ALU.mult)
                self.tick('V', ins, 1)
                vec.tensor_tensor(rloc[:, :, :], rbf[:, :, :], eq[:, :, :], ALU.mult)
                ins = vec.tensor_tensor(kloc[:, :, :], kfm2[:, :, :], el[:, :, :], ALU.mult)
                self.tick('V', ins, 1, ev=f'b_dk3_{jj}')
                ins = None
                pi = 0
                for sb in range(4):
                    for qb in range(sb + 1, 4):
                        for mc in range(HH):
                            ins = vec.tensor_scalar(Rt[:, mc, pi, :], rloc[:, mc, sl(qb, SUB)],
                                                    epair[:, mc, pi:pi + 1], None, ALU.mult)
                        pi += 1
                self.tick('V', ins, 1, ev=f'b_rt_{jj}')
            self.block.vector(dk3)

            def bon(vec, jj=jj):
                self.wait(vec, f'b_dk3_{jj}')
                ins = None
                for h in range(HH):
                    ins = vec.scalar_tensor_tensor(fb[:, h, :], kfm2[:, h, :], self.uc[:, h:h + 1],
                                                   rbf[:, h, :], ALU.mult, ALU.mult)
                self.tick('V', ins, 1, ev=f'b_fb_{jj}')
            self.block.vector(bon)

            def dmm(t, jj=jj):
                self.wait(t, f'b_fb_{jj}')
                self.wait(t, f'b_musev_{jj}_1')
                ins = None
                for h in range(HH):
                    ins = t.matmul(self.psA[0:1, 1, sl(h, CH)], self.onesb[:, 0:1], fb[:, h, :],
                                   start=True, stop=True)
                self.tick('T', ins, 1, ev=f'b_dmm_{jj}')
            self.block.tensor(dmm)

            def drowv(vec, jj=jj):
                self.wait(vec, f'b_dmm_{jj}')
                ins = vec.tensor_copy(drow[0:1, :, :], self.psA[0:1, 1, :].rearrange("p (h t) -> p h t", h=HH))
                self.tick('V', ins, 1, ev=f'b_drow_{jj}')
            self.block.vector(drowv)

            def dbcm(t, jj=jj):
                self.wait(t, f'b_drow_{jj}')
                ins = None
                for h in range(HH):
                    ins = t.matmul(self.psA[:, 2, sl(h, CH)], self.onesbr[0:1, :], drow[0:1, h, :],
                                   start=True, stop=True)
                self.tick('T', ins, 1, ev=f'b_dbc_{jj}')
            self.block.tensor(dbcm)

            def dbv(vec, jj=jj):
                self.wait(vec, f'b_dbc_{jj}')
                ins = vec.tensor_tensor(Db[:, :, :], self.psA[:, 2, :].rearrange("p (h t) -> p h t", h=HH),
                                        self.identb[:, :].unsqueeze(1).broadcast_to((128, HH, CH)), ALU.mult)
                self.tick('V', ins, 1, ev=f'b_db_{jj}')
            self.block.vector(dbv)

            def amm(t, jj=jj):
                self.wait(t, f'b_rt_{jj}')
                if jj >= 1:
                    self.wait(t, f'b_amev_{jj-1}')
                sbst = {0: 0, 1: 3, 2: 5}
                ins = None
                for mc in range(HH):
                    for sb in range(4):
                        kw = {'tile_position': (0, 32 * sb)} if sb == 3 else {}
                        ins = t.matmul(self.psA[32 * sb:32 * sb + 32, 0, mc * CH + 32 * sb:mc * CH + 32 * sb + 32],
                                       kloc[:, mc, sl(sb, SUB)], rloc[:, mc, sl(sb, SUB)],
                                       start=True, stop=True, **kw)
                        if sb < 3:
                            pi0 = sbst[sb]
                            nq = 3 - sb
                            ins = t.matmul(self.psA[32 * sb:32 * sb + 32, 0,
                                                    mc * CH + 32 * (sb + 1):mc * CH + 32 * (sb + 1) + nq * SUB],
                                           kloc[:, mc, sl(sb, SUB)],
                                           Rt[:, mc, pi0:pi0 + nq, :].rearrange("p a b -> p (a b)"),
                                           start=True, stop=True)
                self.tick('T', ins, 1, ev=f'b_atail_{jj}')
            self.block.tensor(amm)

            def amask(vec, jj=jj):
                self.wait(vec, f'b_atail_{jj}')
                self.wait(vec, f'b_db_{jj}')
                vec.scalar_tensor_tensor(Am[:, :, :], self.psA[:, 0, :].rearrange("p (h t) -> p h t", h=HH), 1.0,
                                         self.masksu[:, :].unsqueeze(1).broadcast_to((128, HH, CH)), ALU.mult, ALU.mult)
                ins = vec.tensor_tensor(Am[:, :, :], Am[:, :, :], Db[:, :, :], ALU.add)
                self.tick('V', ins, 1, ev=f'b_amev_{jj}')
            self.block.vector(amask)

            def omm(t, jj=jj):
                self.wait(t, f'b_amev_{jj}')
                self.wait(t, f'b_sg_{jj}')
                ins = None
                for h in range(HH):
                    for vc in range(2):
                        c = 2 * h + vc
                        ins = t.matmul(self.psB[:, c // 4, sl(c % 4, CH)],
                                       vtm2[:, (2 * h + vc) * 128:(2 * h + vc + 1) * 128],
                                       Am[:, h, :], start=True, stop=False)
                        ins = t.matmul(self.psB[:, c // 4, sl(c % 4, CH)],
                                       sp2[:, h, sl(vc, 128)], rtil[:, h, :], start=False, stop=False)
                        ins = t.matmul(self.psB[:, c // 4, sl(c % 4, CH)],
                                       self.sinb[:, h, sl(vc, 128)], rhat[:, h, :], start=False, stop=True)
                self.tick('T', ins, 1, ev=f'b_omm_{jj}')
                self.mark('T', f'b_panelfree_{jj}')
            self.block.tensor(omm)

            def oev(vec, jj=jj):
                self.wait(vec, f'b_omm_{jj}')
                vec.tensor_copy(obf[:, 0:4, :].rearrange("p c t -> p (c t)"), self.psB[:, 0, :])
                ins = vec.tensor_copy(obf[:, 4:8, :].rearrange("p c t -> p (c t)"), self.psB[:, 1, :])
                self.tick('V', ins, 1, ev=f'b_oev_{jj}')
                ins = vec.tensor_tensor(osq[:, :, :], obf[:, :, :], obf[:, :, :], ALU.mult)
                self.tick('V', ins, 1, ev=f'b_osq_{jj}')
            self.block.vector(oev)

            def ost(sc, jj=jj):
                self.wait(sc, f'b_oev_{jj}')
                self.tick('DA', sc.dma_start(out=self.sc_o[jj, :, :, :], in_=obf[:, :, :]), 16, ev=f'b_ost_{jj}')
            self.block.scalar(ost)

            def gnm(t, jj=jj):
                self.wait(t, f'b_osq_{jj}')
                self.wait(t, f'b_drow_{jj}')
                ins = None
                for h in range(HH):
                    for vc in range(2):
                        ins = t.matmul(self.psA[0:1, 3, sl(h, CH)], self.onesb[:, 0:1], obf[:, 2 * h + vc, :],
                                       start=(vc == 0), stop=(vc == 1))
                for h in range(HH):
                    for vc in range(2):
                        ins = t.matmul(self.psA[0:1, 1, sl(h, CH)], self.onesb[:, 0:1], osq[:, 2 * h + vc, :],
                                       start=(vc == 0), stop=(vc == 1))
                self.tick('T', ins, 1, ev=f'b_gnm_{jj}')
            self.block.tensor(gnm)

            def gnv(vec, jj=jj):
                self.wait(vec, f'b_gnm_{jj}')
                vec.tensor_copy(gnrowA, self.psA[0:1, 3, :].rearrange("p (h t) -> p h t", h=HH))
                ins = vec.tensor_copy(gnrowB, self.psA[0:1, 1, :].rearrange("p (h t) -> p h t", h=HH))
                self.tick('V', ins, 1, ev=f'b_gnv_{jj}')
            self.block.vector(gnv)

            def gnst(sc, jj=jj):
                self.wait(sc, f'b_gnv_{jj}')
                self.tick('DA', sc.dma_start(out=self.sc_gn[jj, 0:1, :, :], in_=gnrowA), 16)
                self.tick('DA', sc.dma_start(out=self.sc_gn[jj, 1:2, :, :], in_=gnrowB), 16,
                          ev=f'b_store_{jj}')
            self.block.scalar(gnst)

            if 'p3bdbg' in self.dbg and jj == 0:
                o1 = self.nc.dram_tensor("dbg_str0", [1, 2, CH], BF16, kind="ExternalOutput").ap()
                self.dbg_out['str0'] = o1
                o2 = self.nc.dram_tensor("dbg_mB2", [128, 2, CH], BF16, kind="ExternalOutput").ap()
                self.dbg_out['mB2'] = o2
                def dbs0(sync, jj=jj):
                    self.wait(sync, f'b_bcev_{jj}')
                    self.tick('DS', sync.dma_start(out=o1, in_=strow[0][0:1, :, :]), 16)
                    self.tick('DS', sync.dma_start(out=o2[:, 0, :], in_=mB2[:, :]), 16)
                    self.tick('DS', sync.dma_start(out=o2[:, 1, :], in_=rB2[:, :]), 16, ev='dbgs0')
                    self.final_evs.append('dbgs0')
                self.block.sync(dbs0)
            if 'p3bdbg' in self.dbg:
                if jj == 0:
                    self.dbydr = self.nc.dram_tensor("dbg_ybf", [self.NCH, 128, DC, CH], BF16, kind="ExternalOutput").ap()
                    self.dbg_out['ybf'] = self.dbydr
                    self.dbrdr = self.nc.dram_tensor("dbg_rbf", [self.NCH, 128, HH, CH], BF16, kind="ExternalOutput").ap()
                    self.dbg_out['rbf'] = self.dbrdr
                    self.dbkl = self.nc.dram_tensor("dbg_kloc", [self.NCH, 128, HH, CH], BF16, kind="ExternalOutput").ap()
                    self.dbg_out['kloc'] = self.dbkl
                    self.dbrl = self.nc.dram_tensor("dbg_rloc", [self.NCH, 128, HH, CH], BF16, kind="ExternalOutput").ap()
                    self.dbg_out['rloc'] = self.dbrl
                    self.dbam = self.nc.dram_tensor("dbg_Am", [self.NCH, 128, HH, CH], BF16, kind="ExternalOutput").ap()
                    self.dbg_out['Am'] = self.dbam
                    self.dbrt = self.nc.dram_tensor("dbg_rtil", [self.NCH, 128, HH, CH], BF16, kind="ExternalOutput").ap()
                    self.dbg_out['rtil'] = self.dbrt
                def dbst(sync, jj=jj):
                    self.wait(sync, f'b_y_{jj}')
                    self.tick('DS', sync.dma_start(out=self.dbydr[jj], in_=ybf[:, :, :]), 16)
                    self.wait(sync, f'b_dk3_{jj}')
                    self.tick('DS', sync.dma_start(out=self.dbrdr[jj], in_=rbf[:, :, :]), 16)
                    self.tick('DS', sync.dma_start(out=self.dbkl[jj], in_=kloc[:, :, :]), 16)
                    self.tick('DS', sync.dma_start(out=self.dbrl[jj], in_=rloc[:, :, :]), 16)
                    self.wait(sync, f'b_rt_{jj}')
                    self.tick('DS', sync.dma_start(out=self.dbrt[jj], in_=rtil[:, :, :]), 16)
                    self.wait(sync, f'b_amev_{jj}')
                    self.tick('DS', sync.dma_start(out=self.dbam[jj], in_=Am[:, :, :]), 16, ev=f'b_dbst_{jj}')
                    self.final_evs.append(f'b_dbst_{jj}')
                self.block.sync(dbst)
                self.ev[f'b_panelfree_{jj}'] = self.ev[f'b_dbst_{jj}']

        self.mark('DA', 'p3b_done')
        self.dbg_dump('sco', [self.NCH, 128, DC, CH], BF16, self.sc_o[:, :, :, :], [f'b_ost_{NCH-1}'])
        self.dbg_dump('scsg', [self.NCH, 128, DC, CH], BF16, self.sc_sg[:, :, :, :], [f'b_sgst_{NCH-1}'])
        self.dbg_dump('scgn', [self.NCH, 2, HH, CH], F32, self.sc_gn[:, :, :, :], [f'b_store_{NCH-1}'])

    def emit_p4a(self):
        import contextlib
        self.close_stack('p3b_st')
        self.p4_st = contextlib.ExitStack()
        st = self.p4_st
        NCH = self.NCH
        self.woT = self.sb("woT", [128, DC, D], BF16, st)
        obuf2 = [self.sb("cobuf0", [128, DC, CH], BF16, st), self.sb("cobuf1", [128, DC, CH], BF16, st)]
        sgbuf2 = [self.sb("csg0", [128, DC, CH], BF16, st), self.sb("csg1", [128, DC, CH], BF16, st)]
        selpb = self.sb("cselp", [32, NCH * 128], BF16, st)
        gnall = self.sb("cgnall", [32, 2, HH * CH], F32, st)
        gnallb = self.sb("cgnallb", [32, 2, HH * CH], BF16, st)
        rstdBs = self.sb("crstdB", [128, HH, CH], BF16, st)
        mrstdBs = self.sb("cmrstdB", [128, HH, CH], BF16, st)
        og = self.sb("cog", [128, DC, CH], BF16, st)
        gntmp = og[:, :, :].rearrange("p a b -> p (a b)").bitcast(F32)[0:32, 0:HH * CH]
        t2b = self.sb("ct2", [128, DC, CH], BF16, st)
        xbf = self.sb("cxbf", [128, DC, CH], BF16, st)
        xsq = self.sb("cxsq", [128, DC, CH], BF16, st)
        z2o = self.sb("cz2o", [128, DC, CH], BF16, st)
        mrow = self.sb("cmrow", [1, CH], F32, st)
        rrow = self.sb("crrow", [1, CH], F32, st)
        srow = self.sb("csrow", [1, 2, CH], F32, st)
        rowb = self.sb("crowb", [1, 2, CH], BF16, st)
        mB4 = self.sb("cmB4", [128, CH], BF16, st)
        rB4 = self.sb("crB4", [128, CH], BF16, st)
        z2sums = self.sb("cz2sums", [128, DC, NCH], F32, st)
        wst4 = self.aview(obuf2[0], [128, 1, 1024])

        W = self.W

        self.ev['c_prev'] = self.ev['p3b_done']
        for rc in range(DC):
            def ld(gp, rc=rc):
                self.wait(gp, 'c_prev')
                self.tick('DG', gp.dma_start(out=wst4[:, 0, :], in_=W['attn_W_o'][rc * 128:(rc + 1) * 128, :]), 16,
                          ev=f'c_ld_{rc}')
            self.block.gpsimd(ld)

            def tp(te, rc=rc):
                self.wait(te, f'c_ld_{rc}')
                if rc == 0:
                    self.wait(te, f'b_gnm_{NCH-1}')
                ins = None
                for c in range(DC):
                    ins = te.transpose(self.psTb[:, sl(c, 128)], wst4[:, 0, sl(c, 128)], self.identb[:, :])
                self.tick('T', ins, 1, ev=f'c_tp_{rc}')
            self.block.tensor(tp)

            def ev(vec, rc=rc):
                self.wait(vec, f'c_tp_{rc}')
                ins = vec.tensor_copy(self.woT[:, :, sl(rc, 128)],
                                      self.psTb[:, :].rearrange("p (c r) -> p c r", c=DC))
                self.tick('V', ins, 1, ev=f'c_ev_{rc}')
            self.block.vector(ev)
            self.ev['c_prev'] = self.ev[f'c_ev_{rc}']
        self.ev['c_wdone'] = self.ev['c_prev']
        self.mark('DA', 'c_z2st_-1')

        def gld(sync):
            self.wait(sync, f'b_store_{NCH-1}')
            self.wait(sync, f'b_sgst_{NCH-1}')
            self.wait(sync, f'b_ost_{NCH-1}')
            self.tick('DS', sync.dma_start(out=selpb[:, :], in_=self.selpD[:, :]), 16)
            self.tick('DS', sync.dma_start(out=gnall[:, :, :],
                                           in_=self.sc_gn[:, :, :, :].rearrange("n a h t -> n a (h t)")), 16,
                      ev='p4_gnld')
        self.block.sync(gld)

        cc32 = lambda v: self.ccst[0:32, self.CC[v]:self.CC[v] + 1]

        def gmath(vec):
            self.wait(vec, 'p4_gnld')
            vec.tensor_scalar(gnall[:, 0, :], gnall[:, 0, :], cc32(1.0 / float(VH)), None, ALU.mult)
            vec.tensor_scalar(gnall[:, 1, :], gnall[:, 1, :], cc32(1.0 / float(VH)), None, ALU.mult)
            vec.tensor_tensor(gntmp[:, :], gnall[:, 0, :], gnall[:, 0, :], ALU.mult)
            ins = vec.tensor_tensor(gnall[:, 1, :], gnall[:, 1, :], gntmp[:, :], ALU.subtract)
            self.tick('V', ins, 1, ev='p4_gvar')
        self.block.vector(gmath)

        def gsqrt(sc):
            self.wait(sc, 'p4_gvar')
            self.fence(sc, 'S', sc.activation(gnall[:, 1, :], gnall[:, 1, :], AF.Ln, bias=EPS, scale=1.0))
            ins = sc.activation(gnall[:, 1, :], gnall[:, 1, :], AF.Exp, bias=0.0, scale=-0.5)
            self.tick('S', ins, 1, ev='p4_grstd')
        self.block.scalar(gsqrt)

        def gcast(vec):
            self.wait(vec, 'p4_grstd')
            vec.tensor_tensor(gnall[:, 0, :], gnall[:, 0, :], gnall[:, 1, :], ALU.mult)
            ins = vec.tensor_copy(gnallb[:, :, :], gnall[:, :, :])
            self.tick('V', ins, 1, ev='p4_gnall')
        self.block.vector(gcast)

        for jj in range(NCH):
            u0 = (jj + 1) * CH
            obuf = obuf2[jj % 2]
            sgbuf = sgbuf2[jj % 2]

            def lds(sync, jj=jj, obuf=obuf, sgbuf=sgbuf):
                if jj == 0:
                    self.wait(sync, 'p4_gnld')
                    self.wait(sync, 'c_wdone')
                elif jj >= 2:
                    self.wait(sync, f'c_ogdone_{jj-2}')
                self.tick('DS', sync.dma_start(out=obuf[:, :, :], in_=self.sc_o[jj, :, :, :]), 16)
                self.tick('DS', sync.dma_start(out=sgbuf[:, :, :], in_=self.sc_sg[jj, :, :, :]), 16,
                          ev=f'c_ld_{jj}')
            self.block.sync(lds)

            def gbc(t, jj=jj):
                if jj == 0:
                    self.wait(t, 'p4_gnall')
                else:
                    self.wait(t, f'c_bcev_{jj-1}')
                ins = t.matmul(self.psC[:, 0:HH * CH], selpb[0:32, sl(jj, 128)], gnallb[0:32, 1, :],
                               start=True, stop=True)
                ins = t.matmul(self.psA[:, 0, 0:HH * CH], selpb[0:32, sl(jj, 128)], gnallb[0:32, 0, :],
                               start=True, stop=True)
                self.tick('T', ins, 1, ev=f'c_gnbc_{jj}')
            self.block.tensor(gbc)

            def ogv(vec, jj=jj, obuf=obuf, sgbuf=sgbuf):
                self.wait(vec, f'c_gnbc_{jj}')
                self.wait(vec, f'c_ld_{jj}')
                vec.tensor_copy(rstdBs[:, :, :], self.psC[:, :].rearrange("p (h t) -> p h t", h=HH))
                ins = vec.tensor_copy(mrstdBs[:, :, :], self.psA[:, 0, :].rearrange("p (h t) -> p h t", h=HH))
                self.tick('V', ins, 1, ev=f'c_bcev_{jj}')
                ins = None
                for h in range(HH):
                    rb2 = rstdBs[:, h, :].unsqueeze(1).broadcast_to((128, 2, CH))
                    mb2 = mrstdBs[:, h, :].unsqueeze(1).broadcast_to((128, 2, CH))
                    vec.tensor_tensor(t2b[:, 2 * h:2 * h + 2, :], obuf[:, 2 * h:2 * h + 2, :], rb2, ALU.mult)
                for h in range(HH):
                    mb2 = mrstdBs[:, h, :].unsqueeze(1).broadcast_to((128, 2, CH))
                    ins = vec.tensor_tensor(t2b[:, 2 * h:2 * h + 2, :], t2b[:, 2 * h:2 * h + 2, :], mb2, ALU.subtract)
                self.tick('V', ins, 1)
                vec.tensor_tensor(og[:, :, :], t2b[:, :, :],
                                  self.gnW[:, :].unsqueeze(2).broadcast_to((128, DC, CH)), ALU.mult)
                vec.tensor_tensor(og[:, :, :], og[:, :, :],
                                  self.gnB[:, :].unsqueeze(2).broadcast_to((128, DC, CH)), ALU.add)
                ins = vec.tensor_tensor(og[:, :, :], og[:, :, :], sgbuf[:, :, :], ALU.mult)
                self.tick('V', ins, 1, ev=f'c_og_{jj}')
            self.block.vector(ogv)

            for mc in range(DC):
                def wom(t, jj=jj, mc=mc):
                    if mc == 0:
                        self.wait(t, f'c_og_{jj}')
                    prev = (jj, mc - 2) if mc >= 2 else ((jj - 1, DC - 2 + mc) if jj >= 1 else None)
                    if prev is not None:
                        self.wait(t, 'c_x2_{}_{}'.format(*prev))
                    ins = None
                    for c in range(DC):
                        ins = t.matmul(self.psB[:, mc % 2, 0:CH], self.woT[:, c, sl(mc, 128)],
                                       og[:, c, :], start=(c == 0), stop=(c == DC - 1))
                    self.tick('T', ins, 1, ev=f'c_wo_{jj}_{mc}')
                self.block.tensor(wom)

                def x2v(vec, jj=jj, mc=mc, u0=u0):
                    self.wait(vec, f'c_wo_{jj}_{mc}')
                    ins = vec.tensor_tensor(self.resid[:, mc, u0:u0 + CH], self.resid[:, mc, u0:u0 + CH],
                                            self.psB[:, mc % 2, 0:CH], ALU.add)
                    self.tick('V', ins, 1, ev=f'c_x2_{jj}_{mc}')
                self.block.vector(x2v)
            self.mark('V', f'c_ogdone_{jj}')
            if jj == 0 and 'p4dbg' in self.dbg:
                od1 = self.dbg_t('og0', [128, DC, CH], BF16)
                od2 = self.dbg_t('rstdB0', [128, HH, CH], BF16)
                od4 = self.dbg_t('t2b0', [128, DC, CH], BF16)
                od5 = self.dbg_t('obuf0', [128, DC, CH], BF16)
                od6 = self.dbg_t('sgbuf0', [128, DC, CH], BF16)
                def fdz(sync):
                    self.wait(sync, 'c_og_0')
                    self.tick('DS', sync.dma_start(out=od1, in_=og[:, :, :]), 16)
                    self.tick('DS', sync.dma_start(out=od4, in_=t2b[:, :, :]), 16)
                    self.tick('DS', sync.dma_start(out=od5, in_=obuf[:, :, :]), 16)
                    self.tick('DS', sync.dma_start(out=od6, in_=sgbuf[:, :, :]), 16)
                    self.wait(sync, 'c_bcev_0')
                    self.tick('DS', sync.dma_start(out=od2, in_=rstdBs[:, :, :]), 16, ev='p4dbgst')
                    self.final_evs.append('p4dbgst')
                self.block.sync(fdz)
                self.ev['c_ogdone_0'] = self.ev['p4dbgst']

            def zc(vec, jj=jj, u0=u0):
                self.wait(vec, f'c_x2_{jj}_{DC-1}')
                ins = vec.tensor_copy(xbf[:, :, :], self.resid[:, :, u0:u0 + CH])
                self.tick('V', ins, 1)
                ins = vec.tensor_tensor(xsq[:, :, :], xbf[:, :, :], xbf[:, :, :], ALU.mult)
                self.tick('V', ins, 1, ev=f'c_xsq_{jj}')
            self.block.vector(zc)

            def zst(t, jj=jj):
                self.wait(t, f'c_xsq_{jj}')
                if jj >= 1:
                    self.wait(t, f'c_zvar_{jj-1}')
                ins = None
                for c in range(DC):
                    ins = t.matmul(self.psA[0:1, 2, 0:CH], self.onesb[:, 0:1], xbf[:, c, :],
                                   start=(c == 0), stop=(c == DC - 1))
                self.tick('T', ins, 1)
                for c in range(DC):
                    ins = t.matmul(self.psA[0:1, 2, CH:2 * CH], self.onesb[:, 0:1], xsq[:, c, :],
                                   start=(c == 0), stop=(c == DC - 1))
                self.tick('T', ins, 1, ev=f'c_zsums_{jj}')
            self.block.tensor(zst)

            def zsv(vec, jj=jj):
                self.wait(vec, f'c_zsums_{jj}')
                vec.tensor_scalar(mrow[:, :], self.psA[0:1, 2, 0:CH], self.crow(1.0 / 1024.0), None, ALU.mult)
                vec.tensor_scalar(srow[:, 0, :], self.psA[0:1, 2, CH:2 * CH], self.crow(1.0 / 1024.0), None, ALU.mult)
                vec.tensor_tensor(srow[:, 1, :], mrow[:, :], mrow[:, :], ALU.mult)
                ins = vec.tensor_tensor(rrow[:, :], srow[:, 0, :], srow[:, 1, :], ALU.subtract)
                self.tick('V', ins, 1, ev=f'c_zvar_{jj}')
            self.block.vector(zsv)

            def zss(sc, jj=jj):
                self.wait(sc, f'c_zvar_{jj}')
                self.fence(sc, 'S', sc.activation(rrow[:, :], rrow[:, :], AF.Ln, bias=EPS * EPS, scale=1.0 + EPS))
                ins = sc.activation(rrow[:, :], rrow[:, :], AF.Exp, bias=0.0, scale=-0.5)
                self.tick('S', ins, 1, ev=f'c_zrstd_{jj}')
            self.block.scalar(zss)

            def zrb(vec, jj=jj):
                self.wait(vec, f'c_zrstd_{jj}')
                vec.tensor_copy(rowb[:, 0, :], mrow[:, :])
                ins = vec.tensor_copy(rowb[:, 1, :], rrow[:, :])
                self.tick('V', ins, 1, ev=f'c_zrows_{jj}')
            self.block.vector(zrb)

            def zbc(t, jj=jj):
                self.wait(t, f'c_zrows_{jj}')
                if jj >= 1:
                    self.wait(t, f'c_zbcev_{jj-1}')
                ins = t.matmul(self.psA[:, 1, 0:CH], self.onesbr[0:1, :], rowb[0:1, 0, :], start=True, stop=True)
                self.tick('T', ins, 1)
                ins = t.matmul(self.psA[:, 1, CH:2 * CH], self.onesbr[0:1, :], rowb[0:1, 1, :], start=True, stop=True)
                self.tick('T', ins, 1, ev=f'c_zbc_{jj}')
            self.block.tensor(zbc)

            def zbuild(vec, jj=jj):
                self.wait(vec, f'c_zbc_{jj}')
                vec.tensor_copy(mB4[:, :], self.psA[:, 1, 0:CH])
                ins = vec.tensor_copy(rB4[:, :], self.psA[:, 1, CH:2 * CH])
                self.tick('V', ins, 1, ev=f'c_zbcev_{jj}')
                vec.tensor_tensor(xsq[:, :, :], xbf[:, :, :],
                                  mB4[:, :].unsqueeze(1).broadcast_to((128, DC, CH)), ALU.subtract)
                ins = vec.tensor_tensor(xsq[:, :, :], xsq[:, :, :],
                                        rB4[:, :].unsqueeze(1).broadcast_to((128, DC, CH)), ALU.mult)
                self.tick('V', ins, 1, ev=f'c_zt_{jj}')
            self.block.vector(zbuild)

            def zaff(vec, jj=jj):
                self.wait(vec, f'c_zt_{jj}')
                if jj >= 2:
                    self.wait(vec, f'c_z2st_{jj-2}')
                vec.tensor_tensor(z2o[:, :, :], xsq[:, :, :],
                                  self.lnG[:, :].unsqueeze(2).broadcast_to((128, DC, CH)), ALU.mult)
                vec.tensor_tensor(z2o[:, :, :], z2o[:, :, :],
                                  self.lnB[:, :].unsqueeze(2).broadcast_to((128, DC, CH)), ALU.add)
                ins = vec.tensor_reduce(z2sums[:, :, jj:jj + 1], z2o[:, :, :], AXX, ALU.add)
                self.tick('V', ins, 1, ev=f'c_zaff_{jj}')
            self.block.vector(zaff)

            def z2st(sc, jj=jj, u0=u0):
                self.wait(sc, f'c_zaff_{jj}')
                self.tick('DA', sc.dma_start(out=self.sc_z2[:, :, u0:u0 + CH].rearrange("c p t -> p c t"),
                                             in_=z2o[:, :, :]), 16, ev=f'c_z2st_{jj}')
            self.block.scalar(z2st)

        def fin(vec):
            self.wait(vec, f'c_zaff_{NCH-1}')
            ins = vec.tensor_reduce(self.zsumT[:, :].unsqueeze(2), z2sums[:, :, :], AXX, ALU.add)
            self.tick('V', ins, 1, ev='zsumT_2')
        self.block.vector(fin)
        self.mark('DA', 'p4a_done_da')
        self.mark('V', 'p4a_psC_free')
        self.mark('DA', 'p4a_done')
        self.close_stack('p4_st')
        self.dbg_dump('x2', [128, DC, self.TP], F32, self.resid[:, :, :], [f'c_x2_{NCH-1}_{DC-1}'])

        self.dbg_dump('scz2', [DC, 128, self.TP], BF16, self.sc_z2[:, :, :], [f'c_z2st_{NCH-1}'])

    def emit_finish(self):
        def f(sync):
            for ev in self.final_evs:
                self.wait(sync, ev)
            for k in ['DS', 'DA', 'DG', 'CC']:
                if self.cnt[k] > 0:
                    sync.wait_ge(self.sems[k], self.cnt[k])
        self.block.sync(f)


def host_consts():
    ident = np.eye(128, dtype=np.float32)
    mask_su = np.triu(np.ones((128, 128), np.float32), 1)
    sel4 = np.zeros((4, 4, 128), np.float32)
    for h in range(4):
        sel4[h, h, :] = 1.0
    return ident, mask_su, sel4.reshape(4, 512)


def make_in_maps(inputs, T_loc):
    x = np.asarray(inputs['L0_attn_hidden_states'])
    B = x.shape[0]
    ident, mask_su, sel4 = host_consts()
    ncht = T_loc // 128
    selp = np.zeros((32, ncht * 128), ml_dtypes.bfloat16)
    for k in range(min(32, ncht)):
        selp[k, k * 128:(k + 1) * 128] = 1.0
    in_maps = []
    for core in range(2 * B):
        b, half = core // 2, core % 2
        t0 = half * T_loc
        brow = x[b, t0 - 1] if half == 1 else np.zeros(D, np.float32)
        x0ext = np.concatenate([np.broadcast_to(brow, (CH, D)), x[b, t0:t0 + T_loc]], 0)
        im = {
            'x0': np.ascontiguousarray(x0ext, np.float32),
            'bmask': np.full((128, 1), 1.0 if half else 0.0, np.float32),
            'c_ident': ident, 'c_masksu': mask_su, 'c_sel4': sel4,
            'c_selp': selp,
        }
        for n in ATTN_W:
            im['attn_' + n] = np.ascontiguousarray(inputs['L0_attn_' + n], np.float32)
        for n in FFN_W:
            im['ffn_' + n] = np.ascontiguousarray(inputs['L0_ffn_' + n], np.float32)
        in_maps.append(im)
    return in_maps


def kernel(**inputs):
    x = np.asarray(inputs['L0_attn_hidden_states'])
    B, T, Dm = x.shape
    T_loc = T // 2
    mk = MK(T_loc)
    nc = mk.build()
    in_maps = make_in_maps(inputs, T_loc)
    res = run_bass_kernel_spmd(nc, in_maps, core_ids=list(range(2 * B)))
    out = np.empty((B, T, D), np.float32)
    for core in range(2 * B):
        b, half = core // 2, core % 2
        out[b, half * T_loc:(half + 1) * T_loc] = res.results[core]['y']
    return out
